# version 12
# baseline (speedup 1.0000x reference)
import math
from collections import defaultdict
from contextlib import ExitStack

import numpy as np
import ml_dtypes

import concourse.bass as bass
import concourse.mybir as mybir
from concourse.bass_utils import run_bass_kernel_spmd

F32 = mybir.dt.float32
BF16 = mybir.dt.bfloat16
AF = mybir.ActivationFunctionType
ALU = mybir.AluOpType
AX = mybir.AxisListType

T = 2048
C = 2048
NH = 16
NKV = 4
HD = 128
II = 8192
NPT = T // 128
NFT = C // 128
EPS = 1e-5
SCALE = 1.0 / math.sqrt(HD)
NEG = -1.0e30

ENGS = ("sync", "gpsimd", "tensor", "vector", "scalar")


class Prog:

    def __init__(self, sems):
        self.ops = {e: [] for e in ENGS}
        self.cnt = defaultdict(int)
        self.meta = []
        self.sems = list(sems)
        self.bar = {}
        self.bar_done = set(ENGS)
        self.sem_by_id = {id(s): s for s in sems}

    def mark(self, sem):
        return self.cnt[id(sem)]

    def phase(self):
        self.bar = {id(s): self.cnt[id(s)] for s in self.sems}
        self.bar_done = set()

    def emit(self, eng, fn, waits=(), inc=None, inc_amt=1, tag=""):
        waits = [(s, v) for (s, v) in waits if v > 0]
        if eng not in self.bar_done:
            self.bar_done.add(eng)
            waits = [(self.sem_by_id[i], v) for i, v in self.bar.items()
                     if v > 0] + waits
        if inc is not None:
            self.cnt[id(inc)] += inc_amt
        self.meta.append((eng, [(id(s), v) for s, v in waits],
                          [(id(inc), inc_amt)] if inc is not None else [], tag))

        def run(e, fn=fn, waits=waits, inc=inc, amt=inc_amt):
            for s, v in waits:
                e.wait_ge(s, v)
            ins = fn(e)
            if inc is not None:
                ins.then_inc(inc, amt)
        self.ops[eng].append(run)

    def waitop(self, eng, waits, tag=""):
        self.emit(eng, lambda e: None, waits=waits, tag=tag)

    def simulate(self):
        queues = {e: [] for e in ENGS}
        for (eng, waits, incs, tag) in self.meta:
            queues[eng].append((waits, incs, tag))
        heads = {e: 0 for e in ENGS}
        cnt = defaultdict(int)
        progress = True
        while progress:
            progress = False
            for e in ENGS:
                while heads[e] < len(queues[e]):
                    waits, incs, tag = queues[e][heads[e]]
                    if all(cnt[s] >= v for s, v in waits):
                        for s, a in incs:
                            cnt[s] += a
                        heads[e] += 1
                        progress = True
                    else:
                        break
        if any(heads[e] < len(queues[e]) for e in ENGS):
            lines = []
            for e in ENGS:
                if heads[e] < len(queues[e]):
                    waits, incs, tag = queues[e][heads[e]]
                    missing = [(str(s), v, cnt[s]) for s, v in waits
                               if cnt[s] < v]
                    lines.append(f"{e}[{heads[e]}/{len(queues[e])}] tag={tag} "
                                 f"missing={missing}")
            raise RuntimeError("build-time DEADLOCK:\n" + "\n".join(lines))


def _fn(method, *args, **kwargs):
    return lambda e: getattr(e, method)(*args, **kwargs)


def build_nc(phases=5, debug=False):
    nc = bass.Bass("TRN2", target_bir_lowering=False)

    x_in = nc.dram_tensor("x", [T, C], F32, kind="ExternalInput").ap()
    wq_in = nc.dram_tensor("wq", [C, C], F32, kind="ExternalInput").ap()
    wk_in = nc.dram_tensor("wk", [C, NKV * HD], F32, kind="ExternalInput").ap()
    wv_in = nc.dram_tensor("wv", [C, NKV * HD], F32, kind="ExternalInput").ap()
    wp_in = nc.dram_tensor("wp", [C, C], F32, kind="ExternalInput").ap()
    wg_in = nc.dram_tensor("wg", [C, II], F32, kind="ExternalInput").ap()
    wu_in = nc.dram_tensor("wu", [C, II], F32, kind="ExternalInput").ap()
    wd_in = nc.dram_tensor("wd", [II, C], F32, kind="ExternalInput").ap()
    identw = nc.dram_tensor("identw", [128, 128], BF16, kind="ExternalInput").ap()
    maskw = nc.dram_tensor("maskw", [128, 128], BF16, kind="ExternalInput").ap()
    rotw = nc.dram_tensor("rotw", [128, 128], BF16, kind="ExternalInput").ap()
    onesbw = nc.dram_tensor("onesbw", [128, 8], BF16, kind="ExternalInput").ap()
    onesfw = nc.dram_tensor("onesfw", [128, 128], F32, kind="ExternalInput").ap()
    cosw = nc.dram_tensor("cosw", [128, T], BF16, kind="ExternalInput").ap()
    cosw2 = nc.dram_tensor("cosw2", [128, T], BF16, kind="ExternalInput").ap()
    sinw2 = nc.dram_tensor("sinw2", [128, T], BF16, kind="ExternalInput").ap()
    epsw = nc.dram_tensor("epsw", [128, 8], F32, kind="ExternalInput").ap()
    sinw = nc.dram_tensor("sinw", [128, T], BF16, kind="ExternalInput").ap()

    out = nc.dram_tensor("out", [T, C], F32, kind="ExternalOutput").ap()
    dbg = {}
    if debug:
        dbg_rq = nc.dram_tensor("dbg_rq", [128, NPT * C], BF16,
                                kind="ExternalOutput").ap()
        dbg_rkv = nc.dram_tensor("dbg_rkv", [128, 16384], BF16,
                                 kind="ExternalOutput").ap()
        dbg_vecs = nc.dram_tensor("dbg_vecs", [128, 256], F32,
                                  kind="ExternalOutput").ap()
        dbg_rw = nc.dram_tensor("dbg_rw", [128, 16 * C], BF16,
                                kind="ExternalOutput").ap()
        dbg_fbf = nc.dram_tensor("dbg_fbf", [128, 4 * C], BF16,
                                 kind="ExternalOutput").ap()

    with ExitStack() as top:
        E_ = top.enter_context
        rq = E_(nc.sbuf_tensor("rq", [128, NPT * C], BF16))
        rkv = E_(nc.sbuf_tensor("rkv", [128, 16384], BF16))
        rw = E_(nc.sbuf_tensor("rw", [128, 16 * C], BF16))
        vecs = E_(nc.sbuf_tensor("vecs", [128, 256], F32))
        ident = E_(nc.sbuf_tensor("ident", [128, 128], BF16))
        dmask = E_(nc.sbuf_tensor("dmask", [128, 128], BF16))
        rmat = E_(nc.sbuf_tensor("rmat", [128, 128], BF16))
        onesb = E_(nc.sbuf_tensor("onesb", [128, 8], BF16))
        onesf = E_(nc.sbuf_tensor("onesf", [128, 128], F32))
        epsb = E_(nc.sbuf_tensor("epsb", [128, 8], F32))
        psE = E_(nc.psum_tensor("psE", [128, 2048], F32))
        psO = E_(nc.psum_tensor("psO", [128, 2048], F32))

        s_x = E_(nc.semaphore("s_x"))
        s_w = E_(nc.semaphore("s_w"))
        s_pe = E_(nc.semaphore("s_pe"))
        s_v = E_(nc.semaphore("s_v"))
        s_a = E_(nc.semaphore("s_a"))
        s_out = E_(nc.semaphore("s_out"))
        p = Prog([s_x, s_w, s_pe, s_v, s_a, s_out])

        V_BNST, V_STAT = 0, 24
        V_NEGM, V_BIAS, V_SSUM, V_RSUM = 26, 27, 28, 29
        V_LNT, V_RSTD = 30, 32
        V_EPS = 34
        V_CMX = 36
        V_B16 = 80
        V_KN = 204
        V_QN = 212
        V_NTMP = 250
        V_M2, V_SD2 = 48, 64
        V_BNF = 96
        V_M3 = 192
        V_RSTD3 = 200

        def vec(col, n=1):
            return vecs[:, col:col + n]

        def gap(eng="vector"):
            p.emit(eng, _fn("tensor_copy", vecs[:, 44:48], epsb[:, 0:4]),
                   tag="gap")

        def slot(s):
            return rq[:, s * C:(s + 1) * C]

        def K_kvh(k):
            return rkv[:, k * T:(k + 1) * T]

        def V_pt(pt):
            return rkv[:, 8192 + pt * 512: 8192 + (pt + 1) * 512]

        psE_bf = psE[:, 0:1024].bitcast(BF16)
        psO_bf = psO[:, 1024:2048].bitcast(BF16)
        psO_bf0 = psO[:, 0:1024].bitcast(BF16)

        p.emit("sync", _fn("dma_start", ident[:], identw), inc=s_x, inc_amt=16,
               tag="c.id")
        p.emit("sync", _fn("dma_start", dmask[:], maskw), inc=s_x, inc_amt=16,
               tag="c.mask")
        p.emit("sync", _fn("dma_start", rmat[:], rotw), inc=s_x, inc_amt=16,
               tag="c.rot")
        p.emit("sync", _fn("dma_start", onesb[:], onesbw), inc=s_x,
               inc_amt=16, tag="c.ob")
        p.emit("sync", _fn("dma_start", onesf[:], onesfw), inc=s_x,
               inc_amt=16, tag="c.of")
        p.emit("sync", _fn("dma_start", epsb[:], epsw), inc=s_x,
               inc_amt=16, tag="c.eps")

        p.phase()
        with ExitStack() as ph:
            xt = [ph.enter_context(nc.sbuf_tensor(f"p1x{i}", [128, C], F32))
                  for i in range(2)]
            consumed = [0, 0]
            for pt in range(NPT):
                b = pt % 2
                p.emit("sync", _fn("dma_start", xt[b][:],
                                   x_in[pt * 128:(pt + 1) * 128, :]),
                       waits=[(s_v, consumed[b])], inc=s_x, inc_amt=16,
                       tag=f"p1.ld{pt}")
                mx_ld = p.mark(s_x)
                for i in range(4):
                    p.emit("vector", _fn("bn_stats", vec(V_BNST + 6 * i, 6),
                                         xt[b][:, 512 * i:512 * (i + 1)]),
                           waits=[(s_x, mx_ld)] if i == 0 else [],
                           tag=f"p1.bn{pt}.{i}")
                gap()
                p.emit("vector", _fn("bn_aggr", vec(V_STAT, 2), vec(V_BNST, 24)),
                       inc=s_v, tag=f"p1.ag{pt}")
                mv = p.mark(s_v)
                p.emit("scalar", _fn("activation", vec(V_LNT), vec(V_STAT + 1),
                                     AF.Ln, bias=epsb[:, 0:1], scale=1.0),
                       waits=[(s_v, mv)], tag=f"p1.ln{pt}")
                p.emit("scalar", _fn("activation", vec(V_RSTD), vec(V_LNT),
                                     AF.Exp, bias=0.0, scale=-0.5),
                       inc=s_a, tag=f"p1.ex{pt}")
                ma = p.mark(s_a)
                p.emit("vector", _fn("tensor_scalar", slot(pt)[:], xt[b][:],
                                     vec(V_STAT), vec(V_RSTD),
                                     ALU.subtract, ALU.mult),
                       waits=[(s_a, ma)], inc=s_v, tag=f"p1.nm{pt}")
                consumed[b] = p.mark(s_v)

        def qkv_pass(kind, ph):
            xnf4 = ph.enter_context(
                nc.sbuf_tensor(f"xnf4{kind}", [128, NFT * 512], BF16))
            cosb = ph.enter_context(nc.sbuf_tensor(f"cosb{kind}", [128, 512], BF16))
            sinb = ph.enter_context(nc.sbuf_tensor(f"sinb{kind}", [128, 512], BF16))
            rt1 = ph.enter_context(nc.sbuf_tensor(f"rt1{kind}", [128, 512], F32))
            rt2 = ph.enter_context(nc.sbuf_tensor(f"rt2{kind}", [128, 512], F32))
            kraw = ph.enter_context(nc.sbuf_tensor(f"kraw{kind}", [128, 512], BF16))
            sqb = ph.enter_context(nc.sbuf_tensor(f"sqb{kind}", [128, 512], BF16))

            if kind == "kv":
                for f in range(NFT):
                    p.emit("gpsimd", _fn("dma_start",
                                         rw[:, f * 512:(f + 1) * 512],
                                         wk_in[f * 128:(f + 1) * 128, :]),
                           inc=s_w, inc_amt=16, tag=f"p2.wk{f}")
                for f in range(NFT):
                    p.emit("gpsimd", _fn("dma_start",
                                         rw[:, 8192 + f * 512:8192 + (f + 1) * 512],
                                         wv_in[f * 128:(f + 1) * 128, :]),
                           inc=s_w, inc_amt=16, tag=f"p2.wv{f}")
            else:
                for f in range(NFT):
                    p.emit("gpsimd", _fn("dma_start",
                                         rw[:, f * C:(f + 1) * C],
                                         wq_in[f * 128:(f + 1) * 128, :]),
                           inc=s_w, inc_amt=16, tag=f"p3.wq{f}")
            mw = p.mark(s_w)

            m_xnf_ev = 0
            m_rope = 0
            m_cons_E = 0
            m_mm_done = 0
            m_csfree = 0
            for g in range(4):
                csrc = cosw if kind == "kv" else cosw2
                ssrc = sinw if kind == "kv" else sinw2
                p.emit("sync", _fn("dma_start", cosb[:],
                                   csrc[:, g * 512:(g + 1) * 512]),
                       waits=[(s_v, m_csfree)], inc=s_x, inc_amt=16,
                       tag=f"{kind}.cos{g}")
                p.emit("sync", _fn("dma_start", sinb[:],
                                   ssrc[:, g * 512:(g + 1) * 512]),
                       inc=s_x, inc_amt=16, tag=f"{kind}.sin{g}")
                m_cs = p.mark(s_x)

                ev = []
                prev_ev = m_xnf_ev
                for pl in range(4):
                    pt = g * 4 + pl
                    for f in range(NFT):
                        p.emit("tensor",
                               _fn("transpose",
                                   psE_bf[:, f * 128:(f + 1) * 128],
                                   slot(pt)[:, f * 128:(f + 1) * 128],
                                   ident[:]),
                               waits=([(s_v, prev_ev), (s_v, m_cons_E)]
                                      if f == 0 else []),
                               inc=(s_pe if f == NFT - 1 else None),
                               tag=f"{kind}.T{pt}.{f}")
                    mtp = p.mark(s_pe)
                    dst = xnf4[:].rearrange("q (f t) -> q f t", t=512)
                    dst = dst[:, :, pl * 128:(pl + 1) * 128]
                    src = psE_bf[:].rearrange("q (f t) -> q f t", t=128)
                    p.emit("vector", _fn("tensor_copy", dst, src),
                           waits=[(s_pe, mtp), (s_pe, m_mm_done)], inc=s_v,
                           tag=f"{kind}.ev{pt}")
                    ev.append(p.mark(s_v))
                    prev_ev = ev[-1]
                m_xnf_ev = ev[-1]

                def xnf_t(f, width=512, off=0):
                    return xnf4[:, f * 512 + off: f * 512 + off + width]

                if kind == "kv":
                    def wk_t(f, kvh):
                        return rw[:, f * 512 + kvh * 128: f * 512 + (kvh + 1) * 128]

                    def wv_t(f):
                        return rw[:, 8192 + f * 512: 8192 + (f + 1) * 512]

                    for kvh in range(NKV):
                        for f in range(NFT):
                            p.emit("tensor",
                                   _fn("matmul",
                                       psO[:, kvh * 512:(kvh + 1) * 512],
                                       wk_t(f, kvh), xnf_t(f),
                                       start=(f == 0), stop=(f == NFT - 1)),
                                   waits=([(s_v, m_xnf_ev), (s_w, mw),
                                           (s_v, m_rope)]
                                          if (kvh == 0 and f == 0) else []),
                                   inc=(s_pe if f == NFT - 1 else None),
                                   tag=f"p2.K{g}.{kvh}.{f}")
                    mk = p.mark(s_pe)
                    for pl in range(4):
                        for f in range(NFT):
                            p.emit("tensor",
                                   _fn("matmul",
                                       psE[:, pl * 512:(pl + 1) * 512],
                                       xnf_t(f, 128, pl * 128), wv_t(f),
                                       start=(f == 0), stop=(f == NFT - 1)),
                                   waits=([(s_v, m_cons_E)]
                                          if (pl == 0 and f == 0) else []),
                                   inc=(s_pe if f == NFT - 1 else None),
                                   tag=f"p2.V{g}.{pl}.{f}")
                    m_mm_done = p.mark(s_pe)
                    mvv = m_mm_done

                    for kvh in range(NKV):
                        ps = psO[:, kvh * 512:(kvh + 1) * 512]
                        kd = K_kvh(kvh)[:, g * 512:(g + 1) * 512]
                        p.emit("vector", _fn("tensor_copy", kraw[:], ps),
                               waits=[(s_pe, mk), (s_x, m_cs),
                                      (s_pe, p.mark(s_pe))],
                               inc=s_v, tag=f"p2.kr{g}.{kvh}")
                        mkraw = p.mark(s_v)
                        p.emit("tensor", _fn("matmul", ps, rmat[:], kraw[:],
                                             start=True, stop=True),
                               waits=[(s_v, mkraw)], inc=s_pe,
                               tag=f"p2.rot{g}.{kvh}")
                        mrot = p.mark(s_pe)
                        p.emit("vector", _fn("tensor_tensor", rt1[:], kraw[:],
                                             cosb[:], ALU.mult),
                               tag=f"p2.r{g}.{kvh}.a")
                        p.emit("vector", _fn("tensor_tensor", rt2[:], ps,
                                             sinb[:], ALU.mult),
                               waits=[(s_pe, mrot)],
                               tag=f"p2.r{g}.{kvh}.b")
                        p.emit("vector", _fn("tensor_tensor", kd, rt1[:],
                                             rt2[:], ALU.add),
                               inc=s_v, tag=f"p2.r{g}.{kvh}.c")
                        p.emit("vector", _fn("tensor_tensor", sqb[:], kraw[:],
                                             kraw[:], ALU.mult),
                               inc=s_v, tag=f"p2.sq{g}.{kvh}")
                        msq = p.mark(s_v)
                        p.emit("tensor", _fn("matmul", ps[0:1, :],
                                             onesb[:, 0:1], sqb[:],
                                             start=True, stop=True),
                               waits=[(s_v, msq)], inc=s_pe,
                               tag=f"p2.nm{g}.{kvh}")
                        mnm = p.mark(s_pe)
                        p.emit("vector", _fn("reduce_max",
                                             vecs[0:1, V_NTMP:V_NTMP + 1],
                                             ps[0:1, :], AX.X),
                               waits=[(s_pe, mnm)], tag=f"p2.nr{g}.{kvh}")
                        gap()
                        ca = V_KN + kvh * 2 + (g % 2)
                        cb = V_KN + kvh * 2 + ((g + 1) % 2)
                        if g == 0:
                            p.emit("vector", _fn("tensor_copy",
                                                 vecs[0:1, cb:cb + 1],
                                                 vecs[0:1, V_NTMP:V_NTMP + 1]),
                                   inc=s_v, tag=f"p2.nc{g}.{kvh}")
                        else:
                            p.emit("vector", _fn("tensor_tensor",
                                                 vecs[0:1, cb:cb + 1],
                                                 vecs[0:1, ca:ca + 1],
                                                 vecs[0:1, V_NTMP:V_NTMP + 1],
                                                 ALU.max),
                                   inc=s_v, tag=f"p2.nc{g}.{kvh}")
                    m_rope = p.mark(s_v)
                    m_csfree = m_rope
                    for pl in range(4):
                        p.emit("vector", _fn("tensor_copy", V_pt(g * 4 + pl)[:],
                                             psE[:, pl * 512:(pl + 1) * 512]),
                               waits=[(s_pe, mvv)] if pl == 0 else [],
                               inc=(s_v if pl == 3 else None),
                               tag=f"p2.vev{g}.{pl}")
                    m_cons_E = p.mark(s_v)
                else:
                    def wq_t(f, h):
                        return rw[:, f * C + h * 128: f * C + (h + 1) * 128]

                    rq_r = rq[:].rearrange("q (s e) -> q s e", e=C)
                    m_ropeq = [0, 0, 0, 0]
                    for h in range(NH):
                        qtr = h % 4
                        ps = psO[:, qtr * 512:(qtr + 1) * 512]
                        for f in range(NFT):
                            p.emit("tensor",
                                   _fn("matmul", ps, wq_t(f, h), xnf_t(f),
                                       start=(f == 0), stop=(f == NFT - 1)),
                                   waits=([(s_v, m_xnf_ev), (s_w, mw)]
                                          if (h == 0 and f == 0) else []) +
                                         ([(s_v, m_ropeq[qtr])] if f == 0 else []),
                                   inc=(s_pe if f == NFT - 1 else None),
                                   tag=f"p3.Q{g}.{h}.{f}")
                        mq = p.mark(s_pe)
                        if h == NH - 1:
                            m_mm_done = mq
                        dst = rq_r[:, 4 * g:4 * g + 4, h * 128:(h + 1) * 128]
                        p.emit("vector", _fn("tensor_copy", kraw[:], ps),
                               waits=[(s_pe, mq)] + ([(s_x, m_cs)]
                                                     if h == 0 else []),
                               inc=s_v, tag=f"p3.kr{g}.{h}")
                        mkraw = p.mark(s_v)
                        p.emit("tensor", _fn("matmul", ps, rmat[:], kraw[:],
                                             start=True, stop=True),
                               waits=[(s_v, mkraw)], inc=s_pe,
                               tag=f"p3.rot{g}.{h}")
                        mrot = p.mark(s_pe)
                        p.emit("vector", _fn("tensor_tensor", rt1[:], kraw[:],
                                             cosb[:], ALU.mult),
                               tag=f"p3.r{g}.{h}.a")
                        p.emit("vector", _fn("tensor_tensor", rt2[:], ps,
                                             sinb[:], ALU.mult),
                               waits=[(s_pe, mrot)],
                               tag=f"p3.r{g}.{h}.b")
                        p.emit("vector",
                               _fn("tensor_tensor", dst,
                                   rt1[:].rearrange("q (s e) -> q s e", e=128),
                                   rt2[:].rearrange("q (s e) -> q s e", e=128),
                                   ALU.add),
                               inc=s_v, tag=f"p3.r{g}.{h}.c")
                        p.emit("vector", _fn("tensor_tensor", sqb[:], kraw[:],
                                             kraw[:], ALU.mult),
                               inc=s_v, tag=f"p3.sq{g}.{h}")
                        msq = p.mark(s_v)
                        p.emit("tensor", _fn("matmul", ps[0:1, :],
                                             onesb[:, 0:1], sqb[:],
                                             start=True, stop=True),
                               waits=[(s_v, msq)], inc=s_pe,
                               tag=f"p3.nm{g}.{h}")
                        mnm = p.mark(s_pe)
                        p.emit("vector", _fn("reduce_max",
                                             vecs[0:1, V_NTMP:V_NTMP + 1],
                                             ps[0:1, :], AX.X),
                               waits=[(s_pe, mnm)], tag=f"p3.nr{g}.{h}")
                        gap()
                        ca = V_QN + h * 2 + (g % 2)
                        cb = V_QN + h * 2 + ((g + 1) % 2)
                        if g == 0:
                            p.emit("vector", _fn("tensor_copy",
                                                 vecs[0:1, cb:cb + 1],
                                                 vecs[0:1, V_NTMP:V_NTMP + 1]),
                                   inc=s_v, tag=f"p3.nc{g}.{h}")
                        else:
                            p.emit("vector", _fn("tensor_tensor",
                                                 vecs[0:1, cb:cb + 1],
                                                 vecs[0:1, ca:ca + 1],
                                                 vecs[0:1, V_NTMP:V_NTMP + 1],
                                                 ALU.max),
                                   inc=s_v, tag=f"p3.nc{g}.{h}")
                        m_ropeq[qtr] = p.mark(s_v)
                    m_rope = p.mark(s_v)
                    m_csfree = m_rope

        if phases >= 2:
            p.phase()
            with ExitStack() as ph:
                qkv_pass("kv", ph)
        if phases >= 3:
            p.phase()
            with ExitStack() as ph:
                qkv_pass("q", ph)

        if phases >= 4:
            p.phase()
            with ExitStack() as ph:
                xt4 = ph.enter_context(nc.sbuf_tensor("p4x", [128, C], BF16))
                znb = ph.enter_context(nc.sbuf_tensor("p4zn", [128, C], F32))
                x2b = ph.enter_context(nc.sbuf_tensor("p4x2b", [128, C], F32))
                pbf = ph.enter_context(nc.sbuf_tensor("p4pb", [128, T], BF16))
                pnb = ph.enter_context(nc.sbuf_tensor("p4pn", [128, T], BF16))
                ptsb = ph.enter_context(
                    nc.sbuf_tensor("p4pt", [128, 2 * T], BF16))
                ysb = ph.enter_context(nc.sbuf_tensor("p4y", [128, C], BF16))
                xn2tm = ph.enter_context(nc.sbuf_tensor("p4x2", [128, C], BF16))

                for f in range(NFT):
                    p.emit("gpsimd", _fn("dma_start", rw[:, f * C:(f + 1) * C],
                                         wp_in[f * 128:(f + 1) * 128, :]),
                           inc=s_w, inc_amt=16, tag=f"p4.wp{f}")
                mwp = p.mark(s_w)

                def wp_t(f, oc):
                    return rw[:, f * C + oc * 512: f * C + (oc + 1) * 512]

                for h in range(NH):
                    p.emit("vector", _fn("tensor_tensor",
                                         vecs[0:1, V_CMX:V_CMX + 1],
                                         vecs[0:1, V_QN + h * 2:
                                              V_QN + h * 2 + 1],
                                         vecs[0:1, V_KN + (h // 4) * 2:
                                              V_KN + (h // 4) * 2 + 1],
                                         ALU.mult),
                           inc=s_v, tag=f"p4.bb{h}")
                    mbb = p.mark(s_v)
                    p.emit("scalar", _fn("activation",
                                         vecs[0:1, V_CMX + 1:V_CMX + 2],
                                         vecs[0:1, V_CMX:V_CMX + 1], AF.Ln,
                                         bias=0.0, scale=SCALE * SCALE),
                           waits=[(s_v, mbb)], tag=f"p4.bl{h}")
                    p.emit("scalar", _fn("activation",
                                         vecs[0:1, V_NTMP:V_NTMP + 1],
                                         vecs[0:1, V_CMX + 1:V_CMX + 2],
                                         AF.Exp, bias=0.0, scale=0.5),
                           inc=s_a, tag=f"p4.be{h}")
                    mbe = p.mark(s_a)
                    p.emit("tensor", _fn("matmul", psE[:, 0:1],
                                         onesf[0:1, 0:128],
                                         vecs[0:1, V_NTMP:V_NTMP + 1],
                                         start=True, stop=True),
                           waits=[(s_a, mbe), (s_v, p.mark(s_v))],
                           inc=s_pe, tag=f"p4.bm{h}")
                    mbm = p.mark(s_pe)
                    p.emit("vector", _fn("tensor_scalar_mul",
                                         vec(V_B16 + h), psE[:, 0:1], -1.0),
                           waits=[(s_pe, mbm)], inc=s_v, tag=f"p4.bn{h}")
                m_bias16 = p.mark(s_v)
                p.waitop("tensor", [(s_v, m_bias16)], tag="p4.biasbar")

                m_exp = 0
                m_norm = 0
                m_tp = 0
                m_ptev = 0
                m_pv = 0
                m_yev = [0, 0]
                m_proj = 0
                m_zev = 0
                m_x2 = 0
                m_xnt = 0
                m_m2cp = 0
                m_slot_ev = 0

                def head_pe(qt, h):
                    nk = qt + 1
                    L = nk * 128
                    nch = (L + 511) // 512
                    kvh = h // 4
                    first = [(s_a, m_exp), (s_v, m_zev)]
                    marks = []
                    for c_ in range(nch):
                        wdt = min(512, L - c_ * 512)
                        last = (c_ == nch - 1)
                        p.emit("tensor",
                               _fn("matmul", psE[:, c_ * 512:c_ * 512 + wdt],
                                   slot(qt)[:, h * 128:(h + 1) * 128],
                                   K_kvh(kvh)[:, c_ * 512:c_ * 512 + wdt],
                                   start=True, stop=not last),
                               waits=first if c_ == 0 else [],
                               tag=f"p4.S{qt}.{h}.{c_}")
                        first = []
                    p.emit("tensor",
                           _fn("matmul", psE[:, qt * 128:L], ident[:], dmask[:],
                               start=False, stop=True),
                           inc=s_pe, tag=f"p4.M{qt}.{h}")
                    marks.append(p.mark(s_pe))
                    return marks

                def head_va(qt, h, marks):
                    nonlocal m_exp, m_norm
                    L = (qt + 1) * 128
                    p.emit("scalar", _fn("activation", pbf[:, 0:L], psE[:, 0:L],
                                         AF.Exp, bias=vec(V_B16 + h), scale=1.0,
                                         accum_out=vec(V_SSUM)),
                           waits=[(s_pe, marks[-1]), (s_v, m_norm)],
                           inc=s_a, tag=f"p4.ex{qt}.{h}")
                    m_exp = p.mark(s_a)
                    p.emit("vector", _fn("reciprocal", vec(V_RSUM), vec(V_SSUM)),
                           waits=[(s_a, m_exp)], tag=f"p4.rs{qt}.{h}")
                    gap()
                    p.emit("vector", _fn("tensor_scalar_mul", pnb[:, 0:L],
                                         pbf[:, 0:L], vec(V_RSUM)),
                           waits=[(s_pe, m_tp)], inc=s_v, tag=f"p4.nm{qt}.{h}")
                    m_norm = p.mark(s_v)

                def head_tail_pe(qt, h):
                    nonlocal m_tp, m_pv, m_ptev, m_yev
                    nk = qt + 1
                    kvh = h // 4
                    pr = h // 2
                    hh = h % 2
                    pt_r = ptsb[:].rearrange("q (kt e) -> q kt e", e=256)
                    for kt in range(nk):
                        p.emit("tensor",
                               _fn("transpose",
                                   psO_bf[:, kt * 128:(kt + 1) * 128],
                                   pnb[:, kt * 128:(kt + 1) * 128], ident[:]),
                               waits=([(s_v, m_norm), (s_v, m_ptev),
                                       (s_v, m_slot_ev)]
                                      if kt == 0 else []),
                               inc=(s_pe if kt == nk - 1 else None),
                               tag=f"p4.T{qt}.{h}.{kt}")
                    m_tp = p.mark(s_pe)
                    dst = pt_r[:, 0:nk, hh * 128:(hh + 1) * 128]
                    src_ = psO_bf[:, 0:nk * 128].rearrange(
                        "q (kt e) -> q kt e", e=128)
                    p.emit("vector", _fn("tensor_copy", dst, src_),
                           waits=[(s_pe, m_tp), (s_pe, m_pv)], inc=s_v,
                           tag=f"p4.PT{qt}.{h}")
                    m_ptev = p.mark(s_v)
                    if hh == 1:
                        ycol = (pr % 2) * 512
                        for kt in range(nk):
                            p.emit("tensor",
                                   _fn("matmul", psO[:, ycol:ycol + 256],
                                       V_pt(kt)[:, kvh * 128:(kvh + 1) * 128],
                                       ptsb[:, kt * 256:(kt + 1) * 256],
                                       start=(kt == 0), stop=(kt == nk - 1)),
                                   waits=([(s_v, m_ptev),
                                           (s_v, m_yev[pr % 2])]
                                          if kt == 0 else []),
                                   inc=(s_pe if kt == nk - 1 else None),
                                   tag=f"p4.PV{qt}.{pr}.{kt}")
                        m_pv = p.mark(s_pe)
                        p.emit("vector",
                               _fn("tensor_copy",
                                   ysb[:, pr * 256:(pr + 1) * 256],
                                   psO[:, ycol:ycol + 256]),
                               waits=[(s_pe, m_pv), (s_pe, m_proj)],
                               inc=s_v, tag=f"p4.ye{qt}.{pr}")
                        m_yev[pr % 2] = p.mark(s_v)

                pending_x2 = None

                def flush_x2():
                    nonlocal m_xnt, m_slot_ev, pending_x2
                    if pending_x2 is None:
                        return
                    fqt, fmxn = pending_x2
                    pending_x2 = None
                    for f in range(NFT):
                        p.emit("tensor",
                               _fn("transpose",
                                   psO_bf[:, f * 128:(f + 1) * 128],
                                   xn2tm[:, f * 128:(f + 1) * 128], ident[:]),
                               waits=([(s_v, fmxn), (s_v, m_ptev)]
                                      if f == 0 else []),
                               inc=(s_pe if f == NFT - 1 else None),
                               tag=f"p4.xT{fqt}.{f}")
                    m_xnt = p.mark(s_pe)
                    p.emit("vector", _fn("tensor_copy", slot(fqt)[:],
                                         psO_bf[:]),
                           waits=[(s_pe, m_xnt)], inc=s_v, tag=f"p4.xe{fqt}")
                    m_slot_ev = p.mark(s_v)

                for qt in range(NPT):
                    p.emit("gpsimd", _fn("dma_start", xt4[:],
                                         x_in[qt * 128:(qt + 1) * 128, :]),
                           waits=[(s_v, m_x2)], inc=s_w, inc_amt=16,
                           tag=f"p4.ldx{qt}")
                    mx4 = p.mark(s_w)
                    for h in range(NH):
                        marks = head_pe(qt, h)
                        if h == 2:
                            flush_x2()
                        if h > 0:
                            head_tail_pe(qt, h - 1)
                        head_va(qt, h, marks)
                    head_tail_pe(qt, NH - 1)
                    for f in range(NFT):
                        for oc in range(4):
                            p.emit("tensor",
                                   _fn("matmul", psE[:, oc * 512:(oc + 1) * 512],
                                       ysb[:, f * 128:(f + 1) * 128],
                                       wp_t(f, oc),
                                       start=(f == 0), stop=(f == NFT - 1)),
                                   waits=([(s_v, m_yev[0]), (s_v, m_yev[1]),
                                           (s_a, m_exp), (s_w, mwp)]
                                          if (oc == 0 and f == 0) else []),
                                   inc=(s_pe if (oc == 3 and f == NFT - 1)
                                        else None),
                                   tag=f"p4.pj{qt}.{oc}.{f}")
                    m_proj = p.mark(s_pe)
                    for i in range(4):
                        p.emit("vector", _fn("bn_stats", vec(V_BNST + 6 * i, 6),
                                             psE[:, 512 * i:512 * (i + 1)]),
                               waits=([(s_a, m_m2cp), (s_pe, m_proj)]
                                      if i == 0 else []),
                               tag=f"p4.zb{qt}.{i}")
                    gap()
                    p.emit("vector", _fn("bn_aggr", vec(V_STAT, 2),
                                         vec(V_BNST, 24)),
                           inc=s_v, tag=f"p4.za{qt}")
                    mzs = p.mark(s_v)
                    p.emit("scalar", _fn("activation", vec(V_LNT),
                                         vec(V_STAT + 1), AF.Ln,
                                         bias=epsb[:, 0:1], scale=1.0),
                           waits=[(s_v, mzs)], tag=f"p4.zl{qt}")
                    p.emit("scalar", _fn("activation", vec(V_RSTD), vec(V_LNT),
                                         AF.Exp, bias=0.0, scale=-0.5),
                           inc=s_a, tag=f"p4.zr{qt}")
                    mza = p.mark(s_a)
                    p.emit("vector", _fn("tensor_scalar", znb[:], psE[:],
                                         vec(V_STAT), vec(V_RSTD),
                                         ALU.subtract, ALU.mult),
                           waits=[(s_a, mza)], inc=s_v, tag=f"p4.zn{qt}")
                    m_zev = p.mark(s_v)
                    p.emit("vector", _fn("tensor_tensor", x2b[:], znb[:],
                                         xt4[:], ALU.add),
                           waits=[(s_w, mx4)], inc=s_v, tag=f"p4.x2{qt}")
                    m_x2 = p.mark(s_v)
                    for i in range(4):
                        p.emit("vector", _fn("bn_stats", vec(V_BNST + 6 * i, 6),
                                             x2b[:, 512 * i:512 * (i + 1)]),
                               tag=f"p4.b2{qt}.{i}")
                    gap()
                    p.emit("vector", _fn("bn_aggr", vec(V_STAT, 2),
                                         vec(V_BNST, 24)),
                           inc=s_v, tag=f"p4.a2{qt}")
                    m2s = p.mark(s_v)
                    p.emit("scalar", _fn("activation", vec(V_LNT),
                                         vec(V_STAT + 1), AF.Ln,
                                         bias=epsb[:, 0:1], scale=1.0),
                           waits=[(s_v, m2s)], tag=f"p4.l2{qt}")
                    p.emit("scalar", _fn("activation", vec(V_RSTD), vec(V_LNT),
                                         AF.Exp, bias=0.0, scale=-0.5),
                           tag=f"p4.r2{qt}")
                    p.emit("scalar", _fn("activation", vec(V_SD2 + qt),
                                         vec(V_LNT), AF.Exp, bias=0.0,
                                         scale=0.5),
                           tag=f"p4.s2{qt}")
                    p.emit("scalar", _fn("copy", vec(V_M2 + qt), vec(V_STAT)),
                           inc=s_a, tag=f"p4.m2{qt}")
                    m_m2cp = p.mark(s_a)
                    p.emit("vector", _fn("tensor_scalar", xn2tm[:], x2b[:],
                                         vec(V_STAT), vec(V_RSTD),
                                         ALU.subtract, ALU.mult),
                           waits=[(s_a, m_m2cp), (s_pe, m_xnt)], inc=s_v,
                           tag=f"p4.xn{qt}")
                    mxn = p.mark(s_v)
                    pending_x2 = (qt, mxn)
                flush_x2()

        if phases >= 5:
            for ps_i in range(4):
                sl = [4 * ps_i + j for j in range(4)]
                p.phase()
                with ExitStack() as ph:
                    h32 = [ph.enter_context(
                        nc.sbuf_tensor(f"h32{b_}_{ps_i}", [128, 2048], F32))
                        for b_ in range(2)]
                    sg = ph.enter_context(
                        nc.sbuf_tensor(f"sg_{ps_i}", [128, 2048], F32))
                    hsc4 = ph.enter_context(
                        nc.sbuf_tensor(f"hsc4_{ps_i}", [128, 4096], BF16))
                    wex = ph.enter_context(
                        nc.sbuf_tensor(f"wex_{ps_i}", [128, 4096], BF16))
                    aabn = ph.enter_context(
                        nc.sbuf_tensor(f"aabn_{ps_i}", [128, 1024], F32))
                    aab = aabn[:, 0:512]
                    nb = [aabn[:, 512 + i * 128:512 + (i + 1) * 128]
                          for i in range(4)]

                    def hscv(ic_, pt_):
                        o = (ic_ % 2) * 2048 + pt_ * 512
                        return hsc4[:, o:o + 512]

                    def half(s):
                        if s < 4:
                            return rkv[:, s * 4096:(s + 1) * 4096]
                        return wex[:]

                    def wtile(L, f):
                        h = half((2 * L + (1 if f >= 8 else 0)) % 5)
                        return h[:, (f % 8) * 512:((f % 8) + 1) * 512]

                    def dma_half(L, k, waits):
                        ic, j = divmod(L, 2)
                        dst = half((2 * L + k) % 5)[:].rearrange(
                            "q (f i) -> q f i", i=512)
                        srcw = wg_in if j == 0 else wu_in
                        s_ = srcw[k * 1024:(k + 1) * 1024,
                                  ic * 512:(ic + 1) * 512]
                        s_ = s_.rearrange("(f q) i -> q f i", q=128)
                        p.emit("gpsimd", _fn("dma_start", dst, s_),
                               waits=waits, inc=s_w, inc_amt=16,
                               tag=f"f1.w{ps_i}.{L}.{k}")
                        return p.mark(s_w)

                    mld = {}
                    dma_half(0, 0, [])
                    mld[0] = dma_half(0, 1, [])
                    dma_half(1, 0, [])
                    mld[1] = dma_half(1, 1, [])
                    dma_half(2, 0, [])
                    m_silu = 0
                    m_hf = 0
                    m_hf_prev = 0
                    m_sp_par = [0, 0]
                    m_dj = 0
                    m_hst = 0
                    m_hsev = 0
                    m_hsev2 = [0, 0]
                    pend_hst = []
                    pend_sp = None

                    def emit_hst(ic, all_=False):
                        nonlocal m_hst, m_hsev
                        todo = [e for e in pend_hst
                                if all_ or e[0] <= ic - 2]
                        pend_hst[:] = [e for e in pend_hst
                                       if not (all_ or e[0] <= ic - 2)]
                        for (ric, pt_, msp) in todo:
                            par = pt_ % 2
                            bk = psO_bf0[:, par * 1024:(par + 1) * 1024]
                            for k in range(4):
                                p.emit("tensor",
                                       _fn("transpose",
                                           bk[:, k * 128:(k + 1) * 128],
                                           hscv(ric, pt_)[:, k * 128:
                                                          (k + 1) * 128],
                                           ident[:]),
                                       waits=([(s_v, msp),
                                               (s_v, m_hsev2[par])]
                                              if k == 0 else []),
                                       inc=(s_pe if k == 3 else None),
                                       tag=f"f1.hT{ps_i}.{ric}.{pt_}.{k}")
                            m_hst = p.mark(s_pe)
                            rw_r = rw[:].rearrange("q (it t) -> q it t", t=512)
                            dstv = rw_r[:, ric * 4:(ric + 1) * 4,
                                        pt_ * 128:(pt_ + 1) * 128]
                            srcv = bk[:, 0:512].rearrange(
                                "q (k t) -> q k t", t=128)
                            p.emit("vector", _fn("tensor_copy", dstv, srcv),
                                   waits=[(s_pe, m_hst)], inc=s_v,
                                   tag=f"f1.he{ps_i}.{ric}.{pt_}")
                            m_hsev2[par] = p.mark(s_v)
                            m_hsev = m_hsev2[par]

                    def emit_sparsify():
                        nonlocal pend_sp, m_dj, m_sp_par
                        if pend_sp is None:
                            return
                        ric, mhf = pend_sp
                        pend_sp = None

                        def jv(t_, j):
                            v3 = t_.rearrange("q (g j) -> q g j", j=4)
                            return v3[:, :, j]

                        for pt_ in range(4):
                            hv = h32[ric % 2][:, pt_ * 512:(pt_ + 1) * 512]
                            p.emit("scalar", _fn("activation", aab, hv, AF.Abs),
                                   waits=[(s_v, mhf), (s_v, m_dj)],
                                   inc=s_a, tag=f"f1.ab{ps_i}.{ric}.{pt_}")
                            m_ab = p.mark(s_a)
                            gap()
                            p.emit("vector", _fn("tensor_tensor", nb[0][:],
                                                 jv(aab, 0), jv(aab, 1),
                                                 ALU.min),
                                   waits=[(s_a, m_ab)],
                                   tag=f"f1.n1{ps_i}.{ric}.{pt_}")
                            p.emit("vector", _fn("tensor_tensor", nb[1][:],
                                                 jv(aab, 0), jv(aab, 1),
                                                 ALU.max),
                                   tag=f"f1.n2{ps_i}.{ric}.{pt_}")
                            p.emit("vector", _fn("tensor_tensor", nb[2][:],
                                                 jv(aab, 2), jv(aab, 3),
                                                 ALU.min),
                                   tag=f"f1.n3{ps_i}.{ric}.{pt_}")
                            p.emit("vector", _fn("tensor_tensor", nb[3][:],
                                                 jv(aab, 2), jv(aab, 3),
                                                 ALU.max),
                                   tag=f"f1.n4{ps_i}.{ric}.{pt_}")
                            p.emit("vector", _fn("tensor_tensor", nb[0][:],
                                                 nb[0][:], nb[2][:], ALU.max),
                                   tag=f"f1.n5{ps_i}.{ric}.{pt_}")
                            p.emit("vector", _fn("tensor_tensor", nb[1][:],
                                                 nb[1][:], nb[3][:], ALU.min),
                                   tag=f"f1.n6{ps_i}.{ric}.{pt_}")
                            gap()
                            p.emit("vector", _fn("tensor_tensor", nb[0][:],
                                                 nb[0][:], nb[1][:], ALU.min),
                                   tag=f"f1.n7{ps_i}.{ric}.{pt_}")
                            gap()
                            for j in range(4):
                                p.emit("vector",
                                       _fn("tensor_tensor", jv(aab, j),
                                           jv(aab, j), nb[0][:], ALU.subtract),
                                       tag=f"f1.d{ps_i}.{ric}.{pt_}.{j}")
                            for j in range(4):
                                p.emit("vector",
                                       _fn("scalar_tensor_tensor",
                                           jv(hscv(ric, pt_)[:], j),
                                           jv(aab, j), 0.0,
                                           jv(hv, j), ALU.is_gt, ALU.mult),
                                       waits=([(s_pe, m_hst)] if j == 0
                                              else []),
                                       inc=(s_v if j == 3 else None),
                                       tag=f"f1.m{ps_i}.{ric}.{pt_}.{j}")
                            m_sp = p.mark(s_v)
                            m_dj = m_sp
                            pend_hst.append((ric, pt_, m_sp))
                        m_sp_par[ric % 2] = m_dj

                    for ic in range(16):
                        Lg, Lu = 2 * ic, 2 * ic + 1
                        for pl in range(4):
                            for f in range(NFT):
                                w0 = ([(s_w, mld.get(Lg, 0)), (s_a, m_silu),
                                       (s_v, m_hf)]
                                      if (pl == 0 and f == 0) else [])
                                p.emit("tensor",
                                       _fn("matmul",
                                           psE[:, pl * 512:(pl + 1) * 512],
                                           slot(sl[pl])[:, f * 128:(f + 1) * 128],
                                           wtile(Lg, f),
                                           start=(f == 0), stop=(f == NFT - 1)),
                                       waits=w0,
                                       inc=(s_pe if (pl == 3 and f == NFT - 1)
                                            else None),
                                       tag=f"f1.g{ps_i}.{ic}.{pl}.{f}")
                        m_gd = p.mark(s_pe)
                        if Lg + 2 < 32:
                            dma_half(Lg + 2, 1, [(s_pe, m_gd)])
                            mld[Lg + 2] = p.mark(s_w)
                        if Lg + 3 < 32:
                            dma_half(Lg + 3, 0, [])
                        emit_hst(ic)
                        for pl in range(4):
                            for f in range(NFT):
                                p.emit("tensor",
                                       _fn("matmul",
                                           psO[:, pl * 512:(pl + 1) * 512],
                                           slot(sl[pl])[:, f * 128:(f + 1) * 128],
                                           wtile(Lu, f),
                                           start=(f == 0), stop=(f == NFT - 1)),
                                       waits=([(s_w, mld.get(Lu, 0)),
                                               (s_v, m_hsev)]
                                              if (pl == 0 and f == 0) else []),
                                       inc=(s_pe if (pl == 3 and f == NFT - 1)
                                            else None),
                                       tag=f"f1.u{ps_i}.{ic}.{pl}.{f}")
                        m_umm = p.mark(s_pe)
                        if Lu + 2 < 32:
                            dma_half(Lu + 2, 1, [(s_pe, m_umm)])
                            mld[Lu + 2] = p.mark(s_w)
                        if Lu + 3 < 32:
                            dma_half(Lu + 3, 0, [])
                        p.emit("scalar", _fn("activation", sg[:],
                                             psE[:, 0:2048], AF.Silu),
                               waits=[(s_pe, m_gd), (s_v, m_hf)],
                               inc=s_a, tag=f"f1.si{ps_i}.{ic}")
                        m_silu = p.mark(s_a)
                        hd_ = h32[ic % 2][:].rearrange("q (pt e) -> q pt e",
                                                       e=512)
                        sg3 = sg[:].rearrange("q (pt e) -> q pt e", e=512)
                        pu3 = psO[:].rearrange("q (pt e) -> q pt e", e=512)
                        p.emit("vector", _fn("tensor_tensor", hd_, sg3, pu3,
                                             ALU.mult),
                               waits=[(s_a, m_silu), (s_pe, m_umm),
                                      (s_v, m_sp_par[ic % 2])],
                               inc=s_v, tag=f"f1.h{ps_i}.{ic}")
                        m_hf_prev = m_hf
                        m_hf = p.mark(s_v)
                        emit_sparsify()
                        pend_sp = (ic, m_hf)
                    emit_sparsify()
                    emit_hst(99, all_=True)

                p.phase()
                with ExitStack() as ph:
                    fbf = ph.enter_context(
                        nc.sbuf_tensor(f"fbf_{ps_i}", [128, 4 * C], BF16))
                    wd0 = ph.enter_context(
                        nc.sbuf_tensor(f"wd0_{ps_i}", [128, 4096], BF16))
                    wd1 = ph.enter_context(
                        nc.sbuf_tensor(f"wd1_{ps_i}", [128, 4096], BF16))
                    fo1 = ph.enter_context(
                        nc.sbuf_tensor(f"fo1_{ps_i}", [128, 512], F32))
                    fo2 = ph.enter_context(
                        nc.sbuf_tensor(f"fo2_{ps_i}", [128, 512], F32))
                    fo3 = ph.enter_context(
                        nc.sbuf_tensor(f"fo3_{ps_i}", [128, 512], F32))
                    wdsl = [wd0, wd1]

                    def hsf_t(it, pl):
                        return rw[:, it * 512 + pl * 128: it * 512 + (pl + 1) * 128]

                    def dma_wd(os_, ic4, waits):
                        a = ic4 % 2
                        dst = wdsl[a][:].rearrange("q (it o) -> q it o", o=512)
                        src = wd_in[ic4 * 1024:(ic4 + 1) * 1024,
                                    os_ * 512:(os_ + 1) * 512]
                        src = src.rearrange("(it q) o -> q it o", q=128)
                        p.emit("gpsimd", _fn("dma_start", dst, src),
                               waits=waits, inc=s_w, inc_amt=16,
                               tag=f"f2.wd{ps_i}.{os_}.{ic4}")
                        return p.mark(s_w)

                    m_fev = 0
                    m_wmm = {}
                    for os_ in range(4):
                        mw2 = {}
                        mw2[0] = dma_wd(os_, 0,
                                        [(s_pe, m_wmm.get((os_ - 1, 7), 0))])
                        mw2[1] = dma_wd(os_, 1, [])
                        for ic4 in range(8):
                            a = ic4 % 2
                            for pl in range(4):
                                for it in range(8):
                                    itg = ic4 * 8 + it
                                    p.emit("tensor",
                                           _fn("matmul",
                                               psE[:, pl * 512:(pl + 1) * 512],
                                               hsf_t(itg, pl),
                                               wdsl[a][:, it * 512:(it + 1) * 512],
                                               start=(ic4 == 0 and it == 0),
                                               stop=(ic4 == 7 and it == 7)),
                                           waits=([(s_w, mw2[ic4]),
                                                   (s_v, m_fev)]
                                                  if (pl == 0 and it == 0)
                                                  else []),
                                           inc=(s_pe if (pl == 3 and it == 7)
                                                else None),
                                           tag=f"f2.m{ps_i}.{os_}.{ic4}.{pl}")
                            m_wmm[(os_, ic4)] = p.mark(s_pe)
                            if ic4 + 2 < 8:
                                mw2[ic4 + 2] = dma_wd(
                                    os_, ic4 + 2,
                                    [(s_pe, m_wmm[(os_, ic4)])])
                        mlast = m_wmm[(os_, 7)]
                        for pl in range(4):
                            p.emit("vector",
                                   _fn("bn_stats",
                                       vec(V_BNF + pl * 24 + os_ * 6, 6),
                                       psE[:, pl * 512:(pl + 1) * 512]),
                                   waits=[(s_pe, mlast)] if pl == 0 else [],
                                   tag=f"f2.bn{ps_i}.{os_}.{pl}")
                            p.emit("vector",
                                   _fn("tensor_copy",
                                       fbf[:, pl * C + os_ * 512:
                                           pl * C + (os_ + 1) * 512],
                                       psE[:, pl * 512:(pl + 1) * 512]),
                                   inc=(s_v if pl == 3 else None),
                                   tag=f"f2.fe{ps_i}.{os_}.{pl}")
                        m_fev = p.mark(s_v)

                    if debug and ps_i == 3:
                        p.emit("sync", _fn("dma_start", dbg_fbf, fbf[:]),
                               waits=[(s_v, m_fev)], inc=s_out, inc_amt=16,
                               tag="dbg.fbf")
                    m_xnt5 = 0
                    m_fin = 0
                    m_store = p.mark(s_out)
                    for pl in range(4):
                        gap()
                        p.emit("vector", _fn("bn_aggr", vec(V_M3 + pl * 2, 2),
                                             vec(V_BNF + pl * 24, 24)),
                               waits=[(s_v, m_fev)] if pl == 0 else [],
                               inc=s_v, tag=f"f2.ag{ps_i}.{pl}")
                        mag = p.mark(s_v)
                        p.emit("scalar", _fn("activation", vec(V_LNT),
                                             vec(V_M3 + pl * 2 + 1), AF.Ln,
                                             bias=epsb[:, 0:1], scale=1.0),
                               waits=[(s_v, mag)], tag=f"f2.l{ps_i}.{pl}")
                        p.emit("scalar", _fn("activation", vec(V_RSTD3 + pl),
                                             vec(V_LNT), AF.Exp, bias=0.0,
                                             scale=-0.5),
                               inc=s_a, tag=f"f2.r{ps_i}.{pl}")
                        mra = p.mark(s_a)
                        for f in range(NFT):
                            p.emit("tensor",
                                   _fn("transpose",
                                       psO_bf0[:, f * 128:(f + 1) * 128],
                                       slot(sl[pl])[:, f * 128:(f + 1) * 128],
                                       ident[:]),
                                   waits=[(s_v, m_fin)] if f == 0 else [],
                                   inc=(s_pe if f == NFT - 1 else None),
                                   tag=f"f2.xT{ps_i}.{pl}.{f}")
                        m_xnt5 = p.mark(s_pe)
                        for os_ in range(4):
                            cs = slice(os_ * 512, (os_ + 1) * 512)
                            p.emit("vector",
                                   _fn("tensor_scalar", fo1[:], psO_bf0[:, cs],
                                       vec(V_SD2 + sl[pl]), vec(V_M2 + sl[pl]),
                                       ALU.mult, ALU.add),
                                   waits=([(s_pe, m_xnt5), (s_a, mra),
                                           (s_out, m_store)]
                                          if os_ == 0 else
                                          [(s_out, m_store)]),
                                   tag=f"f2.o1{ps_i}.{pl}.{os_}")
                            p.emit("vector",
                                   _fn("tensor_scalar", fo2[:],
                                       fbf[:, pl * C + os_ * 512:
                                           pl * C + (os_ + 1) * 512],
                                       vec(V_M3 + pl * 2),
                                       vec(V_RSTD3 + pl),
                                       ALU.subtract, ALU.mult),
                                   tag=f"f2.o2{ps_i}.{pl}.{os_}")
                            p.emit("vector", _fn("tensor_tensor", fo3[:],
                                                 fo1[:], fo2[:], ALU.add),
                                   inc=s_v, tag=f"f2.o3{ps_i}.{pl}.{os_}")
                            mo = p.mark(s_v)
                            r0 = sl[pl] * 128
                            p.emit("sync", _fn("dma_start",
                                               out[r0:r0 + 128, cs], fo3[:]),
                                   waits=[(s_v, mo)], inc=s_out, inc_amt=16,
                                   tag=f"f2.st{ps_i}.{pl}.{os_}")
                            m_store = p.mark(s_out)
                        m_fin = p.mark(s_v)

            p.phase()
            p.waitop("sync", [(s_out, p.mark(s_out))], tag="final")

        if debug:
            p.phase()
            nst = 0
            p.emit("sync", _fn("dma_start", dbg_rq, rq[:]), inc=s_out,
                   inc_amt=16, tag="dbg.rq")
            p.emit("sync", _fn("dma_start", dbg_rkv, rkv[:]), inc=s_out,
                   inc_amt=16, tag="dbg.rkv")
            p.emit("sync", _fn("dma_start", dbg_rw, rw[:]), inc=s_out,
                   inc_amt=16, tag="dbg.rw")
            p.emit("sync", _fn("dma_start", dbg_vecs, vecs[:]), inc=s_out,
                   inc_amt=16, tag="dbg.vecs")
            p.waitop("sync", [(s_out, p.mark(s_out))], tag="dbg.final")

        p.simulate()

        with nc.Block() as block:
            @block.sync
            def _(e):
                for f in p.ops["sync"]:
                    f(e)

            @block.gpsimd
            def _(e):
                for f in p.ops["gpsimd"]:
                    f(e)

            @block.tensor
            def _(e):
                for f in p.ops["tensor"]:
                    f(e)

            @block.vector
            def _(e):
                for f in p.ops["vector"]:
                    f(e)

            @block.scalar
            def _(e):
                for f in p.ops["scalar"]:
                    f(e)

    return nc


def make_consts():
    ident = np.eye(128, dtype=np.float32).astype(ml_dtypes.bfloat16)
    mask = np.where(np.arange(128)[None, :] <= np.arange(128)[:, None],
                    np.float32(0.0), np.float32(NEG)).astype(ml_dtypes.bfloat16)
    inv = (10000.0 ** (-(np.arange(0, HD, 2, dtype=np.float64) / HD)))
    th = np.arange(T, dtype=np.float64)[None, :] * inv[:, None]
    cos = np.cos(th).astype(np.float32)
    sin = np.sin(th).astype(np.float32)
    cosf = np.concatenate([cos, cos], axis=0).astype(ml_dtypes.bfloat16)
    sinf = np.concatenate([sin, sin], axis=0).astype(ml_dtypes.bfloat16)
    cosf2 = (np.concatenate([cos, cos], axis=0) / math.sqrt(HD)).astype(
        ml_dtypes.bfloat16)
    sinf2 = (np.concatenate([sin, sin], axis=0) / math.sqrt(HD)).astype(
        ml_dtypes.bfloat16)
    rot = np.zeros((128, 128), np.float32)
    for d in range(64):
        rot[d + 64, d] = -1.0
        rot[d, d + 64] = 1.0
    rot = rot.astype(ml_dtypes.bfloat16)
    onesbv = np.ones((128, 8), dtype=ml_dtypes.bfloat16)
    onesfv = np.ones((128, 128), dtype=np.float32)
    return ident, mask, cosf, sinf, rot, cosf2, sinf2, onesbv, onesfv


def kernel(L0_attn_x, L0_attn_Wq, L0_attn_Wk, L0_attn_Wv, L0_attn_Wproj,
           L0_ffn_w_gate, L0_ffn_w_up, L0_ffn_w_down,
           _phases=5, _debug=False):
    x = np.ascontiguousarray(np.asarray(L0_attn_x, dtype=np.float32))
    wq = np.ascontiguousarray(np.asarray(L0_attn_Wq, dtype=np.float32))
    wk = np.ascontiguousarray(np.asarray(L0_attn_Wk, dtype=np.float32))
    wv = np.ascontiguousarray(np.asarray(L0_attn_Wv, dtype=np.float32))
    wp = np.ascontiguousarray(np.asarray(L0_attn_Wproj, dtype=np.float32))
    wg = np.ascontiguousarray(np.asarray(L0_ffn_w_gate, dtype=np.float32))
    wu = np.ascontiguousarray(np.asarray(L0_ffn_w_up, dtype=np.float32))
    wd = np.ascontiguousarray(np.asarray(L0_ffn_w_down, dtype=np.float32))
    ident, mask, cosf, sinf, rot, cosf2, sinf2, onesbv, onesfv = make_consts()
    epsv = np.full((128, 8), EPS, dtype=np.float32)

    nc = build_nc(phases=_phases, debug=_debug)
    in_maps = []
    for c in range(8):
        in_maps.append({
            "x": np.ascontiguousarray(x[c]),
            "wq": wq, "wk": wk, "wv": wv, "wp": wp,
            "wg": wg, "wu": wu, "wd": wd,
            "identw": ident, "maskw": mask, "cosw": cosf, "sinw": sinf,
            "epsw": epsv, "rotw": rot, "cosw2": cosf2, "sinw2": sinf2,
            "onesbw": onesbv, "onesfw": onesfv,
        })
    res = run_bass_kernel_spmd(nc, in_maps, core_ids=list(range(8)))
    outs = np.stack([np.asarray(r["out"], dtype=np.float32)
                     for r in res.results], axis=0)
    if _debug:
        dbgs = [{k: np.asarray(v) for k, v in r.items() if k.startswith("dbg")}
                for r in res.results]
        return outs, dbgs
    return outs


# version 13
# speedup vs baseline: 1.0214x; 1.0214x over previous
import math
from collections import defaultdict
from contextlib import ExitStack

import numpy as np
import ml_dtypes

import concourse.bass as bass
import concourse.mybir as mybir
from concourse.bass_utils import run_bass_kernel_spmd

F32 = mybir.dt.float32
BF16 = mybir.dt.bfloat16
AF = mybir.ActivationFunctionType
ALU = mybir.AluOpType
AX = mybir.AxisListType

T = 2048
C = 2048
NH = 16
NKV = 4
HD = 128
II = 8192
NPT = T // 128
NFT = C // 128
EPS = 1e-5
SCALE = 1.0 / math.sqrt(HD)
NEG = -1.0e30

ENGS = ("sync", "gpsimd", "tensor", "vector", "scalar")


class Prog:

    def __init__(self, sems):
        self.ops = {e: [] for e in ENGS}
        self.cnt = defaultdict(int)
        self.meta = []
        self.sems = list(sems)
        self.bar = {}
        self.bar_done = set(ENGS)
        self.sem_by_id = {id(s): s for s in sems}

    def mark(self, sem):
        return self.cnt[id(sem)]

    def phase(self):
        self.bar = {id(s): self.cnt[id(s)] for s in self.sems}
        self.bar_done = set()

    def emit(self, eng, fn, waits=(), inc=None, inc_amt=1, tag=""):
        waits = [(s, v) for (s, v) in waits if v > 0]
        if eng not in self.bar_done:
            self.bar_done.add(eng)
            waits = [(self.sem_by_id[i], v) for i, v in self.bar.items()
                     if v > 0] + waits
        if inc is not None:
            self.cnt[id(inc)] += inc_amt
        self.meta.append((eng, [(id(s), v) for s, v in waits],
                          [(id(inc), inc_amt)] if inc is not None else [], tag))

        def run(e, fn=fn, waits=waits, inc=inc, amt=inc_amt):
            for s, v in waits:
                e.wait_ge(s, v)
            ins = fn(e)
            if inc is not None:
                ins.then_inc(inc, amt)
        self.ops[eng].append(run)

    def waitop(self, eng, waits, tag=""):
        self.emit(eng, lambda e: None, waits=waits, tag=tag)

    def simulate(self):
        queues = {e: [] for e in ENGS}
        for (eng, waits, incs, tag) in self.meta:
            queues[eng].append((waits, incs, tag))
        heads = {e: 0 for e in ENGS}
        cnt = defaultdict(int)
        progress = True
        while progress:
            progress = False
            for e in ENGS:
                while heads[e] < len(queues[e]):
                    waits, incs, tag = queues[e][heads[e]]
                    if all(cnt[s] >= v for s, v in waits):
                        for s, a in incs:
                            cnt[s] += a
                        heads[e] += 1
                        progress = True
                    else:
                        break
        if any(heads[e] < len(queues[e]) for e in ENGS):
            lines = []
            for e in ENGS:
                if heads[e] < len(queues[e]):
                    waits, incs, tag = queues[e][heads[e]]
                    missing = [(str(s), v, cnt[s]) for s, v in waits
                               if cnt[s] < v]
                    lines.append(f"{e}[{heads[e]}/{len(queues[e])}] tag={tag} "
                                 f"missing={missing}")
            raise RuntimeError("build-time DEADLOCK:\n" + "\n".join(lines))


def _fn(method, *args, **kwargs):
    return lambda e: getattr(e, method)(*args, **kwargs)


def build_nc(phases=5, debug=False):
    nc = bass.Bass("TRN2", target_bir_lowering=False)

    x_in = nc.dram_tensor("x", [T, C], F32, kind="ExternalInput").ap()
    wq_in = nc.dram_tensor("wq", [C, C], F32, kind="ExternalInput").ap()
    wk_in = nc.dram_tensor("wk", [C, NKV * HD], F32, kind="ExternalInput").ap()
    wv_in = nc.dram_tensor("wv", [C, NKV * HD], F32, kind="ExternalInput").ap()
    wp_in = nc.dram_tensor("wp", [C, C], F32, kind="ExternalInput").ap()
    wg_in = nc.dram_tensor("wg", [C, II], F32, kind="ExternalInput").ap()
    wu_in = nc.dram_tensor("wu", [C, II], F32, kind="ExternalInput").ap()
    wd_in = nc.dram_tensor("wd", [II, C], F32, kind="ExternalInput").ap()
    identw = nc.dram_tensor("identw", [128, 128], BF16, kind="ExternalInput").ap()
    maskw = nc.dram_tensor("maskw", [128, 128], BF16, kind="ExternalInput").ap()
    rotw = nc.dram_tensor("rotw", [128, 128], BF16, kind="ExternalInput").ap()
    onesbw = nc.dram_tensor("onesbw", [128, 8], BF16, kind="ExternalInput").ap()
    onesfw = nc.dram_tensor("onesfw", [128, 128], F32, kind="ExternalInput").ap()
    cosw = nc.dram_tensor("cosw", [128, T], BF16, kind="ExternalInput").ap()
    cosw2 = nc.dram_tensor("cosw2", [128, T], BF16, kind="ExternalInput").ap()
    sinw2 = nc.dram_tensor("sinw2", [128, T], BF16, kind="ExternalInput").ap()
    epsw = nc.dram_tensor("epsw", [128, 8], F32, kind="ExternalInput").ap()
    sinw = nc.dram_tensor("sinw", [128, T], BF16, kind="ExternalInput").ap()

    out = nc.dram_tensor("out", [T, C], F32, kind="ExternalOutput").ap()
    dbg = {}
    if debug:
        dbg_rq = nc.dram_tensor("dbg_rq", [128, NPT * C], BF16,
                                kind="ExternalOutput").ap()
        dbg_rkv = nc.dram_tensor("dbg_rkv", [128, 16384], BF16,
                                 kind="ExternalOutput").ap()
        dbg_vecs = nc.dram_tensor("dbg_vecs", [128, 256], F32,
                                  kind="ExternalOutput").ap()
        dbg_rw = nc.dram_tensor("dbg_rw", [128, 16 * C], BF16,
                                kind="ExternalOutput").ap()
        dbg_fbf = nc.dram_tensor("dbg_fbf", [128, 4 * C], BF16,
                                 kind="ExternalOutput").ap()

    with ExitStack() as top:
        E_ = top.enter_context
        rq = E_(nc.sbuf_tensor("rq", [128, NPT * C], BF16))
        rkv = E_(nc.sbuf_tensor("rkv", [128, 16384], BF16))
        rw = E_(nc.sbuf_tensor("rw", [128, 16 * C], BF16))
        vecs = E_(nc.sbuf_tensor("vecs", [128, 256], F32))
        ident = E_(nc.sbuf_tensor("ident", [128, 128], BF16))
        dmask = E_(nc.sbuf_tensor("dmask", [128, 128], BF16))
        rmat = E_(nc.sbuf_tensor("rmat", [128, 128], BF16))
        onesb = E_(nc.sbuf_tensor("onesb", [128, 8], BF16))
        onesf = E_(nc.sbuf_tensor("onesf", [128, 128], F32))
        epsb = E_(nc.sbuf_tensor("epsb", [128, 8], F32))
        psE = E_(nc.psum_tensor("psE", [128, 2048], F32))
        psO = E_(nc.psum_tensor("psO", [128, 2048], F32))

        s_x = E_(nc.semaphore("s_x"))
        s_w = E_(nc.semaphore("s_w"))
        s_pe = E_(nc.semaphore("s_pe"))
        s_v = E_(nc.semaphore("s_v"))
        s_a = E_(nc.semaphore("s_a"))
        s_out = E_(nc.semaphore("s_out"))
        p = Prog([s_x, s_w, s_pe, s_v, s_a, s_out])

        V_BNST, V_STAT = 0, 24
        V_NEGM, V_BIAS, V_SSUM, V_RSUM = 26, 27, 28, 29
        V_LNT, V_RSTD = 30, 32
        V_EPS = 34
        V_CMX = 36
        V_B16 = 80
        V_KN = 204
        V_QN = 212
        V_NTMP = 250
        V_M2, V_SD2 = 48, 64
        V_BNF = 96
        V_M3 = 192
        V_RSTD3 = 200

        def vec(col, n=1):
            return vecs[:, col:col + n]

        def gap(eng="vector"):
            p.emit(eng, _fn("tensor_copy", vecs[:, 44:48], epsb[:, 0:4]),
                   tag="gap")

        def slot(s):
            return rq[:, s * C:(s + 1) * C]

        def K_kvh(k):
            return rkv[:, k * T:(k + 1) * T]

        def V_pt(pt):
            return rkv[:, 8192 + pt * 512: 8192 + (pt + 1) * 512]

        psE_bf = psE[:, 0:1024].bitcast(BF16)
        psO_bf = psO[:, 1024:2048].bitcast(BF16)
        psO_bf0 = psO[:, 0:1024].bitcast(BF16)

        p.emit("sync", _fn("dma_start", ident[:], identw), inc=s_x, inc_amt=16,
               tag="c.id")
        p.emit("sync", _fn("dma_start", dmask[:], maskw), inc=s_x, inc_amt=16,
               tag="c.mask")
        p.emit("sync", _fn("dma_start", rmat[:], rotw), inc=s_x, inc_amt=16,
               tag="c.rot")
        p.emit("sync", _fn("dma_start", onesb[:], onesbw), inc=s_x,
               inc_amt=16, tag="c.ob")
        p.emit("sync", _fn("dma_start", onesf[:], onesfw), inc=s_x,
               inc_amt=16, tag="c.of")
        p.emit("sync", _fn("dma_start", epsb[:], epsw), inc=s_x,
               inc_amt=16, tag="c.eps")

        p.phase()
        with ExitStack() as ph:
            xt = [ph.enter_context(nc.sbuf_tensor(f"p1x{i}", [128, C], F32))
                  for i in range(2)]
            consumed = [0, 0]
            for pt in range(NPT):
                b = pt % 2
                p.emit("sync", _fn("dma_start", xt[b][:],
                                   x_in[pt * 128:(pt + 1) * 128, :]),
                       waits=[(s_v, consumed[b])], inc=s_x, inc_amt=16,
                       tag=f"p1.ld{pt}")
                mx_ld = p.mark(s_x)
                for i in range(4):
                    p.emit("vector", _fn("bn_stats", vec(V_BNST + 6 * i, 6),
                                         xt[b][:, 512 * i:512 * (i + 1)]),
                           waits=[(s_x, mx_ld)] if i == 0 else [],
                           tag=f"p1.bn{pt}.{i}")
                gap()
                p.emit("vector", _fn("bn_aggr", vec(V_STAT, 2), vec(V_BNST, 24)),
                       inc=s_v, tag=f"p1.ag{pt}")
                mv = p.mark(s_v)
                p.emit("scalar", _fn("activation", vec(V_LNT), vec(V_STAT + 1),
                                     AF.Ln, bias=epsb[:, 0:1], scale=1.0),
                       waits=[(s_v, mv)], tag=f"p1.ln{pt}")
                p.emit("scalar", _fn("activation", vec(V_RSTD), vec(V_LNT),
                                     AF.Exp, bias=0.0, scale=-0.5),
                       inc=s_a, tag=f"p1.ex{pt}")
                ma = p.mark(s_a)
                p.emit("vector", _fn("tensor_scalar", slot(pt)[:], xt[b][:],
                                     vec(V_STAT), vec(V_RSTD),
                                     ALU.subtract, ALU.mult),
                       waits=[(s_a, ma)], inc=s_v, tag=f"p1.nm{pt}")
                consumed[b] = p.mark(s_v)

        def qkv_pass(kind, ph):
            xnf4 = ph.enter_context(
                nc.sbuf_tensor(f"xnf4{kind}", [128, NFT * 512], BF16))
            cosb = ph.enter_context(nc.sbuf_tensor(f"cosb{kind}", [128, 512], BF16))
            sinb = ph.enter_context(nc.sbuf_tensor(f"sinb{kind}", [128, 512], BF16))
            rt1 = ph.enter_context(nc.sbuf_tensor(f"rt1{kind}", [128, 512], F32))
            rt2 = ph.enter_context(nc.sbuf_tensor(f"rt2{kind}", [128, 512], F32))
            kraw = ph.enter_context(nc.sbuf_tensor(f"kraw{kind}", [128, 512], BF16))
            sqb = ph.enter_context(nc.sbuf_tensor(f"sqb{kind}", [128, 512], BF16))

            if kind == "kv":
                for f in range(NFT):
                    p.emit("gpsimd", _fn("dma_start",
                                         rw[:, f * 512:(f + 1) * 512],
                                         wk_in[f * 128:(f + 1) * 128, :]),
                           inc=s_w, inc_amt=16, tag=f"p2.wk{f}")
                for f in range(NFT):
                    p.emit("gpsimd", _fn("dma_start",
                                         rw[:, 8192 + f * 512:8192 + (f + 1) * 512],
                                         wv_in[f * 128:(f + 1) * 128, :]),
                           inc=s_w, inc_amt=16, tag=f"p2.wv{f}")
            else:
                for f in range(NFT):
                    p.emit("gpsimd", _fn("dma_start",
                                         rw[:, f * C:(f + 1) * C],
                                         wq_in[f * 128:(f + 1) * 128, :]),
                           inc=s_w, inc_amt=16, tag=f"p3.wq{f}")
            mw = p.mark(s_w)

            m_xnf_ev = 0
            m_rope = 0
            m_cons_E = 0
            m_mm_done = 0
            m_csfree = 0
            for g in range(4):
                csrc = cosw if kind == "kv" else cosw2
                ssrc = sinw if kind == "kv" else sinw2
                p.emit("sync", _fn("dma_start", cosb[:],
                                   csrc[:, g * 512:(g + 1) * 512]),
                       waits=[(s_v, m_csfree)], inc=s_x, inc_amt=16,
                       tag=f"{kind}.cos{g}")
                p.emit("sync", _fn("dma_start", sinb[:],
                                   ssrc[:, g * 512:(g + 1) * 512]),
                       inc=s_x, inc_amt=16, tag=f"{kind}.sin{g}")
                m_cs = p.mark(s_x)

                ev = []
                prev_ev = m_xnf_ev
                for pl in range(4):
                    pt = g * 4 + pl
                    for f in range(NFT):
                        p.emit("tensor",
                               _fn("transpose",
                                   psE_bf[:, f * 128:(f + 1) * 128],
                                   slot(pt)[:, f * 128:(f + 1) * 128],
                                   ident[:]),
                               waits=([(s_v, prev_ev), (s_v, m_cons_E)]
                                      if f == 0 else []),
                               inc=(s_pe if f == NFT - 1 else None),
                               tag=f"{kind}.T{pt}.{f}")
                    mtp = p.mark(s_pe)
                    dst = xnf4[:].rearrange("q (f t) -> q f t", t=512)
                    dst = dst[:, :, pl * 128:(pl + 1) * 128]
                    src = psE_bf[:].rearrange("q (f t) -> q f t", t=128)
                    p.emit("vector", _fn("tensor_copy", dst, src),
                           waits=[(s_pe, mtp), (s_pe, m_mm_done)], inc=s_v,
                           tag=f"{kind}.ev{pt}")
                    ev.append(p.mark(s_v))
                    prev_ev = ev[-1]
                m_xnf_ev = ev[-1]

                def xnf_t(f, width=512, off=0):
                    return xnf4[:, f * 512 + off: f * 512 + off + width]

                if kind == "kv":
                    def wk_t(f, kvh):
                        return rw[:, f * 512 + kvh * 128: f * 512 + (kvh + 1) * 128]

                    def wv_t(f):
                        return rw[:, 8192 + f * 512: 8192 + (f + 1) * 512]

                    for kvh in range(NKV):
                        for f in range(NFT):
                            p.emit("tensor",
                                   _fn("matmul",
                                       psO[:, kvh * 512:(kvh + 1) * 512],
                                       wk_t(f, kvh), xnf_t(f),
                                       start=(f == 0), stop=(f == NFT - 1)),
                                   waits=([(s_v, m_xnf_ev), (s_w, mw),
                                           (s_v, m_rope)]
                                          if (kvh == 0 and f == 0) else []),
                                   inc=(s_pe if f == NFT - 1 else None),
                                   tag=f"p2.K{g}.{kvh}.{f}")
                    mk = p.mark(s_pe)
                    for pl in range(4):
                        for f in range(NFT):
                            p.emit("tensor",
                                   _fn("matmul",
                                       psE[:, pl * 512:(pl + 1) * 512],
                                       xnf_t(f, 128, pl * 128), wv_t(f),
                                       start=(f == 0), stop=(f == NFT - 1)),
                                   waits=([(s_v, m_cons_E)]
                                          if (pl == 0 and f == 0) else []),
                                   inc=(s_pe if f == NFT - 1 else None),
                                   tag=f"p2.V{g}.{pl}.{f}")
                    m_mm_done = p.mark(s_pe)
                    mvv = m_mm_done

                    for kvh in range(NKV):
                        ps = psO[:, kvh * 512:(kvh + 1) * 512]
                        kd = K_kvh(kvh)[:, g * 512:(g + 1) * 512]
                        p.emit("vector", _fn("tensor_copy", kraw[:], ps),
                               waits=[(s_pe, mk), (s_x, m_cs),
                                      (s_pe, p.mark(s_pe))],
                               inc=s_v, tag=f"p2.kr{g}.{kvh}")
                        mkraw = p.mark(s_v)
                        p.emit("tensor", _fn("matmul", ps, rmat[:], kraw[:],
                                             start=True, stop=True),
                               waits=[(s_v, mkraw)], inc=s_pe,
                               tag=f"p2.rot{g}.{kvh}")
                        mrot = p.mark(s_pe)
                        p.emit("vector", _fn("tensor_tensor", rt1[:], kraw[:],
                                             cosb[:], ALU.mult),
                               tag=f"p2.r{g}.{kvh}.a")
                        p.emit("vector", _fn("tensor_tensor", rt2[:], ps,
                                             sinb[:], ALU.mult),
                               waits=[(s_pe, mrot)],
                               tag=f"p2.r{g}.{kvh}.b")
                        p.emit("vector", _fn("tensor_tensor", kd, rt1[:],
                                             rt2[:], ALU.add),
                               inc=s_v, tag=f"p2.r{g}.{kvh}.c")
                        p.emit("vector", _fn("tensor_tensor", sqb[:], kraw[:],
                                             kraw[:], ALU.mult),
                               inc=s_v, tag=f"p2.sq{g}.{kvh}")
                        msq = p.mark(s_v)
                        p.emit("tensor", _fn("matmul", ps[0:1, :],
                                             onesb[:, 0:1], sqb[:],
                                             start=True, stop=True),
                               waits=[(s_v, msq)], inc=s_pe,
                               tag=f"p2.nm{g}.{kvh}")
                        mnm = p.mark(s_pe)
                        p.emit("vector", _fn("reduce_max",
                                             vecs[0:1, V_NTMP:V_NTMP + 1],
                                             ps[0:1, :], AX.X),
                               waits=[(s_pe, mnm)], tag=f"p2.nr{g}.{kvh}")
                        gap()
                        ca = V_KN + kvh * 2 + (g % 2)
                        cb = V_KN + kvh * 2 + ((g + 1) % 2)
                        if g == 0:
                            p.emit("vector", _fn("tensor_copy",
                                                 vecs[0:1, cb:cb + 1],
                                                 vecs[0:1, V_NTMP:V_NTMP + 1]),
                                   inc=s_v, tag=f"p2.nc{g}.{kvh}")
                        else:
                            p.emit("vector", _fn("tensor_tensor",
                                                 vecs[0:1, cb:cb + 1],
                                                 vecs[0:1, ca:ca + 1],
                                                 vecs[0:1, V_NTMP:V_NTMP + 1],
                                                 ALU.max),
                                   inc=s_v, tag=f"p2.nc{g}.{kvh}")
                    m_rope = p.mark(s_v)
                    m_csfree = m_rope
                    for pl in range(4):
                        p.emit("vector", _fn("tensor_copy", V_pt(g * 4 + pl)[:],
                                             psE[:, pl * 512:(pl + 1) * 512]),
                               waits=[(s_pe, mvv)] if pl == 0 else [],
                               inc=(s_v if pl == 3 else None),
                               tag=f"p2.vev{g}.{pl}")
                    m_cons_E = p.mark(s_v)
                else:
                    def wq_t(f, h):
                        return rw[:, f * C + h * 128: f * C + (h + 1) * 128]

                    rq_r = rq[:].rearrange("q (s e) -> q s e", e=C)
                    m_ropeq = [0, 0, 0, 0]
                    for h in range(NH):
                        qtr = h % 4
                        ps = psO[:, qtr * 512:(qtr + 1) * 512]
                        for f in range(NFT):
                            p.emit("tensor",
                                   _fn("matmul", ps, wq_t(f, h), xnf_t(f),
                                       start=(f == 0), stop=(f == NFT - 1)),
                                   waits=([(s_v, m_xnf_ev), (s_w, mw)]
                                          if (h == 0 and f == 0) else []) +
                                         ([(s_v, m_ropeq[qtr])] if f == 0 else []),
                                   inc=(s_pe if f == NFT - 1 else None),
                                   tag=f"p3.Q{g}.{h}.{f}")
                        mq = p.mark(s_pe)
                        if h == NH - 1:
                            m_mm_done = mq
                        dst = rq_r[:, 4 * g:4 * g + 4, h * 128:(h + 1) * 128]
                        p.emit("vector", _fn("tensor_copy", kraw[:], ps),
                               waits=[(s_pe, mq)] + ([(s_x, m_cs)]
                                                     if h == 0 else []),
                               inc=s_v, tag=f"p3.kr{g}.{h}")
                        mkraw = p.mark(s_v)
                        p.emit("tensor", _fn("matmul", ps, rmat[:], kraw[:],
                                             start=True, stop=True),
                               waits=[(s_v, mkraw)], inc=s_pe,
                               tag=f"p3.rot{g}.{h}")
                        mrot = p.mark(s_pe)
                        p.emit("vector", _fn("tensor_tensor", rt1[:], kraw[:],
                                             cosb[:], ALU.mult),
                               tag=f"p3.r{g}.{h}.a")
                        p.emit("vector", _fn("tensor_tensor", rt2[:], ps,
                                             sinb[:], ALU.mult),
                               waits=[(s_pe, mrot)],
                               tag=f"p3.r{g}.{h}.b")
                        p.emit("vector",
                               _fn("tensor_tensor", dst,
                                   rt1[:].rearrange("q (s e) -> q s e", e=128),
                                   rt2[:].rearrange("q (s e) -> q s e", e=128),
                                   ALU.add),
                               inc=s_v, tag=f"p3.r{g}.{h}.c")
                        p.emit("vector", _fn("tensor_tensor", sqb[:], kraw[:],
                                             kraw[:], ALU.mult),
                               inc=s_v, tag=f"p3.sq{g}.{h}")
                        msq = p.mark(s_v)
                        p.emit("tensor", _fn("matmul", ps[0:1, :],
                                             onesb[:, 0:1], sqb[:],
                                             start=True, stop=True),
                               waits=[(s_v, msq)], inc=s_pe,
                               tag=f"p3.nm{g}.{h}")
                        mnm = p.mark(s_pe)
                        p.emit("vector", _fn("reduce_max",
                                             vecs[0:1, V_NTMP:V_NTMP + 1],
                                             ps[0:1, :], AX.X),
                               waits=[(s_pe, mnm)], tag=f"p3.nr{g}.{h}")
                        gap()
                        ca = V_QN + h * 2 + (g % 2)
                        cb = V_QN + h * 2 + ((g + 1) % 2)
                        if g == 0:
                            p.emit("vector", _fn("tensor_copy",
                                                 vecs[0:1, cb:cb + 1],
                                                 vecs[0:1, V_NTMP:V_NTMP + 1]),
                                   inc=s_v, tag=f"p3.nc{g}.{h}")
                        else:
                            p.emit("vector", _fn("tensor_tensor",
                                                 vecs[0:1, cb:cb + 1],
                                                 vecs[0:1, ca:ca + 1],
                                                 vecs[0:1, V_NTMP:V_NTMP + 1],
                                                 ALU.max),
                                   inc=s_v, tag=f"p3.nc{g}.{h}")
                        m_ropeq[qtr] = p.mark(s_v)
                    m_rope = p.mark(s_v)
                    m_csfree = m_rope

        if phases >= 2:
            p.phase()
            with ExitStack() as ph:
                qkv_pass("kv", ph)
        if phases >= 3:
            p.phase()
            with ExitStack() as ph:
                qkv_pass("q", ph)

        if phases >= 4:
            p.phase()
            with ExitStack() as ph:
                xt4 = ph.enter_context(nc.sbuf_tensor("p4x", [128, C], BF16))
                znb = ph.enter_context(nc.sbuf_tensor("p4zn", [128, C], F32))
                x2b = ph.enter_context(nc.sbuf_tensor("p4x2b", [128, C], F32))
                pbf = ph.enter_context(nc.sbuf_tensor("p4pb", [128, T], BF16))
                pnb = ph.enter_context(nc.sbuf_tensor("p4pn", [128, T], BF16))
                ptsb = ph.enter_context(
                    nc.sbuf_tensor("p4pt", [128, 2 * T], BF16))
                ysb = ph.enter_context(nc.sbuf_tensor("p4y", [128, C], BF16))
                xn2tm = ph.enter_context(nc.sbuf_tensor("p4x2", [128, C], BF16))

                for f in range(NFT):
                    p.emit("gpsimd", _fn("dma_start", rw[:, f * C:(f + 1) * C],
                                         wp_in[f * 128:(f + 1) * 128, :]),
                           inc=s_w, inc_amt=16, tag=f"p4.wp{f}")
                mwp = p.mark(s_w)

                def wp_t(f, oc):
                    return rw[:, f * C + oc * 512: f * C + (oc + 1) * 512]

                for h in range(NH):
                    p.emit("vector", _fn("tensor_tensor",
                                         vecs[0:1, V_CMX:V_CMX + 1],
                                         vecs[0:1, V_QN + h * 2:
                                              V_QN + h * 2 + 1],
                                         vecs[0:1, V_KN + (h // 4) * 2:
                                              V_KN + (h // 4) * 2 + 1],
                                         ALU.mult),
                           inc=s_v, tag=f"p4.bb{h}")
                    mbb = p.mark(s_v)
                    p.emit("scalar", _fn("activation",
                                         vecs[0:1, V_CMX + 1:V_CMX + 2],
                                         vecs[0:1, V_CMX:V_CMX + 1], AF.Ln,
                                         bias=0.0, scale=SCALE * SCALE),
                           waits=[(s_v, mbb)], tag=f"p4.bl{h}")
                    p.emit("scalar", _fn("activation",
                                         vecs[0:1, V_NTMP:V_NTMP + 1],
                                         vecs[0:1, V_CMX + 1:V_CMX + 2],
                                         AF.Exp, bias=0.0, scale=0.5),
                           inc=s_a, tag=f"p4.be{h}")
                    mbe = p.mark(s_a)
                    p.emit("tensor", _fn("matmul", psE[:, 0:1],
                                         onesf[0:1, 0:128],
                                         vecs[0:1, V_NTMP:V_NTMP + 1],
                                         start=True, stop=True),
                           waits=[(s_a, mbe), (s_v, p.mark(s_v))],
                           inc=s_pe, tag=f"p4.bm{h}")
                    mbm = p.mark(s_pe)
                    p.emit("vector", _fn("tensor_scalar_mul",
                                         vec(V_B16 + h), psE[:, 0:1], -1.0),
                           waits=[(s_pe, mbm)], inc=s_v, tag=f"p4.bn{h}")
                m_bias16 = p.mark(s_v)
                p.waitop("tensor", [(s_v, m_bias16)], tag="p4.biasbar")

                m_exp = 0
                m_norm = 0
                m_tp = 0
                m_ptev = 0
                m_pv = 0
                m_yev = [0, 0]
                m_proj = 0
                m_zev = 0
                m_x2 = 0
                m_xnt = 0
                m_m2cp = 0
                m_slot_ev = 0

                def head_pe(qt, h):
                    nk = qt + 1
                    L = nk * 128
                    nch = (L + 511) // 512
                    kvh = h // 4
                    first = [(s_a, m_exp), (s_v, m_zev)]
                    marks = []
                    for c_ in range(nch):
                        wdt = min(512, L - c_ * 512)
                        last = (c_ == nch - 1)
                        p.emit("tensor",
                               _fn("matmul", psE[:, c_ * 512:c_ * 512 + wdt],
                                   slot(qt)[:, h * 128:(h + 1) * 128],
                                   K_kvh(kvh)[:, c_ * 512:c_ * 512 + wdt],
                                   start=True, stop=not last),
                               waits=first if c_ == 0 else [],
                               tag=f"p4.S{qt}.{h}.{c_}")
                        first = []
                    p.emit("tensor",
                           _fn("matmul", psE[:, qt * 128:L], ident[:], dmask[:],
                               start=False, stop=True),
                           inc=s_pe, tag=f"p4.M{qt}.{h}")
                    marks.append(p.mark(s_pe))
                    return marks

                def head_va(qt, h, marks):
                    nonlocal m_exp, m_norm
                    L = (qt + 1) * 128
                    p.emit("scalar", _fn("activation", pbf[:, 0:L], psE[:, 0:L],
                                         AF.Exp, bias=vec(V_B16 + h), scale=1.0,
                                         accum_out=vec(V_SSUM)),
                           waits=[(s_pe, marks[-1]), (s_v, m_norm)],
                           inc=s_a, tag=f"p4.ex{qt}.{h}")
                    m_exp = p.mark(s_a)
                    p.emit("vector", _fn("reciprocal", vec(V_RSUM), vec(V_SSUM)),
                           waits=[(s_a, m_exp)], tag=f"p4.rs{qt}.{h}")
                    gap()
                    p.emit("vector", _fn("tensor_scalar_mul", pnb[:, 0:L],
                                         pbf[:, 0:L], vec(V_RSUM)),
                           waits=[(s_pe, m_tp)], inc=s_v, tag=f"p4.nm{qt}.{h}")
                    m_norm = p.mark(s_v)

                def head_tail_pe(qt, h):
                    nonlocal m_tp, m_pv, m_ptev, m_yev
                    nk = qt + 1
                    kvh = h // 4
                    pr = h // 2
                    hh = h % 2
                    pt_r = ptsb[:].rearrange("q (kt e) -> q kt e", e=256)
                    for kt in range(nk):
                        p.emit("tensor",
                               _fn("transpose",
                                   psO_bf[:, kt * 128:(kt + 1) * 128],
                                   pnb[:, kt * 128:(kt + 1) * 128], ident[:]),
                               waits=([(s_v, m_norm), (s_v, m_ptev),
                                       (s_v, m_slot_ev)]
                                      if kt == 0 else []),
                               inc=(s_pe if kt == nk - 1 else None),
                               tag=f"p4.T{qt}.{h}.{kt}")
                    m_tp = p.mark(s_pe)
                    dst = pt_r[:, 0:nk, hh * 128:(hh + 1) * 128]
                    src_ = psO_bf[:, 0:nk * 128].rearrange(
                        "q (kt e) -> q kt e", e=128)
                    p.emit("vector", _fn("tensor_copy", dst, src_),
                           waits=[(s_pe, m_tp), (s_pe, m_pv)], inc=s_v,
                           tag=f"p4.PT{qt}.{h}")
                    m_ptev = p.mark(s_v)
                    if hh == 1:
                        ycol = (pr % 2) * 512
                        for kt in range(nk):
                            p.emit("tensor",
                                   _fn("matmul", psO[:, ycol:ycol + 256],
                                       V_pt(kt)[:, kvh * 128:(kvh + 1) * 128],
                                       ptsb[:, kt * 256:(kt + 1) * 256],
                                       start=(kt == 0), stop=(kt == nk - 1)),
                                   waits=([(s_v, m_ptev),
                                           (s_v, m_yev[pr % 2])]
                                          if kt == 0 else []),
                                   inc=(s_pe if kt == nk - 1 else None),
                                   tag=f"p4.PV{qt}.{pr}.{kt}")
                        m_pv = p.mark(s_pe)
                        p.emit("vector",
                               _fn("tensor_copy",
                                   ysb[:, pr * 256:(pr + 1) * 256],
                                   psO[:, ycol:ycol + 256]),
                               waits=[(s_pe, m_pv), (s_pe, m_proj)],
                               inc=s_v, tag=f"p4.ye{qt}.{pr}")
                        m_yev[pr % 2] = p.mark(s_v)

                pending_x2 = None

                def flush_x2():
                    nonlocal m_xnt, m_slot_ev, pending_x2
                    if pending_x2 is None:
                        return
                    fqt, fmxn = pending_x2
                    pending_x2 = None
                    for f in range(NFT):
                        p.emit("tensor",
                               _fn("transpose",
                                   psO_bf[:, f * 128:(f + 1) * 128],
                                   xn2tm[:, f * 128:(f + 1) * 128], ident[:]),
                               waits=([(s_v, fmxn), (s_v, m_ptev)]
                                      if f == 0 else []),
                               inc=(s_pe if f == NFT - 1 else None),
                               tag=f"p4.xT{fqt}.{f}")
                    m_xnt = p.mark(s_pe)
                    p.emit("vector", _fn("tensor_copy", slot(fqt)[:],
                                         psO_bf[:]),
                           waits=[(s_pe, m_xnt)], inc=s_v, tag=f"p4.xe{fqt}")
                    m_slot_ev = p.mark(s_v)

                for qt in range(NPT):
                    p.emit("gpsimd", _fn("dma_start", xt4[:],
                                         x_in[qt * 128:(qt + 1) * 128, :]),
                           waits=[(s_v, m_x2)], inc=s_w, inc_amt=16,
                           tag=f"p4.ldx{qt}")
                    mx4 = p.mark(s_w)
                    for h in range(NH):
                        marks = head_pe(qt, h)
                        if h == 2:
                            flush_x2()
                        if h > 0:
                            head_tail_pe(qt, h - 1)
                        head_va(qt, h, marks)
                    head_tail_pe(qt, NH - 1)
                    for f in range(NFT):
                        for oc in range(4):
                            p.emit("tensor",
                                   _fn("matmul", psE[:, oc * 512:(oc + 1) * 512],
                                       ysb[:, f * 128:(f + 1) * 128],
                                       wp_t(f, oc),
                                       start=(f == 0), stop=(f == NFT - 1)),
                                   waits=([(s_v, m_yev[0]), (s_v, m_yev[1]),
                                           (s_a, m_exp), (s_w, mwp)]
                                          if (oc == 0 and f == 0) else []),
                                   inc=(s_pe if (oc == 3 and f == NFT - 1)
                                        else None),
                                   tag=f"p4.pj{qt}.{oc}.{f}")
                    m_proj = p.mark(s_pe)
                    for i in range(4):
                        p.emit("vector", _fn("bn_stats", vec(V_BNST + 6 * i, 6),
                                             psE[:, 512 * i:512 * (i + 1)]),
                               waits=([(s_a, m_m2cp), (s_pe, m_proj)]
                                      if i == 0 else []),
                               tag=f"p4.zb{qt}.{i}")
                    gap()
                    p.emit("vector", _fn("bn_aggr", vec(V_STAT, 2),
                                         vec(V_BNST, 24)),
                           inc=s_v, tag=f"p4.za{qt}")
                    mzs = p.mark(s_v)
                    p.emit("scalar", _fn("activation", vec(V_LNT),
                                         vec(V_STAT + 1), AF.Ln,
                                         bias=epsb[:, 0:1], scale=1.0),
                           waits=[(s_v, mzs)], tag=f"p4.zl{qt}")
                    p.emit("scalar", _fn("activation", vec(V_RSTD), vec(V_LNT),
                                         AF.Exp, bias=0.0, scale=-0.5),
                           inc=s_a, tag=f"p4.zr{qt}")
                    mza = p.mark(s_a)
                    p.emit("vector", _fn("tensor_scalar", znb[:], psE[:],
                                         vec(V_STAT), vec(V_RSTD),
                                         ALU.subtract, ALU.mult),
                           waits=[(s_a, mza)], inc=s_v, tag=f"p4.zn{qt}")
                    m_zev = p.mark(s_v)
                    p.emit("vector", _fn("tensor_tensor", x2b[:], znb[:],
                                         xt4[:], ALU.add),
                           waits=[(s_w, mx4)], inc=s_v, tag=f"p4.x2{qt}")
                    m_x2 = p.mark(s_v)
                    for i in range(4):
                        p.emit("vector", _fn("bn_stats", vec(V_BNST + 6 * i, 6),
                                             x2b[:, 512 * i:512 * (i + 1)]),
                               tag=f"p4.b2{qt}.{i}")
                    gap()
                    p.emit("vector", _fn("bn_aggr", vec(V_STAT, 2),
                                         vec(V_BNST, 24)),
                           inc=s_v, tag=f"p4.a2{qt}")
                    m2s = p.mark(s_v)
                    p.emit("scalar", _fn("activation", vec(V_LNT),
                                         vec(V_STAT + 1), AF.Ln,
                                         bias=epsb[:, 0:1], scale=1.0),
                           waits=[(s_v, m2s)], tag=f"p4.l2{qt}")
                    p.emit("scalar", _fn("activation", vec(V_RSTD), vec(V_LNT),
                                         AF.Exp, bias=0.0, scale=-0.5),
                           tag=f"p4.r2{qt}")
                    p.emit("scalar", _fn("activation", vec(V_SD2 + qt),
                                         vec(V_LNT), AF.Exp, bias=0.0,
                                         scale=0.5),
                           tag=f"p4.s2{qt}")
                    p.emit("scalar", _fn("copy", vec(V_M2 + qt), vec(V_STAT)),
                           inc=s_a, tag=f"p4.m2{qt}")
                    m_m2cp = p.mark(s_a)
                    p.emit("vector", _fn("tensor_scalar", xn2tm[:], x2b[:],
                                         vec(V_STAT), vec(V_RSTD),
                                         ALU.subtract, ALU.mult),
                           waits=[(s_a, m_m2cp), (s_pe, m_xnt)], inc=s_v,
                           tag=f"p4.xn{qt}")
                    mxn = p.mark(s_v)
                    pending_x2 = (qt, mxn)
                flush_x2()

        if phases >= 5:
            for ps_i in range(4):
                sl = [4 * ps_i + j for j in range(4)]
                p.phase()
                with ExitStack() as ph:
                    h32 = [ph.enter_context(
                        nc.sbuf_tensor(f"h32{b_}_{ps_i}", [128, 2048], F32))
                        for b_ in range(2)]
                    sg = ph.enter_context(
                        nc.sbuf_tensor(f"sg_{ps_i}", [128, 2048], F32))
                    hsc4 = ph.enter_context(
                        nc.sbuf_tensor(f"hsc4_{ps_i}", [128, 4096], BF16))
                    wex = ph.enter_context(
                        nc.sbuf_tensor(f"wex_{ps_i}", [128, 4096], BF16))
                    aabn = ph.enter_context(
                        nc.sbuf_tensor(f"aabn_{ps_i}", [128, 1024], F32))
                    aab = aabn[:, 0:512]
                    nb = [aabn[:, 512 + i * 128:512 + (i + 1) * 128]
                          for i in range(4)]

                    def hscv(ic_, pt_):
                        o = (ic_ % 2) * 2048 + pt_ * 512
                        return hsc4[:, o:o + 512]

                    def half(s):
                        if s < 4:
                            return rkv[:, s * 4096:(s + 1) * 4096]
                        return wex[:]

                    def wtile(L, f):
                        h = half((2 * L + (1 if f >= 8 else 0)) % 5)
                        return h[:, (f % 8) * 512:((f % 8) + 1) * 512]

                    def dma_half(L, k, waits):
                        ic, j = divmod(L, 2)
                        dst = half((2 * L + k) % 5)[:].rearrange(
                            "q (f i) -> q f i", i=512)
                        srcw = wg_in if j == 0 else wu_in
                        s_ = srcw[k * 1024:(k + 1) * 1024,
                                  ic * 512:(ic + 1) * 512]
                        s_ = s_.rearrange("(f q) i -> q f i", q=128)
                        p.emit("gpsimd", _fn("dma_start", dst, s_),
                               waits=waits, inc=s_w, inc_amt=16,
                               tag=f"f1.w{ps_i}.{L}.{k}")
                        return p.mark(s_w)

                    mld = {}
                    dma_half(0, 0, [])
                    mld[0] = dma_half(0, 1, [])
                    dma_half(1, 0, [])
                    mld[1] = dma_half(1, 1, [])
                    dma_half(2, 0, [])
                    m_silu = 0
                    m_hf = 0
                    m_hf_prev = 0
                    m_sp_par = [0, 0]
                    m_dj = 0
                    m_hst = 0
                    m_hsev = 0
                    m_hsev2 = [0, 0]
                    pend_hst = []
                    pend_sp = None

                    def emit_hst(ic, all_=False):
                        nonlocal m_hst, m_hsev
                        todo = [e for e in pend_hst
                                if all_ or e[0] <= ic - 2]
                        pend_hst[:] = [e for e in pend_hst
                                       if not (all_ or e[0] <= ic - 2)]
                        for (ric, pt_, msp) in todo:
                            par = pt_ % 2
                            bk = psO_bf0[:, par * 1024:(par + 1) * 1024]
                            for k in range(4):
                                p.emit("tensor",
                                       _fn("transpose",
                                           bk[:, k * 128:(k + 1) * 128],
                                           hscv(ric, pt_)[:, k * 128:
                                                          (k + 1) * 128],
                                           ident[:]),
                                       waits=([(s_v, msp),
                                               (s_v, m_hsev2[par])]
                                              if k == 0 else []),
                                       inc=(s_pe if k == 3 else None),
                                       tag=f"f1.hT{ps_i}.{ric}.{pt_}.{k}")
                            m_hst = p.mark(s_pe)
                            rw_r = rw[:].rearrange("q (it t) -> q it t", t=512)
                            dstv = rw_r[:, ric * 4:(ric + 1) * 4,
                                        pt_ * 128:(pt_ + 1) * 128]
                            srcv = bk[:, 0:512].rearrange(
                                "q (k t) -> q k t", t=128)
                            p.emit("vector", _fn("tensor_copy", dstv, srcv),
                                   waits=[(s_pe, m_hst)], inc=s_v,
                                   tag=f"f1.he{ps_i}.{ric}.{pt_}")
                            m_hsev2[par] = p.mark(s_v)
                            m_hsev = m_hsev2[par]

                    def emit_sparsify():
                        nonlocal pend_sp, m_dj, m_sp_par
                        if pend_sp is None:
                            return
                        ric, mhf = pend_sp
                        pend_sp = None

                        def jv(t_, j):
                            v3 = t_.rearrange("q (g j) -> q g j", j=4)
                            return v3[:, :, j]

                        for pt_ in range(4):
                            hv = h32[ric % 2][:, pt_ * 512:(pt_ + 1) * 512]
                            p.emit("scalar", _fn("activation", aab, hv, AF.Abs),
                                   waits=[(s_v, mhf), (s_v, m_dj)],
                                   inc=s_a, tag=f"f1.ab{ps_i}.{ric}.{pt_}")
                            m_ab = p.mark(s_a)
                            gap()
                            p.emit("vector", _fn("tensor_tensor", nb[0][:],
                                                 jv(aab, 0), jv(aab, 1),
                                                 ALU.min),
                                   waits=[(s_a, m_ab)],
                                   tag=f"f1.n1{ps_i}.{ric}.{pt_}")
                            p.emit("vector", _fn("tensor_tensor", nb[1][:],
                                                 jv(aab, 0), jv(aab, 1),
                                                 ALU.max),
                                   tag=f"f1.n2{ps_i}.{ric}.{pt_}")
                            p.emit("vector", _fn("tensor_tensor", nb[2][:],
                                                 jv(aab, 2), jv(aab, 3),
                                                 ALU.min),
                                   tag=f"f1.n3{ps_i}.{ric}.{pt_}")
                            p.emit("vector", _fn("tensor_tensor", nb[3][:],
                                                 jv(aab, 2), jv(aab, 3),
                                                 ALU.max),
                                   tag=f"f1.n4{ps_i}.{ric}.{pt_}")
                            p.emit("vector", _fn("tensor_tensor", nb[0][:],
                                                 nb[0][:], nb[2][:], ALU.max),
                                   tag=f"f1.n5{ps_i}.{ric}.{pt_}")
                            p.emit("vector", _fn("tensor_tensor", nb[1][:],
                                                 nb[1][:], nb[3][:], ALU.min),
                                   tag=f"f1.n6{ps_i}.{ric}.{pt_}")
                            gap()
                            p.emit("vector", _fn("tensor_tensor", nb[0][:],
                                                 nb[0][:], nb[1][:], ALU.min),
                                   tag=f"f1.n7{ps_i}.{ric}.{pt_}")
                            gap()
                            for j in range(4):
                                p.emit("vector",
                                       _fn("tensor_tensor", jv(aab, j),
                                           jv(aab, j), nb[0][:], ALU.subtract),
                                       tag=f"f1.d{ps_i}.{ric}.{pt_}.{j}")
                            for j in range(4):
                                p.emit("vector",
                                       _fn("scalar_tensor_tensor",
                                           jv(hscv(ric, pt_)[:], j),
                                           jv(aab, j), 0.0,
                                           jv(hv, j), ALU.is_gt, ALU.mult),
                                       waits=([(s_pe, m_hst)] if j == 0
                                              else []),
                                       inc=(s_v if j == 3 else None),
                                       tag=f"f1.m{ps_i}.{ric}.{pt_}.{j}")
                            m_sp = p.mark(s_v)
                            m_dj = m_sp
                            pend_hst.append((ric, pt_, m_sp))
                        m_sp_par[ric % 2] = m_dj

                    for ic in range(16):
                        Lg, Lu = 2 * ic, 2 * ic + 1
                        for pl in range(4):
                            for f in range(NFT):
                                w0 = ([(s_w, mld.get(Lg, 0)), (s_a, m_silu),
                                       (s_v, m_hf)]
                                      if (pl == 0 and f == 0) else [])
                                p.emit("tensor",
                                       _fn("matmul",
                                           psE[:, pl * 512:(pl + 1) * 512],
                                           slot(sl[pl])[:, f * 128:(f + 1) * 128],
                                           wtile(Lg, f),
                                           start=(f == 0), stop=(f == NFT - 1)),
                                       waits=w0,
                                       inc=(s_pe if (pl == 3 and f == NFT - 1)
                                            else None),
                                       tag=f"f1.g{ps_i}.{ic}.{pl}.{f}")
                        m_gd = p.mark(s_pe)
                        if Lg + 2 < 32:
                            dma_half(Lg + 2, 1, [(s_pe, m_gd)])
                            mld[Lg + 2] = p.mark(s_w)
                        if Lg + 3 < 32:
                            dma_half(Lg + 3, 0, [])
                        emit_hst(ic)
                        for pl in range(4):
                            for f in range(NFT):
                                p.emit("tensor",
                                       _fn("matmul",
                                           psO[:, pl * 512:(pl + 1) * 512],
                                           slot(sl[pl])[:, f * 128:(f + 1) * 128],
                                           wtile(Lu, f),
                                           start=(f == 0), stop=(f == NFT - 1)),
                                       waits=([(s_w, mld.get(Lu, 0)),
                                               (s_v, m_hsev)]
                                              if (pl == 0 and f == 0) else []),
                                       inc=(s_pe if (pl == 3 and f == NFT - 1)
                                            else None),
                                       tag=f"f1.u{ps_i}.{ic}.{pl}.{f}")
                        m_umm = p.mark(s_pe)
                        if Lu + 2 < 32:
                            dma_half(Lu + 2, 1, [(s_pe, m_umm)])
                            mld[Lu + 2] = p.mark(s_w)
                        if Lu + 3 < 32:
                            dma_half(Lu + 3, 0, [])
                        p.emit("scalar", _fn("activation", sg[:],
                                             psE[:, 0:2048], AF.Silu),
                               waits=[(s_pe, m_gd), (s_v, m_hf)],
                               inc=s_a, tag=f"f1.si{ps_i}.{ic}")
                        m_silu = p.mark(s_a)
                        hd_ = h32[ic % 2][:].rearrange("q (pt e) -> q pt e",
                                                       e=512)
                        sg3 = sg[:].rearrange("q (pt e) -> q pt e", e=512)
                        pu3 = psO[:].rearrange("q (pt e) -> q pt e", e=512)
                        p.emit("vector", _fn("tensor_tensor", hd_, sg3, pu3,
                                             ALU.mult),
                               waits=[(s_a, m_silu), (s_pe, m_umm),
                                      (s_v, m_sp_par[ic % 2])],
                               inc=s_v, tag=f"f1.h{ps_i}.{ic}")
                        m_hf_prev = m_hf
                        m_hf = p.mark(s_v)
                        emit_sparsify()
                        pend_sp = (ic, m_hf)
                    emit_sparsify()
                    emit_hst(99, all_=True)

                p.phase()
                with ExitStack() as ph:
                    fbf = ph.enter_context(
                        nc.sbuf_tensor(f"fbf_{ps_i}", [128, 4 * C], BF16))
                    wd0 = ph.enter_context(
                        nc.sbuf_tensor(f"wd0_{ps_i}", [128, 4096], BF16))
                    wd1 = ph.enter_context(
                        nc.sbuf_tensor(f"wd1_{ps_i}", [128, 4096], BF16))
                    fo1 = ph.enter_context(
                        nc.sbuf_tensor(f"fo1_{ps_i}", [128, 512], F32))
                    fo2 = ph.enter_context(
                        nc.sbuf_tensor(f"fo2_{ps_i}", [128, 512], F32))
                    fo3 = ph.enter_context(
                        nc.sbuf_tensor(f"fo3_{ps_i}", [128, 512], F32))
                    wdsl = [wd0, wd1]

                    def hsf_t(it, pl):
                        return rw[:, it * 512 + pl * 128: it * 512 + (pl + 1) * 128]

                    def dma_wd(os_, ic4, waits):
                        a = ic4 % 2
                        dst = wdsl[a][:].rearrange("q (it o) -> q it o", o=512)
                        src = wd_in[ic4 * 1024:(ic4 + 1) * 1024,
                                    os_ * 512:(os_ + 1) * 512]
                        src = src.rearrange("(it q) o -> q it o", q=128)
                        p.emit("gpsimd", _fn("dma_start", dst, src),
                               waits=waits, inc=s_w, inc_amt=16,
                               tag=f"f2.wd{ps_i}.{os_}.{ic4}")
                        return p.mark(s_w)

                    m_fev2 = [0, 0]
                    m_wmm = {}
                    for os_ in range(4):
                        mw2 = {}
                        mw2[0] = dma_wd(os_, 0,
                                        [(s_pe, m_wmm.get((os_ - 1, 7), 0))])
                        mw2[1] = dma_wd(os_, 1, [])
                        acc = psE if os_ % 2 == 0 else psO
                        for ic4 in range(8):
                            a = ic4 % 2
                            for pl in range(4):
                                for it in range(8):
                                    itg = ic4 * 8 + it
                                    p.emit("tensor",
                                           _fn("matmul",
                                               acc[:, pl * 512:(pl + 1) * 512],
                                               hsf_t(itg, pl),
                                               wdsl[a][:, it * 512:(it + 1) * 512],
                                               start=(ic4 == 0 and it == 0),
                                               stop=(ic4 == 7 and it == 7)),
                                           waits=([(s_w, mw2[ic4]),
                                                   (s_v, m_fev2[os_ % 2])]
                                                  if (pl == 0 and it == 0)
                                                  else []),
                                           inc=(s_pe if (pl == 3 and it == 7)
                                                else None),
                                           tag=f"f2.m{ps_i}.{os_}.{ic4}.{pl}")
                            m_wmm[(os_, ic4)] = p.mark(s_pe)
                            if ic4 + 2 < 8:
                                mw2[ic4 + 2] = dma_wd(
                                    os_, ic4 + 2,
                                    [(s_pe, m_wmm[(os_, ic4)])])
                        mlast = m_wmm[(os_, 7)]
                        for pl in range(4):
                            p.emit("vector",
                                   _fn("bn_stats",
                                       vec(V_BNF + pl * 24 + os_ * 6, 6),
                                       acc[:, pl * 512:(pl + 1) * 512]),
                                   waits=[(s_pe, mlast)] if pl == 0 else [],
                                   tag=f"f2.bn{ps_i}.{os_}.{pl}")
                            p.emit("vector",
                                   _fn("tensor_copy",
                                       fbf[:, pl * C + os_ * 512:
                                           pl * C + (os_ + 1) * 512],
                                       acc[:, pl * 512:(pl + 1) * 512]),
                                   inc=(s_v if pl == 3 else None),
                                   tag=f"f2.fe{ps_i}.{os_}.{pl}")
                        m_fev2[os_ % 2] = p.mark(s_v)

                    if debug and ps_i == 3:
                        p.emit("sync", _fn("dma_start", dbg_fbf, fbf[:]),
                               waits=[(s_v, m_fev2[0]), (s_v, m_fev2[1])],
                               inc=s_out, inc_amt=16, tag="dbg.fbf")
                    m_store = p.mark(s_out)
                    for pl in range(4):
                        gap()
                        p.emit("vector", _fn("bn_aggr", vec(V_M3 + pl * 2, 2),
                                             vec(V_BNF + pl * 24, 24)),
                               waits=([(s_v, m_fev2[0]), (s_v, m_fev2[1])]
                                      if pl == 0 else []),
                               inc=s_v, tag=f"f2.ag{ps_i}.{pl}")
                    mag = p.mark(s_v)
                    for pl in range(4):
                        p.emit("scalar", _fn("activation", vec(V_LNT),
                                             vec(V_M3 + pl * 2 + 1), AF.Ln,
                                             bias=epsb[:, 0:1], scale=1.0),
                               waits=[(s_v, mag)] if pl == 0 else [],
                               tag=f"f2.l{ps_i}.{pl}")
                        p.emit("scalar", _fn("activation", vec(V_RSTD3 + pl),
                                             vec(V_LNT), AF.Exp, bias=0.0,
                                             scale=-0.5),
                               inc=s_a, tag=f"f2.r{ps_i}.{pl}")
                    mra = p.mark(s_a)
                    for pl in range(4):
                        p.emit("vector", _fn("tensor_tensor",
                                             vec(V_CMX + 4 + pl),
                                             vec(V_M3 + pl * 2),
                                             vec(V_RSTD3 + pl), ALU.mult),
                               waits=[(s_a, mra)] if pl == 0 else [],
                               tag=f"f2.c1{ps_i}.{pl}")
                    gap()
                    for pl in range(4):
                        p.emit("vector", _fn("tensor_tensor",
                                             vec(V_CMX + 4 + pl),
                                             vec(V_M2 + sl[pl]),
                                             vec(V_CMX + 4 + pl), ALU.subtract),
                               inc=(s_v if pl == 3 else None),
                               tag=f"f2.c2{ps_i}.{pl}")
                    mc2 = p.mark(s_v)
                    m_ps_free = [0, 0]
                    for pl in range(4):
                        par = pl % 2
                        bk = (psE if par == 0 else psO)[:, 0:1024].bitcast(BF16)
                        for f in range(NFT):
                            p.emit("tensor",
                                   _fn("transpose",
                                       bk[:, f * 128:(f + 1) * 128],
                                       slot(sl[pl])[:, f * 128:(f + 1) * 128],
                                       ident[:]),
                                   waits=([(s_v, m_ps_free[par]),
                                           (s_v, mc2)] if f == 0 else []),
                                   inc=(s_pe if f == NFT - 1 else None),
                                   tag=f"f2.xT{ps_i}.{pl}.{f}")
                        m_xnt5 = p.mark(s_pe)
                        for os_ in range(4):
                            cs = slice(os_ * 512, (os_ + 1) * 512)
                            p.emit("vector",
                                   _fn("tensor_scalar", fo1[:], bk[:, cs],
                                       vec(V_SD2 + sl[pl]),
                                       vec(V_CMX + 4 + pl),
                                       ALU.mult, ALU.add),
                                   waits=([(s_pe, m_xnt5), (s_out, m_store)]
                                          if os_ == 0 else
                                          [(s_out, m_store)]),
                                   tag=f"f2.o1{ps_i}.{pl}.{os_}")
                            p.emit("vector",
                                   _fn("scalar_tensor_tensor", fo3[:],
                                       fbf[:, pl * C + os_ * 512:
                                           pl * C + (os_ + 1) * 512],
                                       vec(V_RSTD3 + pl), fo1[:],
                                       ALU.mult, ALU.add),
                                   inc=s_v, tag=f"f2.o3{ps_i}.{pl}.{os_}")
                            mo = p.mark(s_v)
                            r0 = sl[pl] * 128
                            p.emit("sync", _fn("dma_start",
                                               out[r0:r0 + 128, cs], fo3[:]),
                                   waits=[(s_v, mo)], inc=s_out, inc_amt=16,
                                   tag=f"f2.st{ps_i}.{pl}.{os_}")
                            m_store = p.mark(s_out)
                        m_ps_free[par] = p.mark(s_v)

            p.phase()
            p.waitop("sync", [(s_out, p.mark(s_out))], tag="final")

        if debug:
            p.phase()
            nst = 0
            p.emit("sync", _fn("dma_start", dbg_rq, rq[:]), inc=s_out,
                   inc_amt=16, tag="dbg.rq")
            p.emit("sync", _fn("dma_start", dbg_rkv, rkv[:]), inc=s_out,
                   inc_amt=16, tag="dbg.rkv")
            p.emit("sync", _fn("dma_start", dbg_rw, rw[:]), inc=s_out,
                   inc_amt=16, tag="dbg.rw")
            p.emit("sync", _fn("dma_start", dbg_vecs, vecs[:]), inc=s_out,
                   inc_amt=16, tag="dbg.vecs")
            p.waitop("sync", [(s_out, p.mark(s_out))], tag="dbg.final")

        p.simulate()

        with nc.Block() as block:
            @block.sync
            def _(e):
                for f in p.ops["sync"]:
                    f(e)

            @block.gpsimd
            def _(e):
                for f in p.ops["gpsimd"]:
                    f(e)

            @block.tensor
            def _(e):
                for f in p.ops["tensor"]:
                    f(e)

            @block.vector
            def _(e):
                for f in p.ops["vector"]:
                    f(e)

            @block.scalar
            def _(e):
                for f in p.ops["scalar"]:
                    f(e)

    return nc


def make_consts():
    ident = np.eye(128, dtype=np.float32).astype(ml_dtypes.bfloat16)
    mask = np.where(np.arange(128)[None, :] <= np.arange(128)[:, None],
                    np.float32(0.0), np.float32(NEG)).astype(ml_dtypes.bfloat16)
    inv = (10000.0 ** (-(np.arange(0, HD, 2, dtype=np.float64) / HD)))
    th = np.arange(T, dtype=np.float64)[None, :] * inv[:, None]
    cos = np.cos(th).astype(np.float32)
    sin = np.sin(th).astype(np.float32)
    cosf = np.concatenate([cos, cos], axis=0).astype(ml_dtypes.bfloat16)
    sinf = np.concatenate([sin, sin], axis=0).astype(ml_dtypes.bfloat16)
    cosf2 = (np.concatenate([cos, cos], axis=0) / math.sqrt(HD)).astype(
        ml_dtypes.bfloat16)
    sinf2 = (np.concatenate([sin, sin], axis=0) / math.sqrt(HD)).astype(
        ml_dtypes.bfloat16)
    rot = np.zeros((128, 128), np.float32)
    for d in range(64):
        rot[d + 64, d] = -1.0
        rot[d, d + 64] = 1.0
    rot = rot.astype(ml_dtypes.bfloat16)
    onesbv = np.ones((128, 8), dtype=ml_dtypes.bfloat16)
    onesfv = np.ones((128, 128), dtype=np.float32)
    return ident, mask, cosf, sinf, rot, cosf2, sinf2, onesbv, onesfv


def kernel(L0_attn_x, L0_attn_Wq, L0_attn_Wk, L0_attn_Wv, L0_attn_Wproj,
           L0_ffn_w_gate, L0_ffn_w_up, L0_ffn_w_down,
           _phases=5, _debug=False):
    x = np.ascontiguousarray(np.asarray(L0_attn_x, dtype=np.float32))
    wq = np.ascontiguousarray(np.asarray(L0_attn_Wq, dtype=np.float32))
    wk = np.ascontiguousarray(np.asarray(L0_attn_Wk, dtype=np.float32))
    wv = np.ascontiguousarray(np.asarray(L0_attn_Wv, dtype=np.float32))
    wp = np.ascontiguousarray(np.asarray(L0_attn_Wproj, dtype=np.float32))
    wg = np.ascontiguousarray(np.asarray(L0_ffn_w_gate, dtype=np.float32))
    wu = np.ascontiguousarray(np.asarray(L0_ffn_w_up, dtype=np.float32))
    wd = np.ascontiguousarray(np.asarray(L0_ffn_w_down, dtype=np.float32))
    ident, mask, cosf, sinf, rot, cosf2, sinf2, onesbv, onesfv = make_consts()
    epsv = np.full((128, 8), EPS, dtype=np.float32)

    nc = build_nc(phases=_phases, debug=_debug)
    in_maps = []
    for c in range(8):
        in_maps.append({
            "x": np.ascontiguousarray(x[c]),
            "wq": wq, "wk": wk, "wv": wv, "wp": wp,
            "wg": wg, "wu": wu, "wd": wd,
            "identw": ident, "maskw": mask, "cosw": cosf, "sinw": sinf,
            "epsw": epsv, "rotw": rot, "cosw2": cosf2, "sinw2": sinf2,
            "onesbw": onesbv, "onesfw": onesfv,
        })
    res = run_bass_kernel_spmd(nc, in_maps, core_ids=list(range(8)))
    outs = np.stack([np.asarray(r["out"], dtype=np.float32)
                     for r in res.results], axis=0)
    if _debug:
        dbgs = [{k: np.asarray(v) for k, v in r.items() if k.startswith("dbg")}
                for r in res.results]
        return outs, dbgs
    return outs


# version 14
# speedup vs baseline: 1.0362x; 1.0145x over previous
import math
from collections import defaultdict
from contextlib import ExitStack

import numpy as np
import ml_dtypes

import concourse.bass as bass
import concourse.mybir as mybir
from concourse.bass_utils import run_bass_kernel_spmd

F32 = mybir.dt.float32
BF16 = mybir.dt.bfloat16
AF = mybir.ActivationFunctionType
ALU = mybir.AluOpType
AX = mybir.AxisListType

T = 2048
C = 2048
NH = 16
NKV = 4
HD = 128
II = 8192
NPT = T // 128
NFT = C // 128
EPS = 1e-5
SCALE = 1.0 / math.sqrt(HD)
NEG = -1.0e30

ENGS = ("sync", "gpsimd", "tensor", "vector", "scalar")


class Prog:

    def __init__(self, sems):
        self.ops = {e: [] for e in ENGS}
        self.cnt = defaultdict(int)
        self.meta = []
        self.sems = list(sems)
        self.bar = {}
        self.bar_done = set(ENGS)
        self.sem_by_id = {id(s): s for s in sems}

    def mark(self, sem):
        return self.cnt[id(sem)]

    def phase(self):
        self.bar = {id(s): self.cnt[id(s)] for s in self.sems}
        self.bar_done = set()

    def emit(self, eng, fn, waits=(), inc=None, inc_amt=1, tag=""):
        waits = [(s, v) for (s, v) in waits if v > 0]
        if eng not in self.bar_done:
            self.bar_done.add(eng)
            waits = [(self.sem_by_id[i], v) for i, v in self.bar.items()
                     if v > 0] + waits
        if inc is not None:
            self.cnt[id(inc)] += inc_amt
        self.meta.append((eng, [(id(s), v) for s, v in waits],
                          [(id(inc), inc_amt)] if inc is not None else [], tag))

        def run(e, fn=fn, waits=waits, inc=inc, amt=inc_amt):
            for s, v in waits:
                e.wait_ge(s, v)
            ins = fn(e)
            if inc is not None:
                ins.then_inc(inc, amt)
        self.ops[eng].append(run)

    def waitop(self, eng, waits, tag=""):
        self.emit(eng, lambda e: None, waits=waits, tag=tag)

    def simulate(self):
        queues = {e: [] for e in ENGS}
        for (eng, waits, incs, tag) in self.meta:
            queues[eng].append((waits, incs, tag))
        heads = {e: 0 for e in ENGS}
        cnt = defaultdict(int)
        progress = True
        while progress:
            progress = False
            for e in ENGS:
                while heads[e] < len(queues[e]):
                    waits, incs, tag = queues[e][heads[e]]
                    if all(cnt[s] >= v for s, v in waits):
                        for s, a in incs:
                            cnt[s] += a
                        heads[e] += 1
                        progress = True
                    else:
                        break
        if any(heads[e] < len(queues[e]) for e in ENGS):
            lines = []
            for e in ENGS:
                if heads[e] < len(queues[e]):
                    waits, incs, tag = queues[e][heads[e]]
                    missing = [(str(s), v, cnt[s]) for s, v in waits
                               if cnt[s] < v]
                    lines.append(f"{e}[{heads[e]}/{len(queues[e])}] tag={tag} "
                                 f"missing={missing}")
            raise RuntimeError("build-time DEADLOCK:\n" + "\n".join(lines))


def _fn(method, *args, **kwargs):
    return lambda e: getattr(e, method)(*args, **kwargs)


def build_nc(phases=5, debug=False):
    nc = bass.Bass("TRN2", target_bir_lowering=False)

    x_in = nc.dram_tensor("x", [T, C], F32, kind="ExternalInput").ap()
    wq_in = nc.dram_tensor("wq", [C, C], F32, kind="ExternalInput").ap()
    wk_in = nc.dram_tensor("wk", [C, NKV * HD], F32, kind="ExternalInput").ap()
    wv_in = nc.dram_tensor("wv", [C, NKV * HD], F32, kind="ExternalInput").ap()
    wp_in = nc.dram_tensor("wp", [C, C], F32, kind="ExternalInput").ap()
    wg_in = nc.dram_tensor("wg", [C, II], F32, kind="ExternalInput").ap()
    wu_in = nc.dram_tensor("wu", [C, II], F32, kind="ExternalInput").ap()
    wd_in = nc.dram_tensor("wd", [II, C], F32, kind="ExternalInput").ap()
    identw = nc.dram_tensor("identw", [128, 128], BF16, kind="ExternalInput").ap()
    maskw = nc.dram_tensor("maskw", [128, 128], BF16, kind="ExternalInput").ap()
    rotw = nc.dram_tensor("rotw", [128, 128], BF16, kind="ExternalInput").ap()
    onesbw = nc.dram_tensor("onesbw", [128, 8], BF16, kind="ExternalInput").ap()
    onesfw = nc.dram_tensor("onesfw", [128, 128], F32, kind="ExternalInput").ap()
    cosw = nc.dram_tensor("cosw", [128, T], BF16, kind="ExternalInput").ap()
    cosw2 = nc.dram_tensor("cosw2", [128, T], BF16, kind="ExternalInput").ap()
    sinw2 = nc.dram_tensor("sinw2", [128, T], BF16, kind="ExternalInput").ap()
    epsw = nc.dram_tensor("epsw", [128, 8], F32, kind="ExternalInput").ap()
    sinw = nc.dram_tensor("sinw", [128, T], BF16, kind="ExternalInput").ap()

    out = nc.dram_tensor("out", [T, C], F32, kind="ExternalOutput").ap()
    dbg = {}
    if debug:
        dbg_rq = nc.dram_tensor("dbg_rq", [128, NPT * C], BF16,
                                kind="ExternalOutput").ap()
        dbg_rkv = nc.dram_tensor("dbg_rkv", [128, 16384], BF16,
                                 kind="ExternalOutput").ap()
        dbg_vecs = nc.dram_tensor("dbg_vecs", [128, 256], F32,
                                  kind="ExternalOutput").ap()
        dbg_rw = nc.dram_tensor("dbg_rw", [128, 16 * C], BF16,
                                kind="ExternalOutput").ap()
        dbg_fbf = nc.dram_tensor("dbg_fbf", [128, 4 * C], BF16,
                                 kind="ExternalOutput").ap()

    with ExitStack() as top:
        E_ = top.enter_context
        rq = E_(nc.sbuf_tensor("rq", [128, NPT * C], BF16))
        rkv = E_(nc.sbuf_tensor("rkv", [128, 16384], BF16))
        rw = E_(nc.sbuf_tensor("rw", [128, 16 * C], BF16))
        vecs = E_(nc.sbuf_tensor("vecs", [128, 256], F32))
        ident = E_(nc.sbuf_tensor("ident", [128, 128], BF16))
        dmask = E_(nc.sbuf_tensor("dmask", [128, 128], BF16))
        rmat = E_(nc.sbuf_tensor("rmat", [128, 128], BF16))
        onesb = E_(nc.sbuf_tensor("onesb", [128, 8], BF16))
        onesf = E_(nc.sbuf_tensor("onesf", [128, 128], F32))
        epsb = E_(nc.sbuf_tensor("epsb", [128, 8], F32))
        psE = E_(nc.psum_tensor("psE", [128, 2048], F32))
        psO = E_(nc.psum_tensor("psO", [128, 2048], F32))

        s_x = E_(nc.semaphore("s_x"))
        s_w = E_(nc.semaphore("s_w"))
        s_pe = E_(nc.semaphore("s_pe"))
        s_v = E_(nc.semaphore("s_v"))
        s_a = E_(nc.semaphore("s_a"))
        s_out = E_(nc.semaphore("s_out"))
        p = Prog([s_x, s_w, s_pe, s_v, s_a, s_out])

        V_BNST, V_STAT = 0, 24
        V_NEGM, V_BIAS, V_SSUM, V_RSUM = 26, 27, 28, 29
        V_LNT, V_RSTD = 30, 32
        V_EPS = 34
        V_CMX = 36
        V_B16 = 80
        V_KN = 204
        V_QN = 212
        V_NTMP = 250
        V_M2, V_SD2 = 48, 64
        V_BNF = 96
        V_M3 = 192
        V_RSTD3 = 200

        def vec(col, n=1):
            return vecs[:, col:col + n]

        def gap(eng="vector"):
            p.emit(eng, _fn("tensor_copy", vecs[:, 44:48], epsb[:, 0:4]),
                   tag="gap")

        def slot(s):
            return rq[:, s * C:(s + 1) * C]

        def K_kvh(k):
            return rkv[:, k * T:(k + 1) * T]

        def V_pt(pt):
            return rkv[:, 8192 + pt * 512: 8192 + (pt + 1) * 512]

        psE_bf = psE[:, 0:1024].bitcast(BF16)
        psO_bf = psO[:, 1024:2048].bitcast(BF16)
        psO_bf0 = psO[:, 0:1024].bitcast(BF16)

        p.emit("sync", _fn("dma_start", ident[:], identw), inc=s_x, inc_amt=16,
               tag="c.id")
        p.emit("sync", _fn("dma_start", dmask[:], maskw), inc=s_x, inc_amt=16,
               tag="c.mask")
        p.emit("sync", _fn("dma_start", rmat[:], rotw), inc=s_x, inc_amt=16,
               tag="c.rot")
        p.emit("sync", _fn("dma_start", onesb[:], onesbw), inc=s_x,
               inc_amt=16, tag="c.ob")
        p.emit("sync", _fn("dma_start", onesf[:], onesfw), inc=s_x,
               inc_amt=16, tag="c.of")
        p.emit("sync", _fn("dma_start", epsb[:], epsw), inc=s_x,
               inc_amt=16, tag="c.eps")

        p.phase()
        with ExitStack() as ph:
            xt = [ph.enter_context(nc.sbuf_tensor(f"p1x{i}", [128, C], F32))
                  for i in range(2)]
            consumed = [0, 0]
            for pt in range(NPT):
                b = pt % 2
                p.emit("sync", _fn("dma_start", xt[b][:],
                                   x_in[pt * 128:(pt + 1) * 128, :]),
                       waits=[(s_v, consumed[b])], inc=s_x, inc_amt=16,
                       tag=f"p1.ld{pt}")
                mx_ld = p.mark(s_x)
                for i in range(4):
                    p.emit("vector", _fn("bn_stats", vec(V_BNST + 6 * i, 6),
                                         xt[b][:, 512 * i:512 * (i + 1)]),
                           waits=[(s_x, mx_ld)] if i == 0 else [],
                           tag=f"p1.bn{pt}.{i}")
                gap()
                p.emit("vector", _fn("bn_aggr", vec(V_STAT, 2), vec(V_BNST, 24)),
                       inc=s_v, tag=f"p1.ag{pt}")
                mv = p.mark(s_v)
                p.emit("scalar", _fn("activation", vec(V_LNT), vec(V_STAT + 1),
                                     AF.Ln, bias=epsb[:, 0:1], scale=1.0),
                       waits=[(s_v, mv)], tag=f"p1.ln{pt}")
                p.emit("scalar", _fn("activation", vec(V_RSTD), vec(V_LNT),
                                     AF.Exp, bias=0.0, scale=-0.5),
                       inc=s_a, tag=f"p1.ex{pt}")
                ma = p.mark(s_a)
                p.emit("vector", _fn("tensor_scalar", slot(pt)[:], xt[b][:],
                                     vec(V_STAT), vec(V_RSTD),
                                     ALU.subtract, ALU.mult),
                       waits=[(s_a, ma)], inc=s_v, tag=f"p1.nm{pt}")
                consumed[b] = p.mark(s_v)

        def qkv_pass(kind, ph):
            xnf4 = ph.enter_context(
                nc.sbuf_tensor(f"xnf4{kind}", [128, NFT * 512], BF16))
            cosb = ph.enter_context(nc.sbuf_tensor(f"cosb{kind}", [128, 512], BF16))
            sinb = ph.enter_context(nc.sbuf_tensor(f"sinb{kind}", [128, 512], BF16))
            rt1 = ph.enter_context(nc.sbuf_tensor(f"rt1{kind}", [128, 512], F32))
            rt2 = ph.enter_context(nc.sbuf_tensor(f"rt2{kind}", [128, 512], F32))
            kraw = ph.enter_context(nc.sbuf_tensor(f"kraw{kind}", [128, 512], BF16))
            sqb = ph.enter_context(nc.sbuf_tensor(f"sqb{kind}", [128, 512], BF16))

            if kind == "kv":
                for f in range(NFT):
                    p.emit("gpsimd", _fn("dma_start",
                                         rw[:, f * 512:(f + 1) * 512],
                                         wk_in[f * 128:(f + 1) * 128, :]),
                           inc=s_w, inc_amt=16, tag=f"p2.wk{f}")
                for f in range(NFT):
                    p.emit("gpsimd", _fn("dma_start",
                                         rw[:, 8192 + f * 512:8192 + (f + 1) * 512],
                                         wv_in[f * 128:(f + 1) * 128, :]),
                           inc=s_w, inc_amt=16, tag=f"p2.wv{f}")
            else:
                for f in range(NFT):
                    p.emit("gpsimd", _fn("dma_start",
                                         rw[:, f * C:(f + 1) * C],
                                         wq_in[f * 128:(f + 1) * 128, :]),
                           inc=s_w, inc_amt=16, tag=f"p3.wq{f}")
            mw = p.mark(s_w)

            m_xnf_ev = 0
            m_rope = 0
            m_cons_E = 0
            m_mm_done = 0
            m_csfree = 0
            for g in range(4):
                csrc = cosw if kind == "kv" else cosw2
                ssrc = sinw if kind == "kv" else sinw2
                p.emit("sync", _fn("dma_start", cosb[:],
                                   csrc[:, g * 512:(g + 1) * 512]),
                       waits=[(s_v, m_csfree)], inc=s_x, inc_amt=16,
                       tag=f"{kind}.cos{g}")
                p.emit("sync", _fn("dma_start", sinb[:],
                                   ssrc[:, g * 512:(g + 1) * 512]),
                       inc=s_x, inc_amt=16, tag=f"{kind}.sin{g}")
                m_cs = p.mark(s_x)

                ev = []
                prev_ev = m_xnf_ev
                for pl in range(4):
                    pt = g * 4 + pl
                    for f in range(NFT):
                        p.emit("tensor",
                               _fn("transpose",
                                   psE_bf[:, f * 128:(f + 1) * 128],
                                   slot(pt)[:, f * 128:(f + 1) * 128],
                                   ident[:]),
                               waits=([(s_v, prev_ev), (s_v, m_cons_E)]
                                      if f == 0 else []),
                               inc=(s_pe if f == NFT - 1 else None),
                               tag=f"{kind}.T{pt}.{f}")
                    mtp = p.mark(s_pe)
                    dst = xnf4[:].rearrange("q (f t) -> q f t", t=512)
                    dst = dst[:, :, pl * 128:(pl + 1) * 128]
                    src = psE_bf[:].rearrange("q (f t) -> q f t", t=128)
                    p.emit("vector", _fn("tensor_copy", dst, src),
                           waits=[(s_pe, mtp), (s_pe, m_mm_done)], inc=s_v,
                           tag=f"{kind}.ev{pt}")
                    ev.append(p.mark(s_v))
                    prev_ev = ev[-1]
                m_xnf_ev = ev[-1]

                def xnf_t(f, width=512, off=0):
                    return xnf4[:, f * 512 + off: f * 512 + off + width]

                if kind == "kv":
                    def wk_t(f, kvh):
                        return rw[:, f * 512 + kvh * 128: f * 512 + (kvh + 1) * 128]

                    def wv_t(f):
                        return rw[:, 8192 + f * 512: 8192 + (f + 1) * 512]

                    for kvh in range(NKV):
                        for f in range(NFT):
                            p.emit("tensor",
                                   _fn("matmul",
                                       psO[:, kvh * 512:(kvh + 1) * 512],
                                       wk_t(f, kvh), xnf_t(f),
                                       start=(f == 0), stop=(f == NFT - 1)),
                                   waits=([(s_v, m_xnf_ev), (s_w, mw),
                                           (s_v, m_rope)]
                                          if (kvh == 0 and f == 0) else []),
                                   inc=(s_pe if f == NFT - 1 else None),
                                   tag=f"p2.K{g}.{kvh}.{f}")
                    mk = p.mark(s_pe)
                    for pl in range(4):
                        for f in range(NFT):
                            p.emit("tensor",
                                   _fn("matmul",
                                       psE[:, pl * 512:(pl + 1) * 512],
                                       xnf_t(f, 128, pl * 128), wv_t(f),
                                       start=(f == 0), stop=(f == NFT - 1)),
                                   waits=([(s_v, m_cons_E)]
                                          if (pl == 0 and f == 0) else []),
                                   inc=(s_pe if f == NFT - 1 else None),
                                   tag=f"p2.V{g}.{pl}.{f}")
                    m_mm_done = p.mark(s_pe)
                    mvv = m_mm_done

                    for kvh in range(NKV):
                        ps = psO[:, kvh * 512:(kvh + 1) * 512]
                        kd = K_kvh(kvh)[:, g * 512:(g + 1) * 512]
                        p.emit("vector", _fn("tensor_copy", kraw[:], ps),
                               waits=[(s_pe, mk), (s_x, m_cs),
                                      (s_pe, p.mark(s_pe))],
                               inc=s_v, tag=f"p2.kr{g}.{kvh}")
                        mkraw = p.mark(s_v)
                        p.emit("tensor", _fn("matmul", ps, rmat[:], kraw[:],
                                             start=True, stop=True),
                               waits=[(s_v, mkraw)], inc=s_pe,
                               tag=f"p2.rot{g}.{kvh}")
                        mrot = p.mark(s_pe)
                        p.emit("vector", _fn("tensor_tensor", rt1[:], kraw[:],
                                             cosb[:], ALU.mult),
                               tag=f"p2.r{g}.{kvh}.a")
                        p.emit("vector", _fn("tensor_tensor", rt2[:], ps,
                                             sinb[:], ALU.mult),
                               waits=[(s_pe, mrot)],
                               tag=f"p2.r{g}.{kvh}.b")
                        p.emit("vector", _fn("tensor_tensor", kd, rt1[:],
                                             rt2[:], ALU.add),
                               inc=s_v, tag=f"p2.r{g}.{kvh}.c")
                        p.emit("vector", _fn("tensor_tensor", sqb[:], kraw[:],
                                             kraw[:], ALU.mult),
                               inc=s_v, tag=f"p2.sq{g}.{kvh}")
                        msq = p.mark(s_v)
                        p.emit("tensor", _fn("matmul", ps[0:1, :],
                                             onesb[:, 0:1], sqb[:],
                                             start=True, stop=True),
                               waits=[(s_v, msq)], inc=s_pe,
                               tag=f"p2.nm{g}.{kvh}")
                        mnm = p.mark(s_pe)
                        p.emit("vector", _fn("reduce_max",
                                             vecs[0:1, V_NTMP:V_NTMP + 1],
                                             ps[0:1, :], AX.X),
                               waits=[(s_pe, mnm)], tag=f"p2.nr{g}.{kvh}")
                        gap()
                        ca = V_KN + kvh * 2 + (g % 2)
                        cb = V_KN + kvh * 2 + ((g + 1) % 2)
                        if g == 0:
                            p.emit("vector", _fn("tensor_copy",
                                                 vecs[0:1, cb:cb + 1],
                                                 vecs[0:1, V_NTMP:V_NTMP + 1]),
                                   inc=s_v, tag=f"p2.nc{g}.{kvh}")
                        else:
                            p.emit("vector", _fn("tensor_tensor",
                                                 vecs[0:1, cb:cb + 1],
                                                 vecs[0:1, ca:ca + 1],
                                                 vecs[0:1, V_NTMP:V_NTMP + 1],
                                                 ALU.max),
                                   inc=s_v, tag=f"p2.nc{g}.{kvh}")
                    m_rope = p.mark(s_v)
                    m_csfree = m_rope
                    for pl in range(4):
                        p.emit("vector", _fn("tensor_copy", V_pt(g * 4 + pl)[:],
                                             psE[:, pl * 512:(pl + 1) * 512]),
                               waits=[(s_pe, mvv)] if pl == 0 else [],
                               inc=(s_v if pl == 3 else None),
                               tag=f"p2.vev{g}.{pl}")
                    m_cons_E = p.mark(s_v)
                else:
                    def wq_t(f, h):
                        return rw[:, f * C + h * 128: f * C + (h + 1) * 128]

                    rq_r = rq[:].rearrange("q (s e) -> q s e", e=C)
                    m_ropeq = [0, 0, 0, 0]
                    for h in range(NH):
                        qtr = h % 4
                        ps = psO[:, qtr * 512:(qtr + 1) * 512]
                        for f in range(NFT):
                            p.emit("tensor",
                                   _fn("matmul", ps, wq_t(f, h), xnf_t(f),
                                       start=(f == 0), stop=(f == NFT - 1)),
                                   waits=([(s_v, m_xnf_ev), (s_w, mw)]
                                          if (h == 0 and f == 0) else []) +
                                         ([(s_v, m_ropeq[qtr])] if f == 0 else []),
                                   inc=(s_pe if f == NFT - 1 else None),
                                   tag=f"p3.Q{g}.{h}.{f}")
                        mq = p.mark(s_pe)
                        if h == NH - 1:
                            m_mm_done = mq
                        dst = rq_r[:, 4 * g:4 * g + 4, h * 128:(h + 1) * 128]
                        p.emit("vector", _fn("tensor_copy", kraw[:], ps),
                               waits=[(s_pe, mq)] + ([(s_x, m_cs)]
                                                     if h == 0 else []),
                               inc=s_v, tag=f"p3.kr{g}.{h}")
                        mkraw = p.mark(s_v)
                        p.emit("tensor", _fn("matmul", ps, rmat[:], kraw[:],
                                             start=True, stop=True),
                               waits=[(s_v, mkraw)], inc=s_pe,
                               tag=f"p3.rot{g}.{h}")
                        mrot = p.mark(s_pe)
                        p.emit("vector", _fn("tensor_tensor", rt1[:], kraw[:],
                                             cosb[:], ALU.mult),
                               tag=f"p3.r{g}.{h}.a")
                        p.emit("vector", _fn("tensor_tensor", rt2[:], ps,
                                             sinb[:], ALU.mult),
                               waits=[(s_pe, mrot)],
                               tag=f"p3.r{g}.{h}.b")
                        p.emit("vector",
                               _fn("tensor_tensor", dst,
                                   rt1[:].rearrange("q (s e) -> q s e", e=128),
                                   rt2[:].rearrange("q (s e) -> q s e", e=128),
                                   ALU.add),
                               inc=s_v, tag=f"p3.r{g}.{h}.c")
                        p.emit("vector", _fn("tensor_tensor", sqb[:], kraw[:],
                                             kraw[:], ALU.mult),
                               inc=s_v, tag=f"p3.sq{g}.{h}")
                        msq = p.mark(s_v)
                        p.emit("tensor", _fn("matmul", ps[0:1, :],
                                             onesb[:, 0:1], sqb[:],
                                             start=True, stop=True),
                               waits=[(s_v, msq)], inc=s_pe,
                               tag=f"p3.nm{g}.{h}")
                        mnm = p.mark(s_pe)
                        p.emit("vector", _fn("reduce_max",
                                             vecs[0:1, V_NTMP:V_NTMP + 1],
                                             ps[0:1, :], AX.X),
                               waits=[(s_pe, mnm)], tag=f"p3.nr{g}.{h}")
                        gap()
                        ca = V_QN + h * 2 + (g % 2)
                        cb = V_QN + h * 2 + ((g + 1) % 2)
                        if g == 0:
                            p.emit("vector", _fn("tensor_copy",
                                                 vecs[0:1, cb:cb + 1],
                                                 vecs[0:1, V_NTMP:V_NTMP + 1]),
                                   inc=s_v, tag=f"p3.nc{g}.{h}")
                        else:
                            p.emit("vector", _fn("tensor_tensor",
                                                 vecs[0:1, cb:cb + 1],
                                                 vecs[0:1, ca:ca + 1],
                                                 vecs[0:1, V_NTMP:V_NTMP + 1],
                                                 ALU.max),
                                   inc=s_v, tag=f"p3.nc{g}.{h}")
                        m_ropeq[qtr] = p.mark(s_v)
                    m_rope = p.mark(s_v)
                    m_csfree = m_rope

        if phases >= 2:
            p.phase()
            with ExitStack() as ph:
                qkv_pass("kv", ph)
        if phases >= 3:
            p.phase()
            with ExitStack() as ph:
                qkv_pass("q", ph)

        if phases >= 4:
            p.phase()
            with ExitStack() as ph:
                xt4 = ph.enter_context(nc.sbuf_tensor("p4x", [128, C], BF16))
                znb = ph.enter_context(nc.sbuf_tensor("p4zn", [128, C], F32))
                x2b = ph.enter_context(nc.sbuf_tensor("p4x2b", [128, C], F32))
                pbf = ph.enter_context(nc.sbuf_tensor("p4pb", [128, T], BF16))
                pnb = ph.enter_context(nc.sbuf_tensor("p4pn", [128, T], BF16))
                ptsb = ph.enter_context(
                    nc.sbuf_tensor("p4pt", [128, 2 * T], BF16))
                ysb = ph.enter_context(nc.sbuf_tensor("p4y", [128, C], BF16))
                xn2tm = ph.enter_context(nc.sbuf_tensor("p4x2", [128, C], BF16))

                for f in range(NFT):
                    p.emit("gpsimd", _fn("dma_start", rw[:, f * C:(f + 1) * C],
                                         wp_in[f * 128:(f + 1) * 128, :]),
                           inc=s_w, inc_amt=16, tag=f"p4.wp{f}")
                mwp = p.mark(s_w)

                def wp_t(f, oc):
                    return rw[:, f * C + oc * 512: f * C + (oc + 1) * 512]

                for h in range(NH):
                    p.emit("vector", _fn("tensor_tensor",
                                         vecs[0:1, V_CMX:V_CMX + 1],
                                         vecs[0:1, V_QN + h * 2:
                                              V_QN + h * 2 + 1],
                                         vecs[0:1, V_KN + (h // 4) * 2:
                                              V_KN + (h // 4) * 2 + 1],
                                         ALU.mult),
                           inc=s_v, tag=f"p4.bb{h}")
                    mbb = p.mark(s_v)
                    p.emit("scalar", _fn("activation",
                                         vecs[0:1, V_CMX + 1:V_CMX + 2],
                                         vecs[0:1, V_CMX:V_CMX + 1], AF.Ln,
                                         bias=0.0, scale=SCALE * SCALE),
                           waits=[(s_v, mbb)], tag=f"p4.bl{h}")
                    p.emit("scalar", _fn("activation",
                                         vecs[0:1, V_NTMP:V_NTMP + 1],
                                         vecs[0:1, V_CMX + 1:V_CMX + 2],
                                         AF.Exp, bias=0.0, scale=0.5),
                           inc=s_a, tag=f"p4.be{h}")
                    mbe = p.mark(s_a)
                    p.emit("tensor", _fn("matmul", psE[:, 0:1],
                                         onesf[0:1, 0:128],
                                         vecs[0:1, V_NTMP:V_NTMP + 1],
                                         start=True, stop=True),
                           waits=[(s_a, mbe), (s_v, p.mark(s_v))],
                           inc=s_pe, tag=f"p4.bm{h}")
                    mbm = p.mark(s_pe)
                    p.emit("vector", _fn("tensor_scalar_mul",
                                         vec(V_B16 + h), psE[:, 0:1], -1.0),
                           waits=[(s_pe, mbm)], inc=s_v, tag=f"p4.bn{h}")
                m_bias16 = p.mark(s_v)
                p.waitop("tensor", [(s_v, m_bias16)], tag="p4.biasbar")

                m_exp = 0
                m_norm = 0
                m_tp = 0
                m_ptev = 0
                m_pv = 0
                m_yev = [0, 0]
                m_proj = 0
                m_zev = 0
                m_x2 = 0
                m_xnt = 0
                m_m2cp = 0
                m_slot_ev = 0

                def head_pe(qt, h):
                    nk = qt + 1
                    L = nk * 128
                    nch = (L + 511) // 512
                    kvh = h // 4
                    first = [(s_a, m_exp), (s_v, m_zev)]
                    marks = []
                    for c_ in range(nch):
                        wdt = min(512, L - c_ * 512)
                        last = (c_ == nch - 1)
                        p.emit("tensor",
                               _fn("matmul", psE[:, c_ * 512:c_ * 512 + wdt],
                                   slot(qt)[:, h * 128:(h + 1) * 128],
                                   K_kvh(kvh)[:, c_ * 512:c_ * 512 + wdt],
                                   start=True, stop=not last),
                               waits=first if c_ == 0 else [],
                               tag=f"p4.S{qt}.{h}.{c_}")
                        first = []
                    p.emit("tensor",
                           _fn("matmul", psE[:, qt * 128:L], ident[:], dmask[:],
                               start=False, stop=True),
                           inc=s_pe, tag=f"p4.M{qt}.{h}")
                    marks.append(p.mark(s_pe))
                    return marks

                def head_va(qt, h, marks):
                    nonlocal m_exp, m_norm
                    L = (qt + 1) * 128
                    p.emit("scalar", _fn("activation", pbf[:, 0:L], psE[:, 0:L],
                                         AF.Exp, bias=vec(V_B16 + h), scale=1.0,
                                         accum_out=vec(V_SSUM)),
                           waits=[(s_pe, marks[-1]), (s_v, m_norm)],
                           inc=s_a, tag=f"p4.ex{qt}.{h}")
                    m_exp = p.mark(s_a)
                    p.emit("vector", _fn("reciprocal", vec(V_RSUM), vec(V_SSUM)),
                           waits=[(s_a, m_exp)], tag=f"p4.rs{qt}.{h}")
                    gap()
                    p.emit("vector", _fn("tensor_scalar_mul", pnb[:, 0:L],
                                         pbf[:, 0:L], vec(V_RSUM)),
                           waits=[(s_pe, m_tp)], inc=s_v, tag=f"p4.nm{qt}.{h}")
                    m_norm = p.mark(s_v)

                def head_tail_pe(qt, h):
                    nonlocal m_tp, m_pv, m_ptev, m_yev
                    nk = qt + 1
                    kvh = h // 4
                    pr = h // 2
                    hh = h % 2
                    pt_r = ptsb[:].rearrange("q (kt e) -> q kt e", e=256)
                    for kt in range(nk):
                        p.emit("tensor",
                               _fn("transpose",
                                   psO_bf[:, kt * 128:(kt + 1) * 128],
                                   pnb[:, kt * 128:(kt + 1) * 128], ident[:]),
                               waits=([(s_v, m_norm), (s_v, m_ptev),
                                       (s_v, m_slot_ev)]
                                      if kt == 0 else []),
                               inc=(s_pe if kt == nk - 1 else None),
                               tag=f"p4.T{qt}.{h}.{kt}")
                    m_tp = p.mark(s_pe)
                    dst = pt_r[:, 0:nk, hh * 128:(hh + 1) * 128]
                    src_ = psO_bf[:, 0:nk * 128].rearrange(
                        "q (kt e) -> q kt e", e=128)
                    p.emit("vector", _fn("tensor_copy", dst, src_),
                           waits=[(s_pe, m_tp), (s_pe, m_pv)], inc=s_v,
                           tag=f"p4.PT{qt}.{h}")
                    m_ptev = p.mark(s_v)
                    if hh == 1:
                        ycol = (pr % 2) * 512
                        for kt in range(nk):
                            p.emit("tensor",
                                   _fn("matmul", psO[:, ycol:ycol + 256],
                                       V_pt(kt)[:, kvh * 128:(kvh + 1) * 128],
                                       ptsb[:, kt * 256:(kt + 1) * 256],
                                       start=(kt == 0), stop=(kt == nk - 1)),
                                   waits=([(s_v, m_ptev),
                                           (s_v, m_yev[pr % 2])]
                                          if kt == 0 else []),
                                   inc=(s_pe if kt == nk - 1 else None),
                                   tag=f"p4.PV{qt}.{pr}.{kt}")
                        m_pv = p.mark(s_pe)
                        p.emit("vector",
                               _fn("tensor_copy",
                                   ysb[:, pr * 256:(pr + 1) * 256],
                                   psO[:, ycol:ycol + 256]),
                               waits=[(s_pe, m_pv), (s_pe, m_proj)],
                               inc=s_v, tag=f"p4.ye{qt}.{pr}")
                        m_yev[pr % 2] = p.mark(s_v)

                pending_x2 = None

                def flush_x2():
                    nonlocal m_xnt, m_slot_ev, pending_x2
                    if pending_x2 is None:
                        return
                    fqt, fmxn = pending_x2
                    pending_x2 = None
                    for f in range(NFT):
                        p.emit("tensor",
                               _fn("transpose",
                                   psO_bf[:, f * 128:(f + 1) * 128],
                                   xn2tm[:, f * 128:(f + 1) * 128], ident[:]),
                               waits=([(s_v, fmxn), (s_v, m_ptev)]
                                      if f == 0 else []),
                               inc=(s_pe if f == NFT - 1 else None),
                               tag=f"p4.xT{fqt}.{f}")
                    m_xnt = p.mark(s_pe)
                    p.emit("vector", _fn("tensor_copy", slot(fqt)[:],
                                         psO_bf[:]),
                           waits=[(s_pe, m_xnt)], inc=s_v, tag=f"p4.xe{fqt}")
                    m_slot_ev = p.mark(s_v)

                for qt in range(NPT):
                    p.emit("gpsimd", _fn("dma_start", xt4[:],
                                         x_in[qt * 128:(qt + 1) * 128, :]),
                           waits=[(s_v, m_x2)], inc=s_w, inc_amt=16,
                           tag=f"p4.ldx{qt}")
                    mx4 = p.mark(s_w)
                    for h in range(NH):
                        marks = head_pe(qt, h)
                        if h == 2:
                            flush_x2()
                        if h > 0:
                            head_tail_pe(qt, h - 1)
                        head_va(qt, h, marks)
                    head_tail_pe(qt, NH - 1)
                    for f in range(NFT):
                        for oc in range(4):
                            p.emit("tensor",
                                   _fn("matmul", psE[:, oc * 512:(oc + 1) * 512],
                                       ysb[:, f * 128:(f + 1) * 128],
                                       wp_t(f, oc),
                                       start=(f == 0), stop=(f == NFT - 1)),
                                   waits=([(s_v, m_yev[0]), (s_v, m_yev[1]),
                                           (s_a, m_exp), (s_w, mwp)]
                                          if (oc == 0 and f == 0) else []),
                                   inc=(s_pe if (oc == 3 and f == NFT - 1)
                                        else None),
                                   tag=f"p4.pj{qt}.{oc}.{f}")
                    m_proj = p.mark(s_pe)
                    for i in range(4):
                        p.emit("vector", _fn("bn_stats", vec(V_BNST + 6 * i, 6),
                                             psE[:, 512 * i:512 * (i + 1)]),
                               waits=([(s_a, m_m2cp), (s_pe, m_proj)]
                                      if i == 0 else []),
                               tag=f"p4.zb{qt}.{i}")
                    gap()
                    p.emit("vector", _fn("bn_aggr", vec(V_STAT, 2),
                                         vec(V_BNST, 24)),
                           inc=s_v, tag=f"p4.za{qt}")
                    mzs = p.mark(s_v)
                    p.emit("scalar", _fn("activation", vec(V_LNT),
                                         vec(V_STAT + 1), AF.Ln,
                                         bias=epsb[:, 0:1], scale=1.0),
                           waits=[(s_v, mzs)], tag=f"p4.zl{qt}")
                    p.emit("scalar", _fn("activation", vec(V_RSTD), vec(V_LNT),
                                         AF.Exp, bias=0.0, scale=-0.5),
                           inc=s_a, tag=f"p4.zr{qt}")
                    mza = p.mark(s_a)
                    p.emit("vector", _fn("tensor_scalar", znb[:], psE[:],
                                         vec(V_STAT), vec(V_RSTD),
                                         ALU.subtract, ALU.mult),
                           waits=[(s_a, mza)], inc=s_v, tag=f"p4.zn{qt}")
                    m_zev = p.mark(s_v)
                    p.emit("vector", _fn("tensor_tensor", x2b[:], znb[:],
                                         xt4[:], ALU.add),
                           waits=[(s_w, mx4)], inc=s_v, tag=f"p4.x2{qt}")
                    m_x2 = p.mark(s_v)
                    for i in range(4):
                        p.emit("vector", _fn("bn_stats", vec(V_BNST + 6 * i, 6),
                                             x2b[:, 512 * i:512 * (i + 1)]),
                               tag=f"p4.b2{qt}.{i}")
                    gap()
                    p.emit("vector", _fn("bn_aggr", vec(V_STAT, 2),
                                         vec(V_BNST, 24)),
                           inc=s_v, tag=f"p4.a2{qt}")
                    m2s = p.mark(s_v)
                    p.emit("scalar", _fn("activation", vec(V_LNT),
                                         vec(V_STAT + 1), AF.Ln,
                                         bias=epsb[:, 0:1], scale=1.0),
                           waits=[(s_v, m2s)], tag=f"p4.l2{qt}")
                    p.emit("scalar", _fn("activation", vec(V_RSTD), vec(V_LNT),
                                         AF.Exp, bias=0.0, scale=-0.5),
                           tag=f"p4.r2{qt}")
                    p.emit("scalar", _fn("activation", vec(V_SD2 + qt),
                                         vec(V_LNT), AF.Exp, bias=0.0,
                                         scale=0.5),
                           tag=f"p4.s2{qt}")
                    p.emit("scalar", _fn("copy", vec(V_M2 + qt), vec(V_STAT)),
                           inc=s_a, tag=f"p4.m2{qt}")
                    m_m2cp = p.mark(s_a)
                    p.emit("vector", _fn("tensor_scalar", xn2tm[:], x2b[:],
                                         vec(V_STAT), vec(V_RSTD),
                                         ALU.subtract, ALU.mult),
                           waits=[(s_a, m_m2cp), (s_pe, m_xnt)], inc=s_v,
                           tag=f"p4.xn{qt}")
                    mxn = p.mark(s_v)
                    pending_x2 = (qt, mxn)
                flush_x2()

        if phases >= 5:
            for ps_i in range(4):
                sl = [4 * ps_i + j for j in range(4)]
                p.phase()
                with ExitStack() as ph:
                    h32 = [ph.enter_context(
                        nc.sbuf_tensor(f"h32{b_}_{ps_i}", [128, 2048], F32))
                        for b_ in range(2)]
                    sg = ph.enter_context(
                        nc.sbuf_tensor(f"sg_{ps_i}", [128, 2048], F32))
                    hsc4 = ph.enter_context(
                        nc.sbuf_tensor(f"hsc4_{ps_i}", [128, 4096], BF16))
                    wex = ph.enter_context(
                        nc.sbuf_tensor(f"wex_{ps_i}", [128, 4096], BF16))
                    aabn = ph.enter_context(
                        nc.sbuf_tensor(f"aabn_{ps_i}", [128, 1024], F32))
                    aab = aabn[:, 0:512]
                    nb = [aabn[:, 512 + i * 128:512 + (i + 1) * 128]
                          for i in range(4)]

                    def hscv(ic_, pt_):
                        o = (ic_ % 2) * 2048 + pt_ * 512
                        return hsc4[:, o:o + 512]

                    def half(s):
                        if s < 4:
                            return rkv[:, s * 4096:(s + 1) * 4096]
                        return wex[:]

                    def wtile(L, f):
                        h = half((2 * L + (1 if f >= 8 else 0)) % 5)
                        return h[:, (f % 8) * 512:((f % 8) + 1) * 512]

                    def dma_half(L, k, waits):
                        ic, j = divmod(L, 2)
                        dst = half((2 * L + k) % 5)[:].rearrange(
                            "q (f i) -> q f i", i=512)
                        srcw = wg_in if j == 0 else wu_in
                        s_ = srcw[k * 1024:(k + 1) * 1024,
                                  ic * 512:(ic + 1) * 512]
                        s_ = s_.rearrange("(f q) i -> q f i", q=128)
                        p.emit("gpsimd", _fn("dma_start", dst, s_),
                               waits=waits, inc=s_w, inc_amt=16,
                               tag=f"f1.w{ps_i}.{L}.{k}")
                        return p.mark(s_w)

                    mld = {}
                    dma_half(0, 0, [])
                    mld[0] = dma_half(0, 1, [])
                    dma_half(1, 0, [])
                    mld[1] = dma_half(1, 1, [])
                    dma_half(2, 0, [])
                    m_silu = 0
                    m_hf = 0
                    m_hf_prev = 0
                    m_sp_par = [0, 0]
                    m_dj = 0
                    m_hst = 0
                    m_hsev = 0
                    m_hsev2 = [0, 0]
                    pend_hst = []
                    pend_sp = None

                    def emit_hst(ic, all_=False):
                        nonlocal m_hst, m_hsev
                        todo = [e for e in pend_hst
                                if all_ or e[0] <= ic - 2]
                        pend_hst[:] = [e for e in pend_hst
                                       if not (all_ or e[0] <= ic - 2)]
                        for (ric, pt_, msp) in todo:
                            par = pt_ % 2
                            bk = psO_bf0[:, par * 1024:(par + 1) * 1024]
                            for k in range(4):
                                p.emit("tensor",
                                       _fn("transpose",
                                           bk[:, k * 128:(k + 1) * 128],
                                           hscv(ric, pt_)[:, k * 128:
                                                          (k + 1) * 128],
                                           ident[:]),
                                       waits=([(s_v, msp),
                                               (s_v, m_hsev2[par])]
                                              if k == 0 else []),
                                       inc=(s_pe if k == 3 else None),
                                       tag=f"f1.hT{ps_i}.{ric}.{pt_}.{k}")
                            m_hst = p.mark(s_pe)
                            rw_r = rw[:].rearrange("q (it t) -> q it t", t=512)
                            dstv = rw_r[:, ric * 4:(ric + 1) * 4,
                                        pt_ * 128:(pt_ + 1) * 128]
                            srcv = bk[:, 0:512].rearrange(
                                "q (k t) -> q k t", t=128)
                            p.emit("vector", _fn("tensor_copy", dstv, srcv),
                                   waits=[(s_pe, m_hst)], inc=s_v,
                                   tag=f"f1.he{ps_i}.{ric}.{pt_}")
                            m_hsev2[par] = p.mark(s_v)
                            m_hsev = m_hsev2[par]

                    def emit_sparsify():
                        nonlocal pend_sp, m_dj, m_sp_par
                        if pend_sp is None:
                            return
                        ric, mhf = pend_sp
                        pend_sp = None

                        def jv(t_, j):
                            v3 = t_.rearrange("q (g j) -> q g j", j=4)
                            return v3[:, :, j]

                        for pt_ in range(4):
                            hv = h32[ric % 2][:, pt_ * 512:(pt_ + 1) * 512]
                            p.emit("scalar", _fn("activation", aab, hv, AF.Abs),
                                   waits=[(s_v, mhf), (s_v, m_dj)],
                                   inc=s_a, tag=f"f1.ab{ps_i}.{ric}.{pt_}")
                            m_ab = p.mark(s_a)
                            gap()
                            p.emit("vector", _fn("tensor_tensor", nb[0][:],
                                                 jv(aab, 0), jv(aab, 1),
                                                 ALU.min),
                                   waits=[(s_a, m_ab)],
                                   tag=f"f1.n1{ps_i}.{ric}.{pt_}")
                            p.emit("vector", _fn("tensor_tensor", nb[1][:],
                                                 jv(aab, 0), jv(aab, 1),
                                                 ALU.max),
                                   tag=f"f1.n2{ps_i}.{ric}.{pt_}")
                            p.emit("vector", _fn("tensor_tensor", nb[2][:],
                                                 jv(aab, 2), jv(aab, 3),
                                                 ALU.min),
                                   tag=f"f1.n3{ps_i}.{ric}.{pt_}")
                            p.emit("vector", _fn("tensor_tensor", nb[3][:],
                                                 jv(aab, 2), jv(aab, 3),
                                                 ALU.max),
                                   tag=f"f1.n4{ps_i}.{ric}.{pt_}")
                            p.emit("vector", _fn("tensor_tensor", nb[0][:],
                                                 nb[0][:], nb[2][:], ALU.max),
                                   tag=f"f1.n5{ps_i}.{ric}.{pt_}")
                            p.emit("vector", _fn("tensor_tensor", nb[1][:],
                                                 nb[1][:], nb[3][:], ALU.min),
                                   tag=f"f1.n6{ps_i}.{ric}.{pt_}")
                            gap()
                            p.emit("vector", _fn("tensor_tensor", nb[0][:],
                                                 nb[0][:], nb[1][:], ALU.min),
                                   tag=f"f1.n7{ps_i}.{ric}.{pt_}")
                            gap()
                            for j in range(4):
                                p.emit("vector",
                                       _fn("tensor_tensor", jv(aab, j),
                                           jv(aab, j), nb[0][:], ALU.subtract),
                                       tag=f"f1.d{ps_i}.{ric}.{pt_}.{j}")
                            for j in range(4):
                                p.emit("vector",
                                       _fn("scalar_tensor_tensor",
                                           jv(hscv(ric, pt_)[:], j),
                                           jv(aab, j), 0.0,
                                           jv(hv, j), ALU.is_gt, ALU.mult),
                                       waits=([(s_pe, m_hst)] if j == 0
                                              else []),
                                       inc=(s_v if j == 3 else None),
                                       tag=f"f1.m{ps_i}.{ric}.{pt_}.{j}")
                            m_sp = p.mark(s_v)
                            m_dj = m_sp
                            pend_hst.append((ric, pt_, m_sp))
                        m_sp_par[ric % 2] = m_dj

                    for ic in range(16):
                        Lg, Lu = 2 * ic, 2 * ic + 1
                        for pl in range(4):
                            for f in range(NFT):
                                w0 = ([(s_w, mld.get(Lg, 0)), (s_a, m_silu)]
                                      if (pl == 0 and f == 0) else [])
                                p.emit("tensor",
                                       _fn("matmul",
                                           psE[:, pl * 512:(pl + 1) * 512],
                                           slot(sl[pl])[:, f * 128:(f + 1) * 128],
                                           wtile(Lg, f),
                                           start=(f == 0), stop=(f == NFT - 1)),
                                       waits=w0,
                                       inc=(s_pe if (pl == 3 and f == NFT - 1)
                                            else None),
                                       tag=f"f1.g{ps_i}.{ic}.{pl}.{f}")
                        m_gd = p.mark(s_pe)
                        if Lg + 2 < 32:
                            dma_half(Lg + 2, 1, [(s_pe, m_gd)])
                            mld[Lg + 2] = p.mark(s_w)
                        if Lg + 3 < 32:
                            dma_half(Lg + 3, 0, [])
                        emit_hst(ic)
                        for pl in range(4):
                            for f in range(NFT):
                                p.emit("tensor",
                                       _fn("matmul",
                                           psO[:, pl * 512:(pl + 1) * 512],
                                           slot(sl[pl])[:, f * 128:(f + 1) * 128],
                                           wtile(Lu, f),
                                           start=(f == 0), stop=(f == NFT - 1)),
                                       waits=([(s_w, mld.get(Lu, 0)),
                                               (s_v, m_hsev), (s_v, m_hf)]
                                              if (pl == 0 and f == 0) else []),
                                       inc=(s_pe if (pl == 3 and f == NFT - 1)
                                            else None),
                                       tag=f"f1.u{ps_i}.{ic}.{pl}.{f}")
                        m_umm = p.mark(s_pe)
                        if Lu + 2 < 32:
                            dma_half(Lu + 2, 1, [(s_pe, m_umm)])
                            mld[Lu + 2] = p.mark(s_w)
                        if Lu + 3 < 32:
                            dma_half(Lu + 3, 0, [])
                        p.emit("scalar", _fn("activation", sg[:],
                                             psE[:, 0:2048], AF.Silu),
                               waits=[(s_pe, m_gd), (s_v, m_hf)],
                               inc=s_a, tag=f"f1.si{ps_i}.{ic}")
                        m_silu = p.mark(s_a)
                        hd_ = h32[ic % 2][:].rearrange("q (pt e) -> q pt e",
                                                       e=512)
                        sg3 = sg[:].rearrange("q (pt e) -> q pt e", e=512)
                        pu3 = psO[:].rearrange("q (pt e) -> q pt e", e=512)
                        p.emit("vector", _fn("tensor_tensor", hd_, sg3, pu3,
                                             ALU.mult),
                               waits=[(s_a, m_silu), (s_pe, m_umm),
                                      (s_v, m_sp_par[ic % 2])],
                               inc=s_v, tag=f"f1.h{ps_i}.{ic}")
                        m_hf_prev = m_hf
                        m_hf = p.mark(s_v)
                        emit_sparsify()
                        pend_sp = (ic, m_hf)
                    emit_sparsify()
                    emit_hst(99, all_=True)

                p.phase()
                with ExitStack() as ph:
                    fbf = ph.enter_context(
                        nc.sbuf_tensor(f"fbf_{ps_i}", [128, 4 * C], BF16))
                    wd0 = ph.enter_context(
                        nc.sbuf_tensor(f"wd0_{ps_i}", [128, 4096], BF16))
                    wd1 = ph.enter_context(
                        nc.sbuf_tensor(f"wd1_{ps_i}", [128, 4096], BF16))
                    fo1 = ph.enter_context(
                        nc.sbuf_tensor(f"fo1_{ps_i}", [128, 512], F32))
                    fo2 = ph.enter_context(
                        nc.sbuf_tensor(f"fo2_{ps_i}", [128, 512], F32))
                    fo3 = ph.enter_context(
                        nc.sbuf_tensor(f"fo3_{ps_i}", [128, 512], F32))
                    wdsl = [wd0, wd1]

                    def hsf_t(it, pl):
                        return rw[:, it * 512 + pl * 128: it * 512 + (pl + 1) * 128]

                    def dma_wd(os_, ic4, waits):
                        a = ic4 % 2
                        dst = wdsl[a][:].rearrange("q (it o) -> q it o", o=512)
                        src = wd_in[ic4 * 1024:(ic4 + 1) * 1024,
                                    os_ * 512:(os_ + 1) * 512]
                        src = src.rearrange("(it q) o -> q it o", q=128)
                        p.emit("gpsimd", _fn("dma_start", dst, src),
                               waits=waits, inc=s_w, inc_amt=16,
                               tag=f"f2.wd{ps_i}.{os_}.{ic4}")
                        return p.mark(s_w)

                    m_fev2 = [0, 0]
                    m_wmm = {}
                    for os_ in range(4):
                        mw2 = {}
                        mw2[0] = dma_wd(os_, 0,
                                        [(s_pe, m_wmm.get((os_ - 1, 7), 0))])
                        mw2[1] = dma_wd(os_, 1, [])
                        acc = psE if os_ % 2 == 0 else psO
                        for ic4 in range(8):
                            a = ic4 % 2
                            for pl in range(4):
                                for it in range(8):
                                    itg = ic4 * 8 + it
                                    p.emit("tensor",
                                           _fn("matmul",
                                               acc[:, pl * 512:(pl + 1) * 512],
                                               hsf_t(itg, pl),
                                               wdsl[a][:, it * 512:(it + 1) * 512],
                                               start=(ic4 == 0 and it == 0),
                                               stop=(ic4 == 7 and it == 7)),
                                           waits=([(s_w, mw2[ic4]),
                                                   (s_v, m_fev2[os_ % 2])]
                                                  if (pl == 0 and it == 0)
                                                  else []),
                                           inc=(s_pe if (pl == 3 and it == 7)
                                                else None),
                                           tag=f"f2.m{ps_i}.{os_}.{ic4}.{pl}")
                            m_wmm[(os_, ic4)] = p.mark(s_pe)
                            if ic4 + 2 < 8:
                                mw2[ic4 + 2] = dma_wd(
                                    os_, ic4 + 2,
                                    [(s_pe, m_wmm[(os_, ic4)])])
                        mlast = m_wmm[(os_, 7)]
                        for pl in range(4):
                            p.emit("vector",
                                   _fn("bn_stats",
                                       vec(V_BNF + pl * 24 + os_ * 6, 6),
                                       acc[:, pl * 512:(pl + 1) * 512]),
                                   waits=[(s_pe, mlast)] if pl == 0 else [],
                                   tag=f"f2.bn{ps_i}.{os_}.{pl}")
                            p.emit("vector",
                                   _fn("tensor_copy",
                                       fbf[:, pl * C + os_ * 512:
                                           pl * C + (os_ + 1) * 512],
                                       acc[:, pl * 512:(pl + 1) * 512]),
                                   inc=(s_v if pl == 3 else None),
                                   tag=f"f2.fe{ps_i}.{os_}.{pl}")
                        m_fev2[os_ % 2] = p.mark(s_v)

                    if debug and ps_i == 3:
                        p.emit("sync", _fn("dma_start", dbg_fbf, fbf[:]),
                               waits=[(s_v, m_fev2[0]), (s_v, m_fev2[1])],
                               inc=s_out, inc_amt=16, tag="dbg.fbf")
                    m_store = p.mark(s_out)
                    for pl in range(4):
                        gap()
                        p.emit("vector", _fn("bn_aggr", vec(V_M3 + pl * 2, 2),
                                             vec(V_BNF + pl * 24, 24)),
                               waits=([(s_v, m_fev2[0]), (s_v, m_fev2[1])]
                                      if pl == 0 else []),
                               inc=s_v, tag=f"f2.ag{ps_i}.{pl}")
                    mag = p.mark(s_v)
                    for pl in range(4):
                        p.emit("scalar", _fn("activation", vec(V_LNT),
                                             vec(V_M3 + pl * 2 + 1), AF.Ln,
                                             bias=epsb[:, 0:1], scale=1.0),
                               waits=[(s_v, mag)] if pl == 0 else [],
                               tag=f"f2.l{ps_i}.{pl}")
                        p.emit("scalar", _fn("activation", vec(V_RSTD3 + pl),
                                             vec(V_LNT), AF.Exp, bias=0.0,
                                             scale=-0.5),
                               inc=s_a, tag=f"f2.r{ps_i}.{pl}")
                    mra = p.mark(s_a)
                    for pl in range(4):
                        p.emit("vector", _fn("tensor_tensor",
                                             vec(V_CMX + 4 + pl),
                                             vec(V_M3 + pl * 2),
                                             vec(V_RSTD3 + pl), ALU.mult),
                               waits=[(s_a, mra)] if pl == 0 else [],
                               tag=f"f2.c1{ps_i}.{pl}")
                    gap()
                    for pl in range(4):
                        p.emit("vector", _fn("tensor_tensor",
                                             vec(V_CMX + 4 + pl),
                                             vec(V_M2 + sl[pl]),
                                             vec(V_CMX + 4 + pl), ALU.subtract),
                               inc=(s_v if pl == 3 else None),
                               tag=f"f2.c2{ps_i}.{pl}")
                    mc2 = p.mark(s_v)
                    m_ps_free = [0, 0]
                    for pl in range(4):
                        par = pl % 2
                        bk = (psE if par == 0 else psO)[:, 0:1024].bitcast(BF16)
                        for f in range(NFT):
                            p.emit("tensor",
                                   _fn("transpose",
                                       bk[:, f * 128:(f + 1) * 128],
                                       slot(sl[pl])[:, f * 128:(f + 1) * 128],
                                       ident[:]),
                                   waits=([(s_v, m_ps_free[par]),
                                           (s_v, mc2)] if f == 0 else []),
                                   inc=(s_pe if f == NFT - 1 else None),
                                   tag=f"f2.xT{ps_i}.{pl}.{f}")
                        m_xnt5 = p.mark(s_pe)
                        for os_ in range(4):
                            cs = slice(os_ * 512, (os_ + 1) * 512)
                            p.emit("vector",
                                   _fn("tensor_scalar", fo1[:], bk[:, cs],
                                       vec(V_SD2 + sl[pl]),
                                       vec(V_CMX + 4 + pl),
                                       ALU.mult, ALU.add),
                                   waits=([(s_pe, m_xnt5), (s_out, m_store)]
                                          if os_ == 0 else
                                          [(s_out, m_store)]),
                                   tag=f"f2.o1{ps_i}.{pl}.{os_}")
                            p.emit("vector",
                                   _fn("scalar_tensor_tensor", fo3[:],
                                       fbf[:, pl * C + os_ * 512:
                                           pl * C + (os_ + 1) * 512],
                                       vec(V_RSTD3 + pl), fo1[:],
                                       ALU.mult, ALU.add),
                                   inc=s_v, tag=f"f2.o3{ps_i}.{pl}.{os_}")
                            mo = p.mark(s_v)
                            r0 = sl[pl] * 128
                            p.emit("sync", _fn("dma_start",
                                               out[r0:r0 + 128, cs], fo3[:]),
                                   waits=[(s_v, mo)], inc=s_out, inc_amt=16,
                                   tag=f"f2.st{ps_i}.{pl}.{os_}")
                            m_store = p.mark(s_out)
                        m_ps_free[par] = p.mark(s_v)

            p.phase()
            p.waitop("sync", [(s_out, p.mark(s_out))], tag="final")

        if debug:
            p.phase()
            nst = 0
            p.emit("sync", _fn("dma_start", dbg_rq, rq[:]), inc=s_out,
                   inc_amt=16, tag="dbg.rq")
            p.emit("sync", _fn("dma_start", dbg_rkv, rkv[:]), inc=s_out,
                   inc_amt=16, tag="dbg.rkv")
            p.emit("sync", _fn("dma_start", dbg_rw, rw[:]), inc=s_out,
                   inc_amt=16, tag="dbg.rw")
            p.emit("sync", _fn("dma_start", dbg_vecs, vecs[:]), inc=s_out,
                   inc_amt=16, tag="dbg.vecs")
            p.waitop("sync", [(s_out, p.mark(s_out))], tag="dbg.final")

        p.simulate()

        with nc.Block() as block:
            @block.sync
            def _(e):
                for f in p.ops["sync"]:
                    f(e)

            @block.gpsimd
            def _(e):
                for f in p.ops["gpsimd"]:
                    f(e)

            @block.tensor
            def _(e):
                for f in p.ops["tensor"]:
                    f(e)

            @block.vector
            def _(e):
                for f in p.ops["vector"]:
                    f(e)

            @block.scalar
            def _(e):
                for f in p.ops["scalar"]:
                    f(e)

    return nc


def make_consts():
    ident = np.eye(128, dtype=np.float32).astype(ml_dtypes.bfloat16)
    mask = np.where(np.arange(128)[None, :] <= np.arange(128)[:, None],
                    np.float32(0.0), np.float32(NEG)).astype(ml_dtypes.bfloat16)
    inv = (10000.0 ** (-(np.arange(0, HD, 2, dtype=np.float64) / HD)))
    th = np.arange(T, dtype=np.float64)[None, :] * inv[:, None]
    cos = np.cos(th).astype(np.float32)
    sin = np.sin(th).astype(np.float32)
    cosf = np.concatenate([cos, cos], axis=0).astype(ml_dtypes.bfloat16)
    sinf = np.concatenate([sin, sin], axis=0).astype(ml_dtypes.bfloat16)
    cosf2 = (np.concatenate([cos, cos], axis=0) / math.sqrt(HD)).astype(
        ml_dtypes.bfloat16)
    sinf2 = (np.concatenate([sin, sin], axis=0) / math.sqrt(HD)).astype(
        ml_dtypes.bfloat16)
    rot = np.zeros((128, 128), np.float32)
    for d in range(64):
        rot[d + 64, d] = -1.0
        rot[d, d + 64] = 1.0
    rot = rot.astype(ml_dtypes.bfloat16)
    onesbv = np.ones((128, 8), dtype=ml_dtypes.bfloat16)
    onesfv = np.ones((128, 128), dtype=np.float32)
    return ident, mask, cosf, sinf, rot, cosf2, sinf2, onesbv, onesfv


def kernel(L0_attn_x, L0_attn_Wq, L0_attn_Wk, L0_attn_Wv, L0_attn_Wproj,
           L0_ffn_w_gate, L0_ffn_w_up, L0_ffn_w_down,
           _phases=5, _debug=False):
    x = np.ascontiguousarray(np.asarray(L0_attn_x, dtype=np.float32))
    wq = np.ascontiguousarray(np.asarray(L0_attn_Wq, dtype=np.float32))
    wk = np.ascontiguousarray(np.asarray(L0_attn_Wk, dtype=np.float32))
    wv = np.ascontiguousarray(np.asarray(L0_attn_Wv, dtype=np.float32))
    wp = np.ascontiguousarray(np.asarray(L0_attn_Wproj, dtype=np.float32))
    wg = np.ascontiguousarray(np.asarray(L0_ffn_w_gate, dtype=np.float32))
    wu = np.ascontiguousarray(np.asarray(L0_ffn_w_up, dtype=np.float32))
    wd = np.ascontiguousarray(np.asarray(L0_ffn_w_down, dtype=np.float32))
    ident, mask, cosf, sinf, rot, cosf2, sinf2, onesbv, onesfv = make_consts()
    epsv = np.full((128, 8), EPS, dtype=np.float32)

    nc = build_nc(phases=_phases, debug=_debug)
    in_maps = []
    for c in range(8):
        in_maps.append({
            "x": np.ascontiguousarray(x[c]),
            "wq": wq, "wk": wk, "wv": wv, "wp": wp,
            "wg": wg, "wu": wu, "wd": wd,
            "identw": ident, "maskw": mask, "cosw": cosf, "sinw": sinf,
            "epsw": epsv, "rotw": rot, "cosw2": cosf2, "sinw2": sinf2,
            "onesbw": onesbv, "onesfw": onesfv,
        })
    res = run_bass_kernel_spmd(nc, in_maps, core_ids=list(range(8)))
    outs = np.stack([np.asarray(r["out"], dtype=np.float32)
                     for r in res.results], axis=0)
    if _debug:
        dbgs = [{k: np.asarray(v) for k, v in r.items() if k.startswith("dbg")}
                for r in res.results]
        return outs, dbgs
    return outs


# version 15
# speedup vs baseline: 1.0678x; 1.0304x over previous
import math
from collections import defaultdict
from contextlib import ExitStack

import numpy as np
import ml_dtypes

import concourse.bass as bass
import concourse.mybir as mybir
from concourse.bass_utils import run_bass_kernel_spmd

F32 = mybir.dt.float32
BF16 = mybir.dt.bfloat16
AF = mybir.ActivationFunctionType
ALU = mybir.AluOpType
AX = mybir.AxisListType

T = 2048
C = 2048
NH = 16
NKV = 4
HD = 128
II = 8192
NPT = T // 128
NFT = C // 128
EPS = 1e-5
SCALE = 1.0 / math.sqrt(HD)
NEG = -1.0e30

ENGS = ("sync", "gpsimd", "tensor", "vector", "scalar")


class Prog:

    def __init__(self, sems):
        self.ops = {e: [] for e in ENGS}
        self.cnt = defaultdict(int)
        self.meta = []
        self.sems = list(sems)
        self.bar = {}
        self.bar_done = set(ENGS)
        self.sem_by_id = {id(s): s for s in sems}

    def mark(self, sem):
        return self.cnt[id(sem)]

    def phase(self):
        self.bar = {id(s): self.cnt[id(s)] for s in self.sems}
        self.bar_done = set()

    def emit(self, eng, fn, waits=(), inc=None, inc_amt=1, tag=""):
        waits = [(s, v) for (s, v) in waits if v > 0]
        if eng not in self.bar_done:
            self.bar_done.add(eng)
            waits = [(self.sem_by_id[i], v) for i, v in self.bar.items()
                     if v > 0] + waits
        if inc is not None:
            self.cnt[id(inc)] += inc_amt
        self.meta.append((eng, [(id(s), v) for s, v in waits],
                          [(id(inc), inc_amt)] if inc is not None else [], tag))

        def run(e, fn=fn, waits=waits, inc=inc, amt=inc_amt):
            for s, v in waits:
                e.wait_ge(s, v)
            ins = fn(e)
            if inc is not None:
                ins.then_inc(inc, amt)
        self.ops[eng].append(run)

    def waitop(self, eng, waits, tag=""):
        self.emit(eng, lambda e: None, waits=waits, tag=tag)

    def simulate(self):
        queues = {e: [] for e in ENGS}
        for (eng, waits, incs, tag) in self.meta:
            queues[eng].append((waits, incs, tag))
        heads = {e: 0 for e in ENGS}
        cnt = defaultdict(int)
        progress = True
        while progress:
            progress = False
            for e in ENGS:
                while heads[e] < len(queues[e]):
                    waits, incs, tag = queues[e][heads[e]]
                    if all(cnt[s] >= v for s, v in waits):
                        for s, a in incs:
                            cnt[s] += a
                        heads[e] += 1
                        progress = True
                    else:
                        break
        if any(heads[e] < len(queues[e]) for e in ENGS):
            lines = []
            for e in ENGS:
                if heads[e] < len(queues[e]):
                    waits, incs, tag = queues[e][heads[e]]
                    missing = [(str(s), v, cnt[s]) for s, v in waits
                               if cnt[s] < v]
                    lines.append(f"{e}[{heads[e]}/{len(queues[e])}] tag={tag} "
                                 f"missing={missing}")
            raise RuntimeError("build-time DEADLOCK:\n" + "\n".join(lines))


def _fn(method, *args, **kwargs):
    return lambda e: getattr(e, method)(*args, **kwargs)


def build_nc(phases=5, debug=False):
    nc = bass.Bass("TRN2", target_bir_lowering=False)

    x_in = nc.dram_tensor("x", [T, C], F32, kind="ExternalInput").ap()
    wq_in = nc.dram_tensor("wq", [C, C], F32, kind="ExternalInput").ap()
    wk_in = nc.dram_tensor("wk", [C, NKV * HD], F32, kind="ExternalInput").ap()
    wv_in = nc.dram_tensor("wv", [C, NKV * HD], F32, kind="ExternalInput").ap()
    wp_in = nc.dram_tensor("wp", [C, C], F32, kind="ExternalInput").ap()
    wg_in = nc.dram_tensor("wg", [C, II], F32, kind="ExternalInput").ap()
    wu_in = nc.dram_tensor("wu", [C, II], F32, kind="ExternalInput").ap()
    wd_in = nc.dram_tensor("wd", [II, C], F32, kind="ExternalInput").ap()
    identw = nc.dram_tensor("identw", [128, 128], BF16, kind="ExternalInput").ap()
    maskw = nc.dram_tensor("maskw", [128, 128], BF16, kind="ExternalInput").ap()
    rotw = nc.dram_tensor("rotw", [128, 128], BF16, kind="ExternalInput").ap()
    onesbw = nc.dram_tensor("onesbw", [128, 8], BF16, kind="ExternalInput").ap()
    onesfw = nc.dram_tensor("onesfw", [128, 128], F32, kind="ExternalInput").ap()
    cosw = nc.dram_tensor("cosw", [128, T], BF16, kind="ExternalInput").ap()
    cosw2 = nc.dram_tensor("cosw2", [128, T], BF16, kind="ExternalInput").ap()
    sinw2 = nc.dram_tensor("sinw2", [128, T], BF16, kind="ExternalInput").ap()
    epsw = nc.dram_tensor("epsw", [128, 8], F32, kind="ExternalInput").ap()
    sinw = nc.dram_tensor("sinw", [128, T], BF16, kind="ExternalInput").ap()

    out = nc.dram_tensor("out", [T, C], F32, kind="ExternalOutput").ap()
    dbg = {}
    if debug:
        dbg_rq = nc.dram_tensor("dbg_rq", [128, NPT * C], BF16,
                                kind="ExternalOutput").ap()
        dbg_rkv = nc.dram_tensor("dbg_rkv", [128, 16384], BF16,
                                 kind="ExternalOutput").ap()
        dbg_vecs = nc.dram_tensor("dbg_vecs", [128, 256], F32,
                                  kind="ExternalOutput").ap()
        dbg_rw = nc.dram_tensor("dbg_rw", [128, 16 * C], BF16,
                                kind="ExternalOutput").ap()
        dbg_fbf = nc.dram_tensor("dbg_fbf", [128, 4 * C], BF16,
                                 kind="ExternalOutput").ap()

    with ExitStack() as top:
        E_ = top.enter_context
        rq = E_(nc.sbuf_tensor("rq", [128, NPT * C], BF16))
        rkv = E_(nc.sbuf_tensor("rkv", [128, 16384], BF16))
        rw = E_(nc.sbuf_tensor("rw", [128, 16 * C], BF16))
        vecs = E_(nc.sbuf_tensor("vecs", [128, 256], F32))
        ident = E_(nc.sbuf_tensor("ident", [128, 128], BF16))
        dmask = E_(nc.sbuf_tensor("dmask", [128, 128], BF16))
        rmat = E_(nc.sbuf_tensor("rmat", [128, 128], BF16))
        onesb = E_(nc.sbuf_tensor("onesb", [128, 8], BF16))
        onesf = E_(nc.sbuf_tensor("onesf", [128, 128], F32))
        wexs = E_(nc.sbuf_tensor("wexs", [128, 4096], BF16))
        epsb = E_(nc.sbuf_tensor("epsb", [128, 8], F32))
        psE = E_(nc.psum_tensor("psE", [128, 2048], F32))
        psO = E_(nc.psum_tensor("psO", [128, 2048], F32))

        s_x = E_(nc.semaphore("s_x"))
        s_w = E_(nc.semaphore("s_w"))
        s_pe = E_(nc.semaphore("s_pe"))
        s_v = E_(nc.semaphore("s_v"))
        s_a = E_(nc.semaphore("s_a"))
        s_out = E_(nc.semaphore("s_out"))
        p = Prog([s_x, s_w, s_pe, s_v, s_a, s_out])

        V_BNST, V_STAT = 0, 24
        V_NEGM, V_BIAS, V_SSUM, V_RSUM = 26, 27, 28, 29
        V_LNT, V_RSTD = 30, 32
        V_EPS = 34
        V_CMX = 36
        V_B16 = 80
        V_KN = 204
        V_QN = 212
        V_NTMP = 250
        V_M2, V_SD2 = 48, 64
        V_BNF = 96
        V_M3 = 192
        V_RSTD3 = 200

        def vec(col, n=1):
            return vecs[:, col:col + n]

        def gap(eng="vector"):
            p.emit(eng, _fn("tensor_copy", vecs[:, 44:48], epsb[:, 0:4]),
                   tag="gap")

        def slot(s):
            return rq[:, s * C:(s + 1) * C]

        def K_kvh(k):
            return rkv[:, k * T:(k + 1) * T]

        def V_pt(pt):
            return rkv[:, 8192 + pt * 512: 8192 + (pt + 1) * 512]

        psE_bf = psE[:, 0:1024].bitcast(BF16)
        psO_bf = psO[:, 1024:2048].bitcast(BF16)
        psO_bf0 = psO[:, 0:1024].bitcast(BF16)

        p.emit("sync", _fn("dma_start", ident[:], identw), inc=s_x, inc_amt=16,
               tag="c.id")
        p.emit("sync", _fn("dma_start", dmask[:], maskw), inc=s_x, inc_amt=16,
               tag="c.mask")
        p.emit("sync", _fn("dma_start", rmat[:], rotw), inc=s_x, inc_amt=16,
               tag="c.rot")
        p.emit("sync", _fn("dma_start", onesb[:], onesbw), inc=s_x,
               inc_amt=16, tag="c.ob")
        p.emit("sync", _fn("dma_start", onesf[:], onesfw), inc=s_x,
               inc_amt=16, tag="c.of")
        p.emit("sync", _fn("dma_start", epsb[:], epsw), inc=s_x,
               inc_amt=16, tag="c.eps")

        p.phase()
        with ExitStack() as ph:
            xt = [ph.enter_context(nc.sbuf_tensor(f"p1x{i}", [128, C], F32))
                  for i in range(2)]
            consumed = [0, 0]
            for pt in range(NPT):
                b = pt % 2
                p.emit("sync", _fn("dma_start", xt[b][:],
                                   x_in[pt * 128:(pt + 1) * 128, :]),
                       waits=[(s_v, consumed[b])], inc=s_x, inc_amt=16,
                       tag=f"p1.ld{pt}")
                mx_ld = p.mark(s_x)
                for i in range(4):
                    p.emit("vector", _fn("bn_stats", vec(V_BNST + 6 * i, 6),
                                         xt[b][:, 512 * i:512 * (i + 1)]),
                           waits=[(s_x, mx_ld)] if i == 0 else [],
                           tag=f"p1.bn{pt}.{i}")
                gap()
                p.emit("vector", _fn("bn_aggr", vec(V_STAT, 2), vec(V_BNST, 24)),
                       inc=s_v, tag=f"p1.ag{pt}")
                mv = p.mark(s_v)
                p.emit("scalar", _fn("activation", vec(V_LNT), vec(V_STAT + 1),
                                     AF.Ln, bias=epsb[:, 0:1], scale=1.0),
                       waits=[(s_v, mv)], tag=f"p1.ln{pt}")
                p.emit("scalar", _fn("activation", vec(V_RSTD), vec(V_LNT),
                                     AF.Exp, bias=0.0, scale=-0.5),
                       inc=s_a, tag=f"p1.ex{pt}")
                ma = p.mark(s_a)
                p.emit("vector", _fn("tensor_scalar", slot(pt)[:], xt[b][:],
                                     vec(V_STAT), vec(V_RSTD),
                                     ALU.subtract, ALU.mult),
                       waits=[(s_a, ma)], inc=s_v, tag=f"p1.nm{pt}")
                consumed[b] = p.mark(s_v)

        def qkv_pass(kind, ph):
            xnf4 = ph.enter_context(
                nc.sbuf_tensor(f"xnf4{kind}", [128, NFT * 512], BF16))
            cosb = ph.enter_context(nc.sbuf_tensor(f"cosb{kind}", [128, 512], BF16))
            sinb = ph.enter_context(nc.sbuf_tensor(f"sinb{kind}", [128, 512], BF16))
            rt1 = ph.enter_context(nc.sbuf_tensor(f"rt1{kind}", [128, 512], F32))
            rt2 = ph.enter_context(nc.sbuf_tensor(f"rt2{kind}", [128, 512], F32))
            kraw = ph.enter_context(nc.sbuf_tensor(f"kraw{kind}", [128, 512], BF16))
            sqb = ph.enter_context(nc.sbuf_tensor(f"sqb{kind}", [128, 512], BF16))

            if kind == "kv":
                for f in range(NFT):
                    p.emit("gpsimd", _fn("dma_start",
                                         rw[:, f * 512:(f + 1) * 512],
                                         wk_in[f * 128:(f + 1) * 128, :]),
                           inc=s_w, inc_amt=16, tag=f"p2.wk{f}")
                for f in range(NFT):
                    p.emit("gpsimd", _fn("dma_start",
                                         rw[:, 8192 + f * 512:8192 + (f + 1) * 512],
                                         wv_in[f * 128:(f + 1) * 128, :]),
                           inc=s_w, inc_amt=16, tag=f"p2.wv{f}")
            else:
                for f in range(NFT):
                    p.emit("gpsimd", _fn("dma_start",
                                         rw[:, f * C:(f + 1) * C],
                                         wq_in[f * 128:(f + 1) * 128, :]),
                           inc=s_w, inc_amt=16, tag=f"p3.wq{f}")
            mw = p.mark(s_w)

            m_xnf_ev = 0
            m_rope = 0
            m_cons_E = 0
            m_mm_done = 0
            m_csfree = 0
            for g in range(4):
                csrc = cosw if kind == "kv" else cosw2
                ssrc = sinw if kind == "kv" else sinw2
                p.emit("sync", _fn("dma_start", cosb[:],
                                   csrc[:, g * 512:(g + 1) * 512]),
                       waits=[(s_v, m_csfree)], inc=s_x, inc_amt=16,
                       tag=f"{kind}.cos{g}")
                p.emit("sync", _fn("dma_start", sinb[:],
                                   ssrc[:, g * 512:(g + 1) * 512]),
                       inc=s_x, inc_amt=16, tag=f"{kind}.sin{g}")
                m_cs = p.mark(s_x)

                ev = []
                prev_ev = m_xnf_ev
                for pl in range(4):
                    pt = g * 4 + pl
                    for f in range(NFT):
                        p.emit("tensor",
                               _fn("transpose",
                                   psE_bf[:, f * 128:(f + 1) * 128],
                                   slot(pt)[:, f * 128:(f + 1) * 128],
                                   ident[:]),
                               waits=([(s_v, prev_ev), (s_v, m_cons_E)]
                                      if f == 0 else []),
                               inc=(s_pe if f == NFT - 1 else None),
                               tag=f"{kind}.T{pt}.{f}")
                    mtp = p.mark(s_pe)
                    dst = xnf4[:].rearrange("q (f t) -> q f t", t=512)
                    dst = dst[:, :, pl * 128:(pl + 1) * 128]
                    src = psE_bf[:].rearrange("q (f t) -> q f t", t=128)
                    p.emit("vector", _fn("tensor_copy", dst, src),
                           waits=[(s_pe, mtp), (s_pe, m_mm_done)], inc=s_v,
                           tag=f"{kind}.ev{pt}")
                    ev.append(p.mark(s_v))
                    prev_ev = ev[-1]
                m_xnf_ev = ev[-1]

                def xnf_t(f, width=512, off=0):
                    return xnf4[:, f * 512 + off: f * 512 + off + width]

                if kind == "kv":
                    def wk_t(f, kvh):
                        return rw[:, f * 512 + kvh * 128: f * 512 + (kvh + 1) * 128]

                    def wv_t(f):
                        return rw[:, 8192 + f * 512: 8192 + (f + 1) * 512]

                    for kvh in range(NKV):
                        for f in range(NFT):
                            p.emit("tensor",
                                   _fn("matmul",
                                       psO[:, kvh * 512:(kvh + 1) * 512],
                                       wk_t(f, kvh), xnf_t(f),
                                       start=(f == 0), stop=(f == NFT - 1)),
                                   waits=([(s_v, m_xnf_ev), (s_w, mw),
                                           (s_v, m_rope)]
                                          if (kvh == 0 and f == 0) else []),
                                   inc=(s_pe if f == NFT - 1 else None),
                                   tag=f"p2.K{g}.{kvh}.{f}")
                    mk = p.mark(s_pe)
                    for pl in range(4):
                        for f in range(NFT):
                            p.emit("tensor",
                                   _fn("matmul",
                                       psE[:, pl * 512:(pl + 1) * 512],
                                       xnf_t(f, 128, pl * 128), wv_t(f),
                                       start=(f == 0), stop=(f == NFT - 1)),
                                   waits=([(s_v, m_cons_E)]
                                          if (pl == 0 and f == 0) else []),
                                   inc=(s_pe if f == NFT - 1 else None),
                                   tag=f"p2.V{g}.{pl}.{f}")
                    m_mm_done = p.mark(s_pe)
                    mvv = m_mm_done

                    for kvh in range(NKV):
                        ps = psO[:, kvh * 512:(kvh + 1) * 512]
                        kd = K_kvh(kvh)[:, g * 512:(g + 1) * 512]
                        p.emit("vector", _fn("tensor_copy", kraw[:], ps),
                               waits=[(s_pe, mk), (s_x, m_cs),
                                      (s_pe, p.mark(s_pe))],
                               inc=s_v, tag=f"p2.kr{g}.{kvh}")
                        mkraw = p.mark(s_v)
                        p.emit("tensor", _fn("matmul", ps, rmat[:], kraw[:],
                                             start=True, stop=True),
                               waits=[(s_v, mkraw)], inc=s_pe,
                               tag=f"p2.rot{g}.{kvh}")
                        mrot = p.mark(s_pe)
                        p.emit("vector", _fn("tensor_tensor", rt1[:], kraw[:],
                                             cosb[:], ALU.mult),
                               tag=f"p2.r{g}.{kvh}.a")
                        p.emit("vector", _fn("tensor_tensor", rt2[:], ps,
                                             sinb[:], ALU.mult),
                               waits=[(s_pe, mrot)],
                               tag=f"p2.r{g}.{kvh}.b")
                        p.emit("vector", _fn("tensor_tensor", kd, rt1[:],
                                             rt2[:], ALU.add),
                               inc=s_v, tag=f"p2.r{g}.{kvh}.c")
                        p.emit("vector", _fn("tensor_tensor", sqb[:], kraw[:],
                                             kraw[:], ALU.mult),
                               inc=s_v, tag=f"p2.sq{g}.{kvh}")
                        msq = p.mark(s_v)
                        p.emit("tensor", _fn("matmul", ps[0:1, :],
                                             onesb[:, 0:1], sqb[:],
                                             start=True, stop=True),
                               waits=[(s_v, msq)], inc=s_pe,
                               tag=f"p2.nm{g}.{kvh}")
                        mnm = p.mark(s_pe)
                        p.emit("vector", _fn("reduce_max",
                                             vecs[0:1, V_NTMP:V_NTMP + 1],
                                             ps[0:1, :], AX.X),
                               waits=[(s_pe, mnm)], tag=f"p2.nr{g}.{kvh}")
                        gap()
                        ca = V_KN + kvh * 2 + (g % 2)
                        cb = V_KN + kvh * 2 + ((g + 1) % 2)
                        if g == 0:
                            p.emit("vector", _fn("tensor_copy",
                                                 vecs[0:1, cb:cb + 1],
                                                 vecs[0:1, V_NTMP:V_NTMP + 1]),
                                   inc=s_v, tag=f"p2.nc{g}.{kvh}")
                        else:
                            p.emit("vector", _fn("tensor_tensor",
                                                 vecs[0:1, cb:cb + 1],
                                                 vecs[0:1, ca:ca + 1],
                                                 vecs[0:1, V_NTMP:V_NTMP + 1],
                                                 ALU.max),
                                   inc=s_v, tag=f"p2.nc{g}.{kvh}")
                    m_rope = p.mark(s_v)
                    m_csfree = m_rope
                    for pl in range(4):
                        p.emit("vector", _fn("tensor_copy", V_pt(g * 4 + pl)[:],
                                             psE[:, pl * 512:(pl + 1) * 512]),
                               waits=[(s_pe, mvv)] if pl == 0 else [],
                               inc=(s_v if pl == 3 else None),
                               tag=f"p2.vev{g}.{pl}")
                    m_cons_E = p.mark(s_v)
                else:
                    def wq_t(f, h):
                        return rw[:, f * C + h * 128: f * C + (h + 1) * 128]

                    rq_r = rq[:].rearrange("q (s e) -> q s e", e=C)
                    m_ropeq = [0, 0, 0, 0]
                    for h in range(NH):
                        qtr = h % 4
                        ps = psO[:, qtr * 512:(qtr + 1) * 512]
                        for f in range(NFT):
                            p.emit("tensor",
                                   _fn("matmul", ps, wq_t(f, h), xnf_t(f),
                                       start=(f == 0), stop=(f == NFT - 1)),
                                   waits=([(s_v, m_xnf_ev), (s_w, mw)]
                                          if (h == 0 and f == 0) else []) +
                                         ([(s_v, m_ropeq[qtr])] if f == 0 else []),
                                   inc=(s_pe if f == NFT - 1 else None),
                                   tag=f"p3.Q{g}.{h}.{f}")
                        mq = p.mark(s_pe)
                        if h == NH - 1:
                            m_mm_done = mq
                        dst = rq_r[:, 4 * g:4 * g + 4, h * 128:(h + 1) * 128]
                        p.emit("vector", _fn("tensor_copy", kraw[:], ps),
                               waits=[(s_pe, mq)] + ([(s_x, m_cs)]
                                                     if h == 0 else []),
                               inc=s_v, tag=f"p3.kr{g}.{h}")
                        mkraw = p.mark(s_v)
                        p.emit("tensor", _fn("matmul", ps, rmat[:], kraw[:],
                                             start=True, stop=True),
                               waits=[(s_v, mkraw)], inc=s_pe,
                               tag=f"p3.rot{g}.{h}")
                        mrot = p.mark(s_pe)
                        p.emit("vector", _fn("tensor_tensor", rt1[:], kraw[:],
                                             cosb[:], ALU.mult),
                               tag=f"p3.r{g}.{h}.a")
                        p.emit("vector", _fn("tensor_tensor", rt2[:], ps,
                                             sinb[:], ALU.mult),
                               waits=[(s_pe, mrot)],
                               tag=f"p3.r{g}.{h}.b")
                        p.emit("vector",
                               _fn("tensor_tensor", dst,
                                   rt1[:].rearrange("q (s e) -> q s e", e=128),
                                   rt2[:].rearrange("q (s e) -> q s e", e=128),
                                   ALU.add),
                               inc=s_v, tag=f"p3.r{g}.{h}.c")
                        p.emit("vector", _fn("tensor_tensor", sqb[:], kraw[:],
                                             kraw[:], ALU.mult),
                               inc=s_v, tag=f"p3.sq{g}.{h}")
                        msq = p.mark(s_v)
                        p.emit("tensor", _fn("matmul", ps[0:1, :],
                                             onesb[:, 0:1], sqb[:],
                                             start=True, stop=True),
                               waits=[(s_v, msq)], inc=s_pe,
                               tag=f"p3.nm{g}.{h}")
                        mnm = p.mark(s_pe)
                        p.emit("vector", _fn("reduce_max",
                                             vecs[0:1, V_NTMP:V_NTMP + 1],
                                             ps[0:1, :], AX.X),
                               waits=[(s_pe, mnm)], tag=f"p3.nr{g}.{h}")
                        gap()
                        ca = V_QN + h * 2 + (g % 2)
                        cb = V_QN + h * 2 + ((g + 1) % 2)
                        if g == 0:
                            p.emit("vector", _fn("tensor_copy",
                                                 vecs[0:1, cb:cb + 1],
                                                 vecs[0:1, V_NTMP:V_NTMP + 1]),
                                   inc=s_v, tag=f"p3.nc{g}.{h}")
                        else:
                            p.emit("vector", _fn("tensor_tensor",
                                                 vecs[0:1, cb:cb + 1],
                                                 vecs[0:1, ca:ca + 1],
                                                 vecs[0:1, V_NTMP:V_NTMP + 1],
                                                 ALU.max),
                                   inc=s_v, tag=f"p3.nc{g}.{h}")
                        m_ropeq[qtr] = p.mark(s_v)
                    m_rope = p.mark(s_v)
                    m_csfree = m_rope

        if phases >= 2:
            p.phase()
            with ExitStack() as ph:
                qkv_pass("kv", ph)
        if phases >= 3:
            p.phase()
            with ExitStack() as ph:
                qkv_pass("q", ph)

        if phases >= 4:
            p.phase()
            with ExitStack() as ph:
                xt4 = ph.enter_context(nc.sbuf_tensor("p4x", [128, C], BF16))
                x2b = ph.enter_context(nc.sbuf_tensor("p4x2b", [128, C], F32))
                pbf = ph.enter_context(nc.sbuf_tensor("p4pb", [128, T], BF16))
                pnb = ph.enter_context(nc.sbuf_tensor("p4pn", [128, T], BF16))
                ptsb = ph.enter_context(
                    nc.sbuf_tensor("p4pt", [128, 2 * T], BF16))
                ysb = ph.enter_context(nc.sbuf_tensor("p4y", [128, C], BF16))
                xn2tm = ph.enter_context(nc.sbuf_tensor("p4x2", [128, C], BF16))

                for f in range(NFT):
                    p.emit("gpsimd", _fn("dma_start", rw[:, f * C:(f + 1) * C],
                                         wp_in[f * 128:(f + 1) * 128, :]),
                           inc=s_w, inc_amt=16, tag=f"p4.wp{f}")
                mwp = p.mark(s_w)

                def wp_t(f, oc):
                    return rw[:, f * C + oc * 512: f * C + (oc + 1) * 512]

                for h in range(NH):
                    p.emit("vector", _fn("tensor_tensor",
                                         vecs[0:1, V_CMX:V_CMX + 1],
                                         vecs[0:1, V_QN + h * 2:
                                              V_QN + h * 2 + 1],
                                         vecs[0:1, V_KN + (h // 4) * 2:
                                              V_KN + (h // 4) * 2 + 1],
                                         ALU.mult),
                           inc=s_v, tag=f"p4.bb{h}")
                    mbb = p.mark(s_v)
                    p.emit("scalar", _fn("activation",
                                         vecs[0:1, V_CMX + 1:V_CMX + 2],
                                         vecs[0:1, V_CMX:V_CMX + 1], AF.Ln,
                                         bias=0.0, scale=SCALE * SCALE),
                           waits=[(s_v, mbb)], tag=f"p4.bl{h}")
                    p.emit("scalar", _fn("activation",
                                         vecs[0:1, V_NTMP:V_NTMP + 1],
                                         vecs[0:1, V_CMX + 1:V_CMX + 2],
                                         AF.Exp, bias=0.0, scale=0.5),
                           inc=s_a, tag=f"p4.be{h}")
                    mbe = p.mark(s_a)
                    p.emit("tensor", _fn("matmul", psE[:, 0:1],
                                         onesf[0:1, 0:128],
                                         vecs[0:1, V_NTMP:V_NTMP + 1],
                                         start=True, stop=True),
                           waits=[(s_a, mbe), (s_v, p.mark(s_v))],
                           inc=s_pe, tag=f"p4.bm{h}")
                    mbm = p.mark(s_pe)
                    p.emit("vector", _fn("tensor_scalar_mul",
                                         vec(V_B16 + h), psE[:, 0:1], -1.0),
                           waits=[(s_pe, mbm)], inc=s_v, tag=f"p4.bn{h}")
                m_bias16 = p.mark(s_v)
                p.waitop("tensor", [(s_v, m_bias16)], tag="p4.biasbar")

                m_exp = 0
                m_norm = 0
                m_tp = 0
                m_ptev = 0
                m_pv = 0
                m_yev = [0, 0]
                m_proj = 0
                m_zev = 0
                m_x2 = 0
                m_xnt = 0
                m_m2cp = 0
                m_slot_ev = 0

                def head_pe(qt, h):
                    nk = qt + 1
                    L = nk * 128
                    nch = (L + 511) // 512
                    kvh = h // 4
                    first = [(s_a, m_exp), (s_v, m_zev)]
                    marks = []
                    for c_ in range(nch):
                        wdt = min(512, L - c_ * 512)
                        last = (c_ == nch - 1)
                        p.emit("tensor",
                               _fn("matmul", psE[:, c_ * 512:c_ * 512 + wdt],
                                   slot(qt)[:, h * 128:(h + 1) * 128],
                                   K_kvh(kvh)[:, c_ * 512:c_ * 512 + wdt],
                                   start=True, stop=not last),
                               waits=first if c_ == 0 else [],
                               tag=f"p4.S{qt}.{h}.{c_}")
                        first = []
                    p.emit("tensor",
                           _fn("matmul", psE[:, qt * 128:L], ident[:], dmask[:],
                               start=False, stop=True),
                           inc=s_pe, tag=f"p4.M{qt}.{h}")
                    marks.append(p.mark(s_pe))
                    return marks

                def head_va(qt, h, marks):
                    nonlocal m_exp, m_norm
                    L = (qt + 1) * 128
                    p.emit("scalar", _fn("activation", pbf[:, 0:L], psE[:, 0:L],
                                         AF.Exp, bias=vec(V_B16 + h), scale=1.0,
                                         accum_out=vec(V_SSUM)),
                           waits=[(s_pe, marks[-1]), (s_v, m_norm)],
                           inc=s_a, tag=f"p4.ex{qt}.{h}")
                    m_exp = p.mark(s_a)
                    p.emit("vector", _fn("reciprocal", vec(V_RSUM), vec(V_SSUM)),
                           waits=[(s_a, m_exp)], tag=f"p4.rs{qt}.{h}")
                    gap()
                    p.emit("vector", _fn("tensor_scalar_mul", pnb[:, 0:L],
                                         pbf[:, 0:L], vec(V_RSUM)),
                           waits=[(s_pe, m_tp)], inc=s_v, tag=f"p4.nm{qt}.{h}")
                    m_norm = p.mark(s_v)

                def head_tail_pe(qt, h):
                    nonlocal m_tp, m_pv, m_ptev, m_yev
                    nk = qt + 1
                    kvh = h // 4
                    pr = h // 2
                    hh = h % 2
                    pt_r = ptsb[:].rearrange("q (kt e) -> q kt e", e=256)
                    for kt in range(nk):
                        p.emit("tensor",
                               _fn("transpose",
                                   psO_bf[:, kt * 128:(kt + 1) * 128],
                                   pnb[:, kt * 128:(kt + 1) * 128], ident[:]),
                               waits=([(s_v, m_norm), (s_v, m_ptev),
                                       (s_v, m_slot_ev), (s_v, m_x2)]
                                      if kt == 0 else []),
                               inc=(s_pe if kt == nk - 1 else None),
                               tag=f"p4.T{qt}.{h}.{kt}")
                    m_tp = p.mark(s_pe)
                    dst = pt_r[:, 0:nk, hh * 128:(hh + 1) * 128]
                    src_ = psO_bf[:, 0:nk * 128].rearrange(
                        "q (kt e) -> q kt e", e=128)
                    p.emit("vector", _fn("tensor_copy", dst, src_),
                           waits=[(s_pe, m_tp), (s_pe, m_pv)], inc=s_v,
                           tag=f"p4.PT{qt}.{h}")
                    m_ptev = p.mark(s_v)
                    if hh == 1:
                        ycol = (pr % 2) * 512
                        for kt in range(nk):
                            p.emit("tensor",
                                   _fn("matmul", psO[:, ycol:ycol + 256],
                                       V_pt(kt)[:, kvh * 128:(kvh + 1) * 128],
                                       ptsb[:, kt * 256:(kt + 1) * 256],
                                       start=(kt == 0), stop=(kt == nk - 1)),
                                   waits=([(s_v, m_ptev),
                                           (s_v, m_yev[pr % 2]),
                                           (s_v, m_x2)]
                                          if kt == 0 else []),
                                   inc=(s_pe if kt == nk - 1 else None),
                                   tag=f"p4.PV{qt}.{pr}.{kt}")
                        m_pv = p.mark(s_pe)
                        p.emit("vector",
                               _fn("tensor_copy",
                                   ysb[:, pr * 256:(pr + 1) * 256],
                                   psO[:, ycol:ycol + 256]),
                               waits=[(s_pe, m_pv), (s_pe, m_proj)],
                               inc=s_v, tag=f"p4.ye{qt}.{pr}")
                        m_yev[pr % 2] = p.mark(s_v)

                pending_x2 = None

                def flush_x2():
                    nonlocal m_xnt, m_slot_ev, pending_x2
                    if pending_x2 is None:
                        return
                    fqt, fmxn = pending_x2
                    pending_x2 = None
                    for f in range(NFT):
                        p.emit("tensor",
                               _fn("transpose",
                                   psO_bf[:, f * 128:(f + 1) * 128],
                                   xn2tm[:, f * 128:(f + 1) * 128], ident[:]),
                               waits=([(s_v, fmxn), (s_v, m_ptev)]
                                      if f == 0 else []),
                               inc=(s_pe if f == NFT - 1 else None),
                               tag=f"p4.xT{fqt}.{f}")
                    m_xnt = p.mark(s_pe)
                    p.emit("vector", _fn("tensor_copy", slot(fqt)[:],
                                         psO_bf[:]),
                           waits=[(s_pe, m_xnt)], inc=s_v, tag=f"p4.xe{fqt}")
                    m_slot_ev = p.mark(s_v)

                for qt in range(NPT):
                    p.emit("gpsimd", _fn("dma_start", xt4[:],
                                         x_in[qt * 128:(qt + 1) * 128, :]),
                           waits=[(s_v, m_x2)], inc=s_w, inc_amt=16,
                           tag=f"p4.ldx{qt}")
                    mx4 = p.mark(s_w)
                    for h in range(NH):
                        marks = head_pe(qt, h)
                        if h == 2:
                            flush_x2()
                        if h > 0:
                            head_tail_pe(qt, h - 1)
                        head_va(qt, h, marks)
                    head_tail_pe(qt, NH - 1)
                    for f in range(NFT):
                        for oc in range(4):
                            p.emit("tensor",
                                   _fn("matmul", psE[:, oc * 512:(oc + 1) * 512],
                                       ysb[:, f * 128:(f + 1) * 128],
                                       wp_t(f, oc),
                                       start=(f == 0), stop=(f == NFT - 1)),
                                   waits=([(s_v, m_yev[0]), (s_v, m_yev[1]),
                                           (s_a, m_exp), (s_w, mwp)]
                                          if (oc == 0 and f == 0) else []),
                                   inc=(s_pe if (oc == 3 and f == NFT - 1)
                                        else None),
                                   tag=f"p4.pj{qt}.{oc}.{f}")
                    m_proj = p.mark(s_pe)
                    for i in range(4):
                        p.emit("vector", _fn("bn_stats", vec(V_BNST + 6 * i, 6),
                                             psE[:, 512 * i:512 * (i + 1)]),
                               waits=([(s_a, m_m2cp), (s_pe, m_proj)]
                                      if i == 0 else []),
                               tag=f"p4.zb{qt}.{i}")
                    gap()
                    p.emit("vector", _fn("bn_aggr", vec(V_STAT, 2),
                                         vec(V_BNST, 24)),
                           inc=s_v, tag=f"p4.za{qt}")
                    mzs = p.mark(s_v)
                    p.emit("scalar", _fn("activation", vec(V_LNT),
                                         vec(V_STAT + 1), AF.Ln,
                                         bias=epsb[:, 0:1], scale=1.0),
                           waits=[(s_v, mzs)], tag=f"p4.zl{qt}")
                    p.emit("scalar", _fn("activation", vec(V_RSTD), vec(V_LNT),
                                         AF.Exp, bias=0.0, scale=-0.5),
                           inc=s_a, tag=f"p4.zr{qt}")
                    mza = p.mark(s_a)
                    p.emit("vector", _fn("tensor_scalar", psO[:], psE[:],
                                         vec(V_STAT), vec(V_RSTD),
                                         ALU.subtract, ALU.mult),
                           waits=[(s_a, mza), (s_pe, m_pv)], inc=s_v,
                           tag=f"p4.zn{qt}")
                    m_zev = p.mark(s_v)
                    p.emit("vector", _fn("tensor_tensor", x2b[:], psO[:],
                                         xt4[:], ALU.add),
                           waits=[(s_w, mx4)], inc=s_v, tag=f"p4.x2{qt}")
                    m_x2 = p.mark(s_v)
                    for i in range(4):
                        p.emit("vector", _fn("bn_stats", vec(V_BNST + 6 * i, 6),
                                             x2b[:, 512 * i:512 * (i + 1)]),
                               tag=f"p4.b2{qt}.{i}")
                    gap()
                    p.emit("vector", _fn("bn_aggr", vec(V_STAT, 2),
                                         vec(V_BNST, 24)),
                           inc=s_v, tag=f"p4.a2{qt}")
                    m2s = p.mark(s_v)
                    p.emit("scalar", _fn("activation", vec(V_LNT),
                                         vec(V_STAT + 1), AF.Ln,
                                         bias=epsb[:, 0:1], scale=1.0),
                           waits=[(s_v, m2s)], tag=f"p4.l2{qt}")
                    p.emit("scalar", _fn("activation", vec(V_RSTD), vec(V_LNT),
                                         AF.Exp, bias=0.0, scale=-0.5),
                           tag=f"p4.r2{qt}")
                    p.emit("scalar", _fn("activation", vec(V_SD2 + qt),
                                         vec(V_LNT), AF.Exp, bias=0.0,
                                         scale=0.5),
                           tag=f"p4.s2{qt}")
                    p.emit("scalar", _fn("copy", vec(V_M2 + qt), vec(V_STAT)),
                           inc=s_a, tag=f"p4.m2{qt}")
                    m_m2cp = p.mark(s_a)
                    p.emit("vector", _fn("tensor_scalar", xn2tm[:], x2b[:],
                                         vec(V_STAT), vec(V_RSTD),
                                         ALU.subtract, ALU.mult),
                           waits=[(s_a, m_m2cp), (s_pe, m_xnt)], inc=s_v,
                           tag=f"p4.xn{qt}")
                    mxn = p.mark(s_v)
                    pending_x2 = (qt, mxn)
                flush_x2()

        if phases >= 5:
            preload_marks = {}
            for ps_i in range(4):
                sl = [4 * ps_i + j for j in range(4)]
                p.phase()
                with ExitStack() as ph:
                    h32 = [ph.enter_context(
                        nc.sbuf_tensor(f"h32{b_}_{ps_i}", [128, 2048], F32))
                        for b_ in range(2)]
                    sg = ph.enter_context(
                        nc.sbuf_tensor(f"sg_{ps_i}", [128, 2048], F32))
                    hsc4 = ph.enter_context(
                        nc.sbuf_tensor(f"hsc4_{ps_i}", [128, 4096], BF16))
                    aabn = ph.enter_context(
                        nc.sbuf_tensor(f"aabn_{ps_i}", [128, 1024], F32))
                    aab = aabn[:, 0:512]
                    nb = [aabn[:, 512 + i * 128:512 + (i + 1) * 128]
                          for i in range(4)]

                    def hscv(ic_, pt_):
                        o = (ic_ % 2) * 2048 + pt_ * 512
                        return hsc4[:, o:o + 512]

                    def half(s):
                        if s < 4:
                            return rkv[:, s * 4096:(s + 1) * 4096]
                        return wexs[:]

                    def wtile(L, f):
                        h = half((2 * L + (1 if f >= 8 else 0)) % 5)
                        return h[:, (f % 8) * 512:((f % 8) + 1) * 512]

                    def dma_half(L, k, waits):
                        ic, j = divmod(L, 2)
                        dst = half((2 * L + k) % 5)[:].rearrange(
                            "q (f i) -> q f i", i=512)
                        srcw = wg_in if j == 0 else wu_in
                        s_ = srcw[k * 1024:(k + 1) * 1024,
                                  ic * 512:(ic + 1) * 512]
                        s_ = s_.rearrange("(f q) i -> q f i", q=128)
                        p.emit("gpsimd", _fn("dma_start", dst, s_),
                               waits=waits, inc=s_w, inc_amt=16,
                               tag=f"f1.w{ps_i}.{L}.{k}")
                        return p.mark(s_w)

                    mld = {}
                    if ps_i in preload_marks:
                        mld[0], mld[1] = preload_marks[ps_i]
                        dma_half(2, 0, [])
                    else:
                        dma_half(0, 0, [])
                        mld[0] = dma_half(0, 1, [])
                        dma_half(1, 0, [])
                        mld[1] = dma_half(1, 1, [])
                        dma_half(2, 0, [])
                    m_silu = 0
                    m_hf = 0
                    m_hf_prev = 0
                    m_sp_par = [0, 0]
                    m_dj = 0
                    m_hst = 0
                    m_hsev = 0
                    m_hsev2 = [0, 0]
                    pend_hst = []
                    pend_sp = None

                    def emit_hst(ic, all_=False):
                        nonlocal m_hst, m_hsev
                        todo = [e for e in pend_hst
                                if all_ or e[0] <= ic - 2]
                        pend_hst[:] = [e for e in pend_hst
                                       if not (all_ or e[0] <= ic - 2)]
                        for (ric, pt_, msp) in todo:
                            par = pt_ % 2
                            bk = psO_bf0[:, par * 1024:(par + 1) * 1024]
                            for k in range(4):
                                p.emit("tensor",
                                       _fn("transpose",
                                           bk[:, k * 128:(k + 1) * 128],
                                           hscv(ric, pt_)[:, k * 128:
                                                          (k + 1) * 128],
                                           ident[:]),
                                       waits=([(s_v, msp),
                                               (s_v, m_hsev2[par])]
                                              if k == 0 else []),
                                       inc=(s_pe if k == 3 else None),
                                       tag=f"f1.hT{ps_i}.{ric}.{pt_}.{k}")
                            m_hst = p.mark(s_pe)
                            rw_r = rw[:].rearrange("q (it t) -> q it t", t=512)
                            dstv = rw_r[:, ric * 4:(ric + 1) * 4,
                                        pt_ * 128:(pt_ + 1) * 128]
                            srcv = bk[:, 0:512].rearrange(
                                "q (k t) -> q k t", t=128)
                            p.emit("vector", _fn("tensor_copy", dstv, srcv),
                                   waits=[(s_pe, m_hst)], inc=s_v,
                                   tag=f"f1.he{ps_i}.{ric}.{pt_}")
                            m_hsev2[par] = p.mark(s_v)
                            m_hsev = m_hsev2[par]

                    def emit_sparsify():
                        nonlocal pend_sp, m_dj, m_sp_par
                        if pend_sp is None:
                            return
                        ric, mhf = pend_sp
                        pend_sp = None

                        def jv(t_, j):
                            v3 = t_.rearrange("q (g j) -> q g j", j=4)
                            return v3[:, :, j]

                        for pt_ in range(4):
                            hv = h32[ric % 2][:, pt_ * 512:(pt_ + 1) * 512]
                            p.emit("scalar", _fn("activation", aab, hv, AF.Abs),
                                   waits=[(s_v, mhf), (s_v, m_dj)],
                                   inc=s_a, tag=f"f1.ab{ps_i}.{ric}.{pt_}")
                            m_ab = p.mark(s_a)
                            gap()
                            p.emit("vector", _fn("tensor_tensor", nb[0][:],
                                                 jv(aab, 0), jv(aab, 1),
                                                 ALU.min),
                                   waits=[(s_a, m_ab)],
                                   tag=f"f1.n1{ps_i}.{ric}.{pt_}")
                            p.emit("vector", _fn("tensor_tensor", nb[1][:],
                                                 jv(aab, 0), jv(aab, 1),
                                                 ALU.max),
                                   tag=f"f1.n2{ps_i}.{ric}.{pt_}")
                            p.emit("vector", _fn("tensor_tensor", nb[2][:],
                                                 jv(aab, 2), jv(aab, 3),
                                                 ALU.min),
                                   tag=f"f1.n3{ps_i}.{ric}.{pt_}")
                            p.emit("vector", _fn("tensor_tensor", nb[3][:],
                                                 jv(aab, 2), jv(aab, 3),
                                                 ALU.max),
                                   tag=f"f1.n4{ps_i}.{ric}.{pt_}")
                            p.emit("vector", _fn("tensor_tensor", nb[0][:],
                                                 nb[0][:], nb[2][:], ALU.max),
                                   tag=f"f1.n5{ps_i}.{ric}.{pt_}")
                            p.emit("vector", _fn("tensor_tensor", nb[1][:],
                                                 nb[1][:], nb[3][:], ALU.min),
                                   tag=f"f1.n6{ps_i}.{ric}.{pt_}")
                            gap()
                            p.emit("vector", _fn("tensor_tensor", nb[0][:],
                                                 nb[0][:], nb[1][:], ALU.min),
                                   tag=f"f1.n7{ps_i}.{ric}.{pt_}")
                            gap()
                            for j in range(4):
                                p.emit("vector",
                                       _fn("tensor_tensor", jv(aab, j),
                                           jv(aab, j), nb[0][:], ALU.subtract),
                                       tag=f"f1.d{ps_i}.{ric}.{pt_}.{j}")
                            for j in range(4):
                                p.emit("vector",
                                       _fn("scalar_tensor_tensor",
                                           jv(hscv(ric, pt_)[:], j),
                                           jv(aab, j), 0.0,
                                           jv(hv, j), ALU.is_gt, ALU.mult),
                                       waits=([(s_pe, m_hst)] if j == 0
                                              else []),
                                       inc=(s_v if j == 3 else None),
                                       tag=f"f1.m{ps_i}.{ric}.{pt_}.{j}")
                            m_sp = p.mark(s_v)
                            m_dj = m_sp
                            pend_hst.append((ric, pt_, m_sp))
                        m_sp_par[ric % 2] = m_dj

                    for ic in range(16):
                        Lg, Lu = 2 * ic, 2 * ic + 1
                        for pl in range(4):
                            for f in range(NFT):
                                w0 = ([(s_w, mld.get(Lg, 0)), (s_a, m_silu)]
                                      if (pl == 0 and f == 0) else [])
                                p.emit("tensor",
                                       _fn("matmul",
                                           psE[:, pl * 512:(pl + 1) * 512],
                                           slot(sl[pl])[:, f * 128:(f + 1) * 128],
                                           wtile(Lg, f),
                                           start=(f == 0), stop=(f == NFT - 1)),
                                       waits=w0,
                                       inc=(s_pe if (pl == 3 and f == NFT - 1)
                                            else None),
                                       tag=f"f1.g{ps_i}.{ic}.{pl}.{f}")
                        m_gd = p.mark(s_pe)
                        if Lg + 2 < 32:
                            dma_half(Lg + 2, 1, [(s_pe, m_gd)])
                            mld[Lg + 2] = p.mark(s_w)
                        if Lg + 3 < 32:
                            dma_half(Lg + 3, 0, [])
                        emit_hst(ic)
                        for pl in range(4):
                            for f in range(NFT):
                                p.emit("tensor",
                                       _fn("matmul",
                                           psO[:, pl * 512:(pl + 1) * 512],
                                           slot(sl[pl])[:, f * 128:(f + 1) * 128],
                                           wtile(Lu, f),
                                           start=(f == 0), stop=(f == NFT - 1)),
                                       waits=([(s_w, mld.get(Lu, 0)),
                                               (s_v, m_hsev), (s_v, m_hf)]
                                              if (pl == 0 and f == 0) else []),
                                       inc=(s_pe if (pl == 3 and f == NFT - 1)
                                            else None),
                                       tag=f"f1.u{ps_i}.{ic}.{pl}.{f}")
                        m_umm = p.mark(s_pe)
                        if Lu + 2 < 32:
                            dma_half(Lu + 2, 1, [(s_pe, m_umm)])
                            mld[Lu + 2] = p.mark(s_w)
                        if Lu + 3 < 32:
                            dma_half(Lu + 3, 0, [])
                        p.emit("scalar", _fn("activation", sg[:],
                                             psE[:, 0:2048], AF.Silu),
                               waits=[(s_pe, m_gd), (s_v, m_hf)],
                               inc=s_a, tag=f"f1.si{ps_i}.{ic}")
                        m_silu = p.mark(s_a)
                        hd_ = h32[ic % 2][:].rearrange("q (pt e) -> q pt e",
                                                       e=512)
                        sg3 = sg[:].rearrange("q (pt e) -> q pt e", e=512)
                        pu3 = psO[:].rearrange("q (pt e) -> q pt e", e=512)
                        p.emit("vector", _fn("tensor_tensor", hd_, sg3, pu3,
                                             ALU.mult),
                               waits=[(s_a, m_silu), (s_pe, m_umm),
                                      (s_v, m_sp_par[ic % 2])],
                               inc=s_v, tag=f"f1.h{ps_i}.{ic}")
                        m_hf_prev = m_hf
                        m_hf = p.mark(s_v)
                        emit_sparsify()
                        pend_sp = (ic, m_hf)
                    emit_sparsify()
                    emit_hst(99, all_=True)
                    if ps_i + 1 < 4:
                        m_lastu = p.mark(s_pe)
                        sl2 = [4 * (ps_i + 1) + j for j in range(4)]

                        def dma_half2(L, k, waits):
                            ic_, j_ = divmod(L, 2)
                            dst = half((2 * L + k) % 5)[:].rearrange(
                                "q (f i) -> q f i", i=512)
                            srcw = wg_in if j_ == 0 else wu_in
                            s_ = srcw[k * 1024:(k + 1) * 1024,
                                      ic_ * 512:(ic_ + 1) * 512]
                            s_ = s_.rearrange("(f q) i -> q f i", q=128)
                            p.emit("gpsimd", _fn("dma_start", dst, s_),
                                   waits=waits, inc=s_w, inc_amt=16,
                                   tag=f"f1.pw{ps_i + 1}.{L}.{k}")
                            return p.mark(s_w)

                        dma_half2(0, 0, [(s_pe, m_lastu)])
                        mA = dma_half2(0, 1, [])
                        dma_half2(1, 0, [])
                        mB = dma_half2(1, 1, [])
                        preload_marks[ps_i + 1] = (mA, mB)

                p.phase()
                with ExitStack() as ph:
                    fbf = ph.enter_context(
                        nc.sbuf_tensor(f"fbf_{ps_i}", [128, 4 * C], BF16))
                    wd0 = ph.enter_context(
                        nc.sbuf_tensor(f"wd0_{ps_i}", [128, 4096], BF16))
                    wd1 = ph.enter_context(
                        nc.sbuf_tensor(f"wd1_{ps_i}", [128, 4096], BF16))
                    fo1 = ph.enter_context(
                        nc.sbuf_tensor(f"fo1_{ps_i}", [128, 512], F32))
                    fo3 = ph.enter_context(
                        nc.sbuf_tensor(f"fo3_{ps_i}", [128, 512], F32))
                    wdsl = [wd0, wd1, wexs]

                    def hsf_t(it, pl):
                        return rw[:, it * 512 + pl * 128: it * 512 + (pl + 1) * 128]

                    def dma_wd(os_, ic4, waits):
                        a = (os_ * 8 + ic4) % 3
                        dst = wdsl[a][:].rearrange("q (it o) -> q it o", o=512)
                        src = wd_in[ic4 * 1024:(ic4 + 1) * 1024,
                                    os_ * 512:(os_ + 1) * 512]
                        src = src.rearrange("(it q) o -> q it o", q=128)
                        p.emit("gpsimd", _fn("dma_start", dst, src),
                               waits=waits, inc=s_w, inc_amt=16,
                               tag=f"f2.wd{ps_i}.{os_}.{ic4}")
                        return p.mark(s_w)

                    m_fev2 = [0, 0]
                    m_wmm = {}
                    mw2carry = {}
                    for os_ in range(4):
                        mw2 = {}
                        if os_ == 0:
                            mw2[0] = dma_wd(os_, 0, [])
                            mw2[1] = dma_wd(os_, 1, [])
                            mw2[2] = dma_wd(os_, 2, [])
                        else:
                            for k in list(mw2carry):
                                mw2[k - 8] = mw2carry[k]
                            mw2carry.clear()
                        acc = psE if os_ % 2 == 0 else psO
                        for ic4 in range(8):
                            a = (os_ * 8 + ic4) % 3
                            for pl in range(4):
                                for it in range(8):
                                    itg = ic4 * 8 + it
                                    p.emit("tensor",
                                           _fn("matmul",
                                               acc[:, pl * 512:(pl + 1) * 512],
                                               hsf_t(itg, pl),
                                               wdsl[a][:, it * 512:(it + 1) * 512],
                                               start=(ic4 == 0 and it == 0),
                                               stop=(ic4 == 7 and it == 7)),
                                           waits=([(s_w, mw2[ic4]),
                                                   (s_v, m_fev2[os_ % 2])]
                                                  if (pl == 0 and it == 0)
                                                  else []),
                                           inc=(s_pe if (pl == 3 and it == 7)
                                                else None),
                                           tag=f"f2.m{ps_i}.{os_}.{ic4}.{pl}")
                            m_wmm[(os_, ic4)] = p.mark(s_pe)
                            nxt = ic4 + 3
                            if os_ * 8 + nxt < 32:
                                m_ = dma_wd(os_ + nxt // 8, nxt % 8,
                                            [(s_pe, m_wmm[(os_, ic4)])])
                                if nxt < 8:
                                    mw2[nxt] = m_
                                else:
                                    mw2carry[nxt] = m_
                        mlast = m_wmm[(os_, 7)]
                        for pl in range(4):
                            p.emit("vector",
                                   _fn("bn_stats",
                                       vec(V_BNF + pl * 24 + os_ * 6, 6),
                                       acc[:, pl * 512:(pl + 1) * 512]),
                                   waits=[(s_pe, mlast)] if pl == 0 else [],
                                   tag=f"f2.bn{ps_i}.{os_}.{pl}")
                            p.emit("vector",
                                   _fn("tensor_copy",
                                       fbf[:, pl * C + os_ * 512:
                                           pl * C + (os_ + 1) * 512],
                                       acc[:, pl * 512:(pl + 1) * 512]),
                                   inc=(s_v if pl == 3 else None),
                                   tag=f"f2.fe{ps_i}.{os_}.{pl}")
                        m_fev2[os_ % 2] = p.mark(s_v)

                    if debug and ps_i == 3:
                        p.emit("sync", _fn("dma_start", dbg_fbf, fbf[:]),
                               waits=[(s_v, m_fev2[0]), (s_v, m_fev2[1])],
                               inc=s_out, inc_amt=16, tag="dbg.fbf")
                    m_store = p.mark(s_out)
                    for pl in range(4):
                        gap()
                        p.emit("vector", _fn("bn_aggr", vec(V_M3 + pl * 2, 2),
                                             vec(V_BNF + pl * 24, 24)),
                               waits=([(s_v, m_fev2[0]), (s_v, m_fev2[1])]
                                      if pl == 0 else []),
                               inc=s_v, tag=f"f2.ag{ps_i}.{pl}")
                    mag = p.mark(s_v)
                    for pl in range(4):
                        p.emit("scalar", _fn("activation", vec(V_LNT),
                                             vec(V_M3 + pl * 2 + 1), AF.Ln,
                                             bias=epsb[:, 0:1], scale=1.0),
                               waits=[(s_v, mag)] if pl == 0 else [],
                               tag=f"f2.l{ps_i}.{pl}")
                        p.emit("scalar", _fn("activation", vec(V_RSTD3 + pl),
                                             vec(V_LNT), AF.Exp, bias=0.0,
                                             scale=-0.5),
                               inc=s_a, tag=f"f2.r{ps_i}.{pl}")
                    mra = p.mark(s_a)
                    for pl in range(4):
                        p.emit("vector", _fn("tensor_tensor",
                                             vec(V_CMX + 4 + pl),
                                             vec(V_M3 + pl * 2),
                                             vec(V_RSTD3 + pl), ALU.mult),
                               waits=[(s_a, mra)] if pl == 0 else [],
                               tag=f"f2.c1{ps_i}.{pl}")
                    gap()
                    for pl in range(4):
                        p.emit("vector", _fn("tensor_tensor",
                                             vec(V_CMX + 4 + pl),
                                             vec(V_M2 + sl[pl]),
                                             vec(V_CMX + 4 + pl), ALU.subtract),
                               inc=(s_v if pl == 3 else None),
                               tag=f"f2.c2{ps_i}.{pl}")
                    mc2 = p.mark(s_v)
                    m_ps_free = [0, 0]
                    for pl in range(4):
                        par = pl % 2
                        bk = (psE if par == 0 else psO)[:, 0:1024].bitcast(BF16)
                        for f in range(NFT):
                            p.emit("tensor",
                                   _fn("transpose",
                                       bk[:, f * 128:(f + 1) * 128],
                                       slot(sl[pl])[:, f * 128:(f + 1) * 128],
                                       ident[:]),
                                   waits=([(s_v, m_ps_free[par]),
                                           (s_v, mc2)] if f == 0 else []),
                                   inc=(s_pe if f == NFT - 1 else None),
                                   tag=f"f2.xT{ps_i}.{pl}.{f}")
                        m_xnt5 = p.mark(s_pe)
                        for os_ in range(4):
                            cs = slice(os_ * 512, (os_ + 1) * 512)
                            p.emit("vector",
                                   _fn("tensor_scalar", fo1[:], bk[:, cs],
                                       vec(V_SD2 + sl[pl]),
                                       vec(V_CMX + 4 + pl),
                                       ALU.mult, ALU.add),
                                   waits=([(s_pe, m_xnt5), (s_out, m_store)]
                                          if os_ == 0 else
                                          [(s_out, m_store)]),
                                   tag=f"f2.o1{ps_i}.{pl}.{os_}")
                            p.emit("vector",
                                   _fn("scalar_tensor_tensor", fo3[:],
                                       fbf[:, pl * C + os_ * 512:
                                           pl * C + (os_ + 1) * 512],
                                       vec(V_RSTD3 + pl), fo1[:],
                                       ALU.mult, ALU.add),
                                   inc=s_v, tag=f"f2.o3{ps_i}.{pl}.{os_}")
                            mo = p.mark(s_v)
                            r0 = sl[pl] * 128
                            p.emit("sync", _fn("dma_start",
                                               out[r0:r0 + 128, cs], fo3[:]),
                                   waits=[(s_v, mo)], inc=s_out, inc_amt=16,
                                   tag=f"f2.st{ps_i}.{pl}.{os_}")
                            m_store = p.mark(s_out)
                        m_ps_free[par] = p.mark(s_v)

            p.phase()
            p.waitop("sync", [(s_out, p.mark(s_out))], tag="final")

        if debug:
            p.phase()
            nst = 0
            p.emit("sync", _fn("dma_start", dbg_rq, rq[:]), inc=s_out,
                   inc_amt=16, tag="dbg.rq")
            p.emit("sync", _fn("dma_start", dbg_rkv, rkv[:]), inc=s_out,
                   inc_amt=16, tag="dbg.rkv")
            p.emit("sync", _fn("dma_start", dbg_rw, rw[:]), inc=s_out,
                   inc_amt=16, tag="dbg.rw")
            p.emit("sync", _fn("dma_start", dbg_vecs, vecs[:]), inc=s_out,
                   inc_amt=16, tag="dbg.vecs")
            p.waitop("sync", [(s_out, p.mark(s_out))], tag="dbg.final")

        p.simulate()

        with nc.Block() as block:
            @block.sync
            def _(e):
                for f in p.ops["sync"]:
                    f(e)

            @block.gpsimd
            def _(e):
                for f in p.ops["gpsimd"]:
                    f(e)

            @block.tensor
            def _(e):
                for f in p.ops["tensor"]:
                    f(e)

            @block.vector
            def _(e):
                for f in p.ops["vector"]:
                    f(e)

            @block.scalar
            def _(e):
                for f in p.ops["scalar"]:
                    f(e)

    return nc


def make_consts():
    ident = np.eye(128, dtype=np.float32).astype(ml_dtypes.bfloat16)
    mask = np.where(np.arange(128)[None, :] <= np.arange(128)[:, None],
                    np.float32(0.0), np.float32(NEG)).astype(ml_dtypes.bfloat16)
    inv = (10000.0 ** (-(np.arange(0, HD, 2, dtype=np.float64) / HD)))
    th = np.arange(T, dtype=np.float64)[None, :] * inv[:, None]
    cos = np.cos(th).astype(np.float32)
    sin = np.sin(th).astype(np.float32)
    cosf = np.concatenate([cos, cos], axis=0).astype(ml_dtypes.bfloat16)
    sinf = np.concatenate([sin, sin], axis=0).astype(ml_dtypes.bfloat16)
    cosf2 = (np.concatenate([cos, cos], axis=0) / math.sqrt(HD)).astype(
        ml_dtypes.bfloat16)
    sinf2 = (np.concatenate([sin, sin], axis=0) / math.sqrt(HD)).astype(
        ml_dtypes.bfloat16)
    rot = np.zeros((128, 128), np.float32)
    for d in range(64):
        rot[d + 64, d] = -1.0
        rot[d, d + 64] = 1.0
    rot = rot.astype(ml_dtypes.bfloat16)
    onesbv = np.ones((128, 8), dtype=ml_dtypes.bfloat16)
    onesfv = np.ones((128, 128), dtype=np.float32)
    return ident, mask, cosf, sinf, rot, cosf2, sinf2, onesbv, onesfv


def kernel(L0_attn_x, L0_attn_Wq, L0_attn_Wk, L0_attn_Wv, L0_attn_Wproj,
           L0_ffn_w_gate, L0_ffn_w_up, L0_ffn_w_down,
           _phases=5, _debug=False):
    x = np.ascontiguousarray(np.asarray(L0_attn_x, dtype=np.float32))
    wq = np.ascontiguousarray(np.asarray(L0_attn_Wq, dtype=np.float32))
    wk = np.ascontiguousarray(np.asarray(L0_attn_Wk, dtype=np.float32))
    wv = np.ascontiguousarray(np.asarray(L0_attn_Wv, dtype=np.float32))
    wp = np.ascontiguousarray(np.asarray(L0_attn_Wproj, dtype=np.float32))
    wg = np.ascontiguousarray(np.asarray(L0_ffn_w_gate, dtype=np.float32))
    wu = np.ascontiguousarray(np.asarray(L0_ffn_w_up, dtype=np.float32))
    wd = np.ascontiguousarray(np.asarray(L0_ffn_w_down, dtype=np.float32))
    ident, mask, cosf, sinf, rot, cosf2, sinf2, onesbv, onesfv = make_consts()
    epsv = np.full((128, 8), EPS, dtype=np.float32)

    nc = build_nc(phases=_phases, debug=_debug)
    in_maps = []
    for c in range(8):
        in_maps.append({
            "x": np.ascontiguousarray(x[c]),
            "wq": wq, "wk": wk, "wv": wv, "wp": wp,
            "wg": wg, "wu": wu, "wd": wd,
            "identw": ident, "maskw": mask, "cosw": cosf, "sinw": sinf,
            "epsw": epsv, "rotw": rot, "cosw2": cosf2, "sinw2": sinf2,
            "onesbw": onesbv, "onesfw": onesfv,
        })
    res = run_bass_kernel_spmd(nc, in_maps, core_ids=list(range(8)))
    outs = np.stack([np.asarray(r["out"], dtype=np.float32)
                     for r in res.results], axis=0)
    if _debug:
        dbgs = [{k: np.asarray(v) for k, v in r.items() if k.startswith("dbg")}
                for r in res.results]
        return outs, dbgs
    return outs


# version 16
# speedup vs baseline: 1.0695x; 1.0017x over previous
import math
from collections import defaultdict
from contextlib import ExitStack

import numpy as np
import ml_dtypes

import concourse.bass as bass
import concourse.mybir as mybir
from concourse.bass_utils import run_bass_kernel_spmd

F32 = mybir.dt.float32
BF16 = mybir.dt.bfloat16
AF = mybir.ActivationFunctionType
ALU = mybir.AluOpType
AX = mybir.AxisListType

T = 2048
C = 2048
NH = 16
NKV = 4
HD = 128
II = 8192
NPT = T // 128
NFT = C // 128
EPS = 1e-5
SCALE = 1.0 / math.sqrt(HD)
NEG = -1.0e30

ENGS = ("sync", "gpsimd", "tensor", "vector", "scalar")


class Prog:

    def __init__(self, sems):
        self.ops = {e: [] for e in ENGS}
        self.cnt = defaultdict(int)
        self.meta = []
        self.sems = list(sems)
        self.bar = {}
        self.bar_done = set(ENGS)
        self.sem_by_id = {id(s): s for s in sems}

    def mark(self, sem):
        return self.cnt[id(sem)]

    def phase(self):
        self.bar = {id(s): self.cnt[id(s)] for s in self.sems}
        self.bar_done = set()

    def emit(self, eng, fn, waits=(), inc=None, inc_amt=1, tag=""):
        waits = [(s, v) for (s, v) in waits if v > 0]
        if eng not in self.bar_done:
            self.bar_done.add(eng)
            waits = [(self.sem_by_id[i], v) for i, v in self.bar.items()
                     if v > 0] + waits
        if inc is not None:
            self.cnt[id(inc)] += inc_amt
        self.meta.append((eng, [(id(s), v) for s, v in waits],
                          [(id(inc), inc_amt)] if inc is not None else [], tag))

        def run(e, fn=fn, waits=waits, inc=inc, amt=inc_amt):
            for s, v in waits:
                e.wait_ge(s, v)
            ins = fn(e)
            if inc is not None:
                ins.then_inc(inc, amt)
        self.ops[eng].append(run)

    def waitop(self, eng, waits, tag=""):
        self.emit(eng, lambda e: None, waits=waits, tag=tag)

    def simulate(self):
        queues = {e: [] for e in ENGS}
        for (eng, waits, incs, tag) in self.meta:
            queues[eng].append((waits, incs, tag))
        heads = {e: 0 for e in ENGS}
        cnt = defaultdict(int)
        progress = True
        while progress:
            progress = False
            for e in ENGS:
                while heads[e] < len(queues[e]):
                    waits, incs, tag = queues[e][heads[e]]
                    if all(cnt[s] >= v for s, v in waits):
                        for s, a in incs:
                            cnt[s] += a
                        heads[e] += 1
                        progress = True
                    else:
                        break
        if any(heads[e] < len(queues[e]) for e in ENGS):
            lines = []
            for e in ENGS:
                if heads[e] < len(queues[e]):
                    waits, incs, tag = queues[e][heads[e]]
                    missing = [(str(s), v, cnt[s]) for s, v in waits
                               if cnt[s] < v]
                    lines.append(f"{e}[{heads[e]}/{len(queues[e])}] tag={tag} "
                                 f"missing={missing}")
            raise RuntimeError("build-time DEADLOCK:\n" + "\n".join(lines))


def _fn(method, *args, **kwargs):
    return lambda e: getattr(e, method)(*args, **kwargs)


def build_nc(phases=5, debug=False):
    nc = bass.Bass("TRN2", target_bir_lowering=False)

    x_in = nc.dram_tensor("x", [T, C], F32, kind="ExternalInput").ap()
    wq_in = nc.dram_tensor("wq", [C, C], F32, kind="ExternalInput").ap()
    wk_in = nc.dram_tensor("wk", [C, NKV * HD], F32, kind="ExternalInput").ap()
    wv_in = nc.dram_tensor("wv", [C, NKV * HD], F32, kind="ExternalInput").ap()
    wp_in = nc.dram_tensor("wp", [C, C], F32, kind="ExternalInput").ap()
    wg_in = nc.dram_tensor("wg", [C, II], F32, kind="ExternalInput").ap()
    wu_in = nc.dram_tensor("wu", [C, II], F32, kind="ExternalInput").ap()
    wd_in = nc.dram_tensor("wd", [II, C], F32, kind="ExternalInput").ap()
    identw = nc.dram_tensor("identw", [128, 128], BF16, kind="ExternalInput").ap()
    maskw = nc.dram_tensor("maskw", [128, 128], BF16, kind="ExternalInput").ap()
    rotw = nc.dram_tensor("rotw", [128, 128], BF16, kind="ExternalInput").ap()
    onesbw = nc.dram_tensor("onesbw", [128, 8], BF16, kind="ExternalInput").ap()
    onesfw = nc.dram_tensor("onesfw", [128, 128], F32, kind="ExternalInput").ap()
    cosw = nc.dram_tensor("cosw", [128, T], BF16, kind="ExternalInput").ap()
    cosw2 = nc.dram_tensor("cosw2", [128, T], BF16, kind="ExternalInput").ap()
    sinw2 = nc.dram_tensor("sinw2", [128, T], BF16, kind="ExternalInput").ap()
    epsw = nc.dram_tensor("epsw", [128, 8], F32, kind="ExternalInput").ap()
    sinw = nc.dram_tensor("sinw", [128, T], BF16, kind="ExternalInput").ap()

    out = nc.dram_tensor("out", [T, C], F32, kind="ExternalOutput").ap()
    dbg = {}
    if debug:
        dbg_rq = nc.dram_tensor("dbg_rq", [128, NPT * C], BF16,
                                kind="ExternalOutput").ap()
        dbg_rkv = nc.dram_tensor("dbg_rkv", [128, 16384], BF16,
                                 kind="ExternalOutput").ap()
        dbg_vecs = nc.dram_tensor("dbg_vecs", [128, 256], F32,
                                  kind="ExternalOutput").ap()
        dbg_rw = nc.dram_tensor("dbg_rw", [128, 16 * C], BF16,
                                kind="ExternalOutput").ap()
        dbg_fbf = nc.dram_tensor("dbg_fbf", [128, 4 * C], BF16,
                                 kind="ExternalOutput").ap()

    with ExitStack() as top:
        E_ = top.enter_context
        rq = E_(nc.sbuf_tensor("rq", [128, NPT * C], BF16))
        rkv = E_(nc.sbuf_tensor("rkv", [128, 16384], BF16))
        rw = E_(nc.sbuf_tensor("rw", [128, 16 * C], BF16))
        vecs = E_(nc.sbuf_tensor("vecs", [128, 256], F32))
        ident = E_(nc.sbuf_tensor("ident", [128, 128], BF16))
        dmask = E_(nc.sbuf_tensor("dmask", [128, 128], BF16))
        rmat = E_(nc.sbuf_tensor("rmat", [128, 128], BF16))
        onesb = E_(nc.sbuf_tensor("onesb", [128, 8], BF16))
        onesf = E_(nc.sbuf_tensor("onesf", [128, 128], F32))
        wexs = E_(nc.sbuf_tensor("wexs", [128, 4096], BF16))
        epsb = E_(nc.sbuf_tensor("epsb", [128, 8], F32))
        psE = E_(nc.psum_tensor("psE", [128, 2048], F32))
        psO = E_(nc.psum_tensor("psO", [128, 2048], F32))

        s_x = E_(nc.semaphore("s_x"))
        s_w = E_(nc.semaphore("s_w"))
        s_pe = E_(nc.semaphore("s_pe"))
        s_v = E_(nc.semaphore("s_v"))
        s_a = E_(nc.semaphore("s_a"))
        s_out = E_(nc.semaphore("s_out"))
        p = Prog([s_x, s_w, s_pe, s_v, s_a, s_out])

        V_BNST, V_STAT = 0, 24
        V_NEGM, V_BIAS, V_SSUM, V_RSUM = 26, 27, 28, 29
        V_LNT, V_RSTD = 30, 32
        V_EPS = 34
        V_CMX = 36
        V_B16 = 80
        V_KN = 204
        V_QN = 212
        V_NTMP = 250
        V_M2, V_SD2 = 48, 64
        V_BNF = 96
        V_M3 = 192
        V_RSTD3 = 200

        def vec(col, n=1):
            return vecs[:, col:col + n]

        def gap(eng="vector"):
            p.emit(eng, _fn("tensor_copy", vecs[:, 44:48], epsb[:, 0:4]),
                   tag="gap")

        def slot(s):
            return rq[:, s * C:(s + 1) * C]

        def K_kvh(k):
            return rkv[:, k * T:(k + 1) * T]

        def V_pt(pt):
            return rkv[:, 8192 + pt * 512: 8192 + (pt + 1) * 512]

        psE_bf = psE[:, 0:1024].bitcast(BF16)
        psO_bf = psO[:, 1024:2048].bitcast(BF16)
        psO_bf0 = psO[:, 0:1024].bitcast(BF16)

        p.emit("sync", _fn("dma_start", ident[:], identw), inc=s_x, inc_amt=16,
               tag="c.id")
        p.emit("sync", _fn("dma_start", dmask[:], maskw), inc=s_x, inc_amt=16,
               tag="c.mask")
        p.emit("sync", _fn("dma_start", rmat[:], rotw), inc=s_x, inc_amt=16,
               tag="c.rot")
        p.emit("sync", _fn("dma_start", onesb[:], onesbw), inc=s_x,
               inc_amt=16, tag="c.ob")
        p.emit("sync", _fn("dma_start", onesf[:], onesfw), inc=s_x,
               inc_amt=16, tag="c.of")
        p.emit("sync", _fn("dma_start", epsb[:], epsw), inc=s_x,
               inc_amt=16, tag="c.eps")

        p.phase()
        with ExitStack() as ph:
            xt = [ph.enter_context(nc.sbuf_tensor(f"p1x{i}", [128, C], F32))
                  for i in range(2)]
            consumed = [0, 0]
            for pt in range(NPT):
                b = pt % 2
                p.emit("sync", _fn("dma_start", xt[b][:],
                                   x_in[pt * 128:(pt + 1) * 128, :]),
                       waits=[(s_v, consumed[b])], inc=s_x, inc_amt=16,
                       tag=f"p1.ld{pt}")
                mx_ld = p.mark(s_x)
                for i in range(4):
                    p.emit("vector", _fn("bn_stats", vec(V_BNST + 6 * i, 6),
                                         xt[b][:, 512 * i:512 * (i + 1)]),
                           waits=[(s_x, mx_ld)] if i == 0 else [],
                           tag=f"p1.bn{pt}.{i}")
                gap()
                p.emit("vector", _fn("bn_aggr", vec(V_STAT, 2), vec(V_BNST, 24)),
                       inc=s_v, tag=f"p1.ag{pt}")
                mv = p.mark(s_v)
                p.emit("scalar", _fn("activation", vec(V_LNT), vec(V_STAT + 1),
                                     AF.Ln, bias=epsb[:, 0:1], scale=1.0),
                       waits=[(s_v, mv)], tag=f"p1.ln{pt}")
                p.emit("scalar", _fn("activation", vec(V_RSTD), vec(V_LNT),
                                     AF.Exp, bias=0.0, scale=-0.5),
                       inc=s_a, tag=f"p1.ex{pt}")
                ma = p.mark(s_a)
                p.emit("vector", _fn("tensor_scalar", slot(pt)[:], xt[b][:],
                                     vec(V_STAT), vec(V_RSTD),
                                     ALU.subtract, ALU.mult),
                       waits=[(s_a, ma)], inc=s_v, tag=f"p1.nm{pt}")
                consumed[b] = p.mark(s_v)

        def qkv_pass(kind, ph):
            xnf4 = ph.enter_context(
                nc.sbuf_tensor(f"xnf4{kind}", [128, NFT * 512], BF16))
            cosb = ph.enter_context(nc.sbuf_tensor(f"cosb{kind}", [128, 512], BF16))
            sinb = ph.enter_context(nc.sbuf_tensor(f"sinb{kind}", [128, 512], BF16))
            rt1 = ph.enter_context(nc.sbuf_tensor(f"rt1{kind}", [128, 512], F32))
            rt2 = ph.enter_context(nc.sbuf_tensor(f"rt2{kind}", [128, 512], F32))
            kraw = ph.enter_context(nc.sbuf_tensor(f"kraw{kind}", [128, 512], BF16))
            sqb = ph.enter_context(nc.sbuf_tensor(f"sqb{kind}", [128, 512], BF16))

            if kind == "kv":
                for f in range(NFT):
                    p.emit("gpsimd", _fn("dma_start",
                                         rw[:, f * 512:(f + 1) * 512],
                                         wk_in[f * 128:(f + 1) * 128, :]),
                           inc=s_w, inc_amt=16, tag=f"p2.wk{f}")
                for f in range(NFT):
                    p.emit("gpsimd", _fn("dma_start",
                                         rw[:, 8192 + f * 512:8192 + (f + 1) * 512],
                                         wv_in[f * 128:(f + 1) * 128, :]),
                           inc=s_w, inc_amt=16, tag=f"p2.wv{f}")
            else:
                for f in range(NFT):
                    p.emit("gpsimd", _fn("dma_start",
                                         rw[:, f * C:(f + 1) * C],
                                         wq_in[f * 128:(f + 1) * 128, :]),
                           inc=s_w, inc_amt=16, tag=f"p3.wq{f}")
            mw = p.mark(s_w)

            m_xnf_ev = 0
            m_rope = 0
            m_cons_E = 0
            m_mm_done = 0
            m_csfree = 0
            for g in range(4):
                csrc = cosw if kind == "kv" else cosw2
                ssrc = sinw if kind == "kv" else sinw2
                p.emit("sync", _fn("dma_start", cosb[:],
                                   csrc[:, g * 512:(g + 1) * 512]),
                       waits=[(s_v, m_csfree)], inc=s_x, inc_amt=16,
                       tag=f"{kind}.cos{g}")
                p.emit("sync", _fn("dma_start", sinb[:],
                                   ssrc[:, g * 512:(g + 1) * 512]),
                       inc=s_x, inc_amt=16, tag=f"{kind}.sin{g}")
                m_cs = p.mark(s_x)

                ev = []
                prev_ev = m_xnf_ev
                for pl in range(4):
                    pt = g * 4 + pl
                    for f in range(NFT):
                        p.emit("tensor",
                               _fn("transpose",
                                   psE_bf[:, f * 128:(f + 1) * 128],
                                   slot(pt)[:, f * 128:(f + 1) * 128],
                                   ident[:]),
                               waits=([(s_v, prev_ev), (s_v, m_cons_E)]
                                      if f == 0 else []),
                               inc=(s_pe if f == NFT - 1 else None),
                               tag=f"{kind}.T{pt}.{f}")
                    mtp = p.mark(s_pe)
                    dst = xnf4[:].rearrange("q (f t) -> q f t", t=512)
                    dst = dst[:, :, pl * 128:(pl + 1) * 128]
                    src = psE_bf[:].rearrange("q (f t) -> q f t", t=128)
                    p.emit("vector", _fn("tensor_copy", dst, src),
                           waits=[(s_pe, mtp), (s_pe, m_mm_done)], inc=s_v,
                           tag=f"{kind}.ev{pt}")
                    ev.append(p.mark(s_v))
                    prev_ev = ev[-1]
                m_xnf_ev = ev[-1]

                def xnf_t(f, width=512, off=0):
                    return xnf4[:, f * 512 + off: f * 512 + off + width]

                if kind == "kv":
                    def wk_t(f, kvh):
                        return rw[:, f * 512 + kvh * 128: f * 512 + (kvh + 1) * 128]

                    def wv_t(f):
                        return rw[:, 8192 + f * 512: 8192 + (f + 1) * 512]

                    for kvh in range(NKV):
                        for f in range(NFT):
                            p.emit("tensor",
                                   _fn("matmul",
                                       psO[:, kvh * 512:(kvh + 1) * 512],
                                       wk_t(f, kvh), xnf_t(f),
                                       start=(f == 0), stop=(f == NFT - 1)),
                                   waits=([(s_v, m_xnf_ev), (s_w, mw),
                                           (s_v, m_rope)]
                                          if (kvh == 0 and f == 0) else []),
                                   inc=(s_pe if f == NFT - 1 else None),
                                   tag=f"p2.K{g}.{kvh}.{f}")
                    mk = p.mark(s_pe)
                    for pl in range(4):
                        for f in range(NFT):
                            p.emit("tensor",
                                   _fn("matmul",
                                       psE[:, pl * 512:(pl + 1) * 512],
                                       xnf_t(f, 128, pl * 128), wv_t(f),
                                       start=(f == 0), stop=(f == NFT - 1)),
                                   waits=([(s_v, m_cons_E)]
                                          if (pl == 0 and f == 0) else []),
                                   inc=(s_pe if f == NFT - 1 else None),
                                   tag=f"p2.V{g}.{pl}.{f}")
                    m_mm_done = p.mark(s_pe)
                    mvv = m_mm_done

                    for kvh in range(NKV):
                        ps = psO[:, kvh * 512:(kvh + 1) * 512]
                        kd = K_kvh(kvh)[:, g * 512:(g + 1) * 512]
                        p.emit("vector", _fn("tensor_copy", kraw[:], ps),
                               waits=[(s_pe, mk), (s_x, m_cs),
                                      (s_pe, p.mark(s_pe))],
                               inc=s_v, tag=f"p2.kr{g}.{kvh}")
                        mkraw = p.mark(s_v)
                        p.emit("tensor", _fn("matmul", ps, rmat[:], kraw[:],
                                             start=True, stop=True),
                               waits=[(s_v, mkraw)], inc=s_pe,
                               tag=f"p2.rot{g}.{kvh}")
                        mrot = p.mark(s_pe)
                        p.emit("vector", _fn("tensor_tensor", rt1[:], kraw[:],
                                             cosb[:], ALU.mult),
                               tag=f"p2.r{g}.{kvh}.a")
                        p.emit("vector", _fn("tensor_tensor", rt2[:], ps,
                                             sinb[:], ALU.mult),
                               waits=[(s_pe, mrot)],
                               tag=f"p2.r{g}.{kvh}.b")
                        p.emit("vector", _fn("tensor_tensor", kd, rt1[:],
                                             rt2[:], ALU.add),
                               inc=s_v, tag=f"p2.r{g}.{kvh}.c")
                        p.emit("vector", _fn("tensor_tensor", sqb[:], kraw[:],
                                             kraw[:], ALU.mult),
                               inc=s_v, tag=f"p2.sq{g}.{kvh}")
                        msq = p.mark(s_v)
                        p.emit("tensor", _fn("matmul", ps[0:1, :],
                                             onesb[:, 0:1], sqb[:],
                                             start=True, stop=True),
                               waits=[(s_v, msq)], inc=s_pe,
                               tag=f"p2.nm{g}.{kvh}")
                        mnm = p.mark(s_pe)
                        p.emit("vector", _fn("reduce_max",
                                             vecs[0:1, V_NTMP:V_NTMP + 1],
                                             ps[0:1, :], AX.X),
                               waits=[(s_pe, mnm)], tag=f"p2.nr{g}.{kvh}")
                        gap()
                        ca = V_KN + kvh * 2 + (g % 2)
                        cb = V_KN + kvh * 2 + ((g + 1) % 2)
                        if g == 0:
                            p.emit("vector", _fn("tensor_copy",
                                                 vecs[0:1, cb:cb + 1],
                                                 vecs[0:1, V_NTMP:V_NTMP + 1]),
                                   inc=s_v, tag=f"p2.nc{g}.{kvh}")
                        else:
                            p.emit("vector", _fn("tensor_tensor",
                                                 vecs[0:1, cb:cb + 1],
                                                 vecs[0:1, ca:ca + 1],
                                                 vecs[0:1, V_NTMP:V_NTMP + 1],
                                                 ALU.max),
                                   inc=s_v, tag=f"p2.nc{g}.{kvh}")
                    m_rope = p.mark(s_v)
                    m_csfree = m_rope
                    for pl in range(4):
                        p.emit("vector", _fn("tensor_copy", V_pt(g * 4 + pl)[:],
                                             psE[:, pl * 512:(pl + 1) * 512]),
                               waits=[(s_pe, mvv)] if pl == 0 else [],
                               inc=(s_v if pl == 3 else None),
                               tag=f"p2.vev{g}.{pl}")
                    m_cons_E = p.mark(s_v)
                else:
                    def wq_t(f, h):
                        return rw[:, f * C + h * 128: f * C + (h + 1) * 128]

                    rq_r = rq[:].rearrange("q (s e) -> q s e", e=C)
                    m_ropeq = [0, 0, 0, 0]
                    for h in range(NH):
                        qtr = h % 4
                        ps = psO[:, qtr * 512:(qtr + 1) * 512]
                        for f in range(NFT):
                            p.emit("tensor",
                                   _fn("matmul", ps, wq_t(f, h), xnf_t(f),
                                       start=(f == 0), stop=(f == NFT - 1)),
                                   waits=([(s_v, m_xnf_ev), (s_w, mw)]
                                          if (h == 0 and f == 0) else []) +
                                         ([(s_v, m_ropeq[qtr])] if f == 0 else []),
                                   inc=(s_pe if f == NFT - 1 else None),
                                   tag=f"p3.Q{g}.{h}.{f}")
                        mq = p.mark(s_pe)
                        if h == NH - 1:
                            m_mm_done = mq
                        dst = rq_r[:, 4 * g:4 * g + 4, h * 128:(h + 1) * 128]
                        p.emit("vector", _fn("tensor_copy", kraw[:], ps),
                               waits=[(s_pe, mq)] + ([(s_x, m_cs)]
                                                     if h == 0 else []),
                               inc=s_v, tag=f"p3.kr{g}.{h}")
                        mkraw = p.mark(s_v)
                        p.emit("tensor", _fn("matmul", ps, rmat[:], kraw[:],
                                             start=True, stop=True),
                               waits=[(s_v, mkraw)], inc=s_pe,
                               tag=f"p3.rot{g}.{h}")
                        mrot = p.mark(s_pe)
                        p.emit("vector", _fn("tensor_tensor", rt1[:], kraw[:],
                                             cosb[:], ALU.mult),
                               tag=f"p3.r{g}.{h}.a")
                        p.emit("vector", _fn("tensor_tensor", rt2[:], ps,
                                             sinb[:], ALU.mult),
                               waits=[(s_pe, mrot)],
                               tag=f"p3.r{g}.{h}.b")
                        p.emit("vector",
                               _fn("tensor_tensor", dst,
                                   rt1[:].rearrange("q (s e) -> q s e", e=128),
                                   rt2[:].rearrange("q (s e) -> q s e", e=128),
                                   ALU.add),
                               inc=s_v, tag=f"p3.r{g}.{h}.c")
                        p.emit("vector", _fn("tensor_tensor", sqb[:], kraw[:],
                                             kraw[:], ALU.mult),
                               inc=s_v, tag=f"p3.sq{g}.{h}")
                        msq = p.mark(s_v)
                        p.emit("tensor", _fn("matmul", ps[0:1, :],
                                             onesb[:, 0:1], sqb[:],
                                             start=True, stop=True),
                               waits=[(s_v, msq)], inc=s_pe,
                               tag=f"p3.nm{g}.{h}")
                        mnm = p.mark(s_pe)
                        p.emit("vector", _fn("reduce_max",
                                             vecs[0:1, V_NTMP:V_NTMP + 1],
                                             ps[0:1, :], AX.X),
                               waits=[(s_pe, mnm)], tag=f"p3.nr{g}.{h}")
                        gap()
                        ca = V_QN + h * 2 + (g % 2)
                        cb = V_QN + h * 2 + ((g + 1) % 2)
                        if g == 0:
                            p.emit("vector", _fn("tensor_copy",
                                                 vecs[0:1, cb:cb + 1],
                                                 vecs[0:1, V_NTMP:V_NTMP + 1]),
                                   inc=s_v, tag=f"p3.nc{g}.{h}")
                        else:
                            p.emit("vector", _fn("tensor_tensor",
                                                 vecs[0:1, cb:cb + 1],
                                                 vecs[0:1, ca:ca + 1],
                                                 vecs[0:1, V_NTMP:V_NTMP + 1],
                                                 ALU.max),
                                   inc=s_v, tag=f"p3.nc{g}.{h}")
                        m_ropeq[qtr] = p.mark(s_v)
                    m_rope = p.mark(s_v)
                    m_csfree = m_rope

        if phases >= 2:
            p.phase()
            with ExitStack() as ph:
                qkv_pass("kv", ph)
        if phases >= 3:
            p.phase()
            with ExitStack() as ph:
                qkv_pass("q", ph)

        if phases >= 4:
            p.phase()
            with ExitStack() as ph:
                xt4 = ph.enter_context(nc.sbuf_tensor("p4x", [128, C], BF16))
                x2b = ph.enter_context(nc.sbuf_tensor("p4x2b", [128, C], F32))
                pbf = ph.enter_context(nc.sbuf_tensor("p4pb", [128, T], BF16))
                pnb = ph.enter_context(nc.sbuf_tensor("p4pn", [128, T], BF16))
                ptsb = ph.enter_context(
                    nc.sbuf_tensor("p4pt", [128, 2 * T], BF16))
                ysb = ph.enter_context(nc.sbuf_tensor("p4y", [128, C], BF16))
                xn2tm = ph.enter_context(nc.sbuf_tensor("p4x2", [128, C], BF16))

                for f in range(NFT):
                    p.emit("gpsimd", _fn("dma_start", rw[:, f * C:(f + 1) * C],
                                         wp_in[f * 128:(f + 1) * 128, :]),
                           inc=s_w, inc_amt=16, tag=f"p4.wp{f}")
                mwp = p.mark(s_w)

                def wp_t(f, oc):
                    return rw[:, f * C + oc * 512: f * C + (oc + 1) * 512]

                for h in range(NH):
                    p.emit("vector", _fn("tensor_tensor",
                                         vecs[0:1, V_CMX:V_CMX + 1],
                                         vecs[0:1, V_QN + h * 2:
                                              V_QN + h * 2 + 1],
                                         vecs[0:1, V_KN + (h // 4) * 2:
                                              V_KN + (h // 4) * 2 + 1],
                                         ALU.mult),
                           inc=s_v, tag=f"p4.bb{h}")
                    mbb = p.mark(s_v)
                    p.emit("scalar", _fn("activation",
                                         vecs[0:1, V_CMX + 1:V_CMX + 2],
                                         vecs[0:1, V_CMX:V_CMX + 1], AF.Ln,
                                         bias=0.0, scale=SCALE * SCALE),
                           waits=[(s_v, mbb)], tag=f"p4.bl{h}")
                    p.emit("scalar", _fn("activation",
                                         vecs[0:1, V_NTMP:V_NTMP + 1],
                                         vecs[0:1, V_CMX + 1:V_CMX + 2],
                                         AF.Exp, bias=0.0, scale=0.5),
                           inc=s_a, tag=f"p4.be{h}")
                    mbe = p.mark(s_a)
                    p.emit("tensor", _fn("matmul", psE[:, 0:1],
                                         onesf[0:1, 0:128],
                                         vecs[0:1, V_NTMP:V_NTMP + 1],
                                         start=True, stop=True),
                           waits=[(s_a, mbe), (s_v, p.mark(s_v))],
                           inc=s_pe, tag=f"p4.bm{h}")
                    mbm = p.mark(s_pe)
                    p.emit("vector", _fn("tensor_scalar_mul",
                                         vec(V_B16 + h), psE[:, 0:1], -1.0),
                           waits=[(s_pe, mbm)], inc=s_v, tag=f"p4.bn{h}")
                m_bias16 = p.mark(s_v)
                p.waitop("tensor", [(s_v, m_bias16)], tag="p4.biasbar")

                m_exp = 0
                m_norm = 0
                m_tp = 0
                m_ptev = 0
                m_pv = 0
                m_yev = [0, 0]
                m_proj = 0
                m_zev = 0
                m_x2 = 0
                m_xnt = 0
                m_m2cp = 0
                m_slot_ev = 0

                def head_pe(qt, h):
                    nk = qt + 1
                    L = nk * 128
                    nch = (L + 511) // 512
                    kvh = h // 4
                    first = [(s_a, m_exp), (s_v, m_zev)]
                    marks = []
                    for c_ in range(nch):
                        wdt = min(512, L - c_ * 512)
                        last = (c_ == nch - 1)
                        p.emit("tensor",
                               _fn("matmul", psE[:, c_ * 512:c_ * 512 + wdt],
                                   slot(qt)[:, h * 128:(h + 1) * 128],
                                   K_kvh(kvh)[:, c_ * 512:c_ * 512 + wdt],
                                   start=True, stop=not last),
                               waits=first if c_ == 0 else [],
                               tag=f"p4.S{qt}.{h}.{c_}")
                        first = []
                    p.emit("tensor",
                           _fn("matmul", psE[:, qt * 128:L], ident[:], dmask[:],
                               start=False, stop=True),
                           inc=s_pe, tag=f"p4.M{qt}.{h}")
                    marks.append(p.mark(s_pe))
                    return marks

                def head_va(qt, h, marks):
                    nonlocal m_exp, m_norm
                    L = (qt + 1) * 128
                    p.emit("scalar", _fn("activation", pbf[:, 0:L], psE[:, 0:L],
                                         AF.Exp, bias=vec(V_B16 + h), scale=1.0,
                                         accum_out=vec(V_SSUM)),
                           waits=[(s_pe, marks[-1]), (s_v, m_norm)],
                           inc=s_a, tag=f"p4.ex{qt}.{h}")
                    m_exp = p.mark(s_a)
                    p.emit("vector", _fn("reciprocal", vec(V_RSUM), vec(V_SSUM)),
                           waits=[(s_a, m_exp)], tag=f"p4.rs{qt}.{h}")
                    gap()
                    p.emit("vector", _fn("tensor_scalar_mul", pnb[:, 0:L],
                                         pbf[:, 0:L], vec(V_RSUM)),
                           waits=[(s_pe, m_tp)], inc=s_v, tag=f"p4.nm{qt}.{h}")
                    m_norm = p.mark(s_v)

                def head_tail_pe(qt, h):
                    nonlocal m_tp, m_pv, m_ptev, m_yev
                    nk = qt + 1
                    kvh = h // 4
                    pr = h // 2
                    hh = h % 2
                    pt_r = ptsb[:].rearrange("q (kt e) -> q kt e", e=256)
                    for kt in range(nk):
                        p.emit("tensor",
                               _fn("transpose",
                                   psO_bf[:, kt * 128:(kt + 1) * 128],
                                   pnb[:, kt * 128:(kt + 1) * 128], ident[:]),
                               waits=([(s_v, m_norm), (s_v, m_ptev),
                                       (s_v, m_slot_ev), (s_v, m_x2)]
                                      if kt == 0 else []),
                               inc=(s_pe if kt == nk - 1 else None),
                               tag=f"p4.T{qt}.{h}.{kt}")
                    m_tp = p.mark(s_pe)
                    dst = pt_r[:, 0:nk, hh * 128:(hh + 1) * 128]
                    src_ = psO_bf[:, 0:nk * 128].rearrange(
                        "q (kt e) -> q kt e", e=128)
                    p.emit("vector", _fn("tensor_copy", dst, src_),
                           waits=[(s_pe, m_tp), (s_pe, m_pv)], inc=s_v,
                           tag=f"p4.PT{qt}.{h}")
                    m_ptev = p.mark(s_v)
                    if hh == 1:
                        ycol = (pr % 2) * 512
                        for kt in range(nk):
                            p.emit("tensor",
                                   _fn("matmul", psO[:, ycol:ycol + 256],
                                       V_pt(kt)[:, kvh * 128:(kvh + 1) * 128],
                                       ptsb[:, kt * 256:(kt + 1) * 256],
                                       start=(kt == 0), stop=(kt == nk - 1)),
                                   waits=([(s_v, m_ptev),
                                           (s_v, m_yev[pr % 2]),
                                           (s_v, m_x2)]
                                          if kt == 0 else []),
                                   inc=(s_pe if kt == nk - 1 else None),
                                   tag=f"p4.PV{qt}.{pr}.{kt}")
                        m_pv = p.mark(s_pe)
                        p.emit("vector",
                               _fn("tensor_copy",
                                   ysb[:, pr * 256:(pr + 1) * 256],
                                   psO[:, ycol:ycol + 256]),
                               waits=[(s_pe, m_pv), (s_pe, m_proj)],
                               inc=s_v, tag=f"p4.ye{qt}.{pr}")
                        m_yev[pr % 2] = p.mark(s_v)

                pending_x2 = None

                def flush_x2():
                    nonlocal m_xnt, m_slot_ev, pending_x2
                    if pending_x2 is None:
                        return
                    fqt, fmxn = pending_x2
                    pending_x2 = None
                    for f in range(NFT):
                        p.emit("tensor",
                               _fn("transpose",
                                   psO_bf[:, f * 128:(f + 1) * 128],
                                   xn2tm[:, f * 128:(f + 1) * 128], ident[:]),
                               waits=([(s_v, fmxn), (s_v, m_ptev)]
                                      if f == 0 else []),
                               inc=(s_pe if f == NFT - 1 else None),
                               tag=f"p4.xT{fqt}.{f}")
                    m_xnt = p.mark(s_pe)
                    p.emit("vector", _fn("tensor_copy", slot(fqt)[:],
                                         psO_bf[:]),
                           waits=[(s_pe, m_xnt)], inc=s_v, tag=f"p4.xe{fqt}")
                    m_slot_ev = p.mark(s_v)

                for qt in range(NPT):
                    p.emit("gpsimd", _fn("dma_start", xt4[:],
                                         x_in[qt * 128:(qt + 1) * 128, :]),
                           waits=[(s_v, m_x2)], inc=s_w, inc_amt=16,
                           tag=f"p4.ldx{qt}")
                    mx4 = p.mark(s_w)
                    for h in range(NH):
                        marks = head_pe(qt, h)
                        if h == 2:
                            flush_x2()
                        if h > 0:
                            head_tail_pe(qt, h - 1)
                        head_va(qt, h, marks)
                    head_tail_pe(qt, NH - 1)
                    for f in range(NFT):
                        for oc in range(4):
                            p.emit("tensor",
                                   _fn("matmul", psE[:, oc * 512:(oc + 1) * 512],
                                       ysb[:, f * 128:(f + 1) * 128],
                                       wp_t(f, oc),
                                       start=(f == 0), stop=(f == NFT - 1)),
                                   waits=([(s_v, m_yev[0]), (s_v, m_yev[1]),
                                           (s_a, m_exp), (s_w, mwp)]
                                          if (oc == 0 and f == 0) else []),
                                   inc=(s_pe if (oc == 3 and f == NFT - 1)
                                        else None),
                                   tag=f"p4.pj{qt}.{oc}.{f}")
                    m_proj = p.mark(s_pe)
                    for i in range(4):
                        p.emit("vector", _fn("bn_stats", vec(V_BNST + 6 * i, 6),
                                             psE[:, 512 * i:512 * (i + 1)]),
                               waits=([(s_a, m_m2cp), (s_pe, m_proj)]
                                      if i == 0 else []),
                               tag=f"p4.zb{qt}.{i}")
                    gap()
                    p.emit("vector", _fn("bn_aggr", vec(V_STAT, 2),
                                         vec(V_BNST, 24)),
                           inc=s_v, tag=f"p4.za{qt}")
                    mzs = p.mark(s_v)
                    p.emit("scalar", _fn("activation", vec(V_LNT),
                                         vec(V_STAT + 1), AF.Ln,
                                         bias=epsb[:, 0:1], scale=1.0),
                           waits=[(s_v, mzs)], tag=f"p4.zl{qt}")
                    p.emit("scalar", _fn("activation", vec(V_RSTD), vec(V_LNT),
                                         AF.Exp, bias=0.0, scale=-0.5),
                           inc=s_a, tag=f"p4.zr{qt}")
                    mza = p.mark(s_a)
                    p.emit("vector", _fn("tensor_scalar", psO[:], psE[:],
                                         vec(V_STAT), vec(V_RSTD),
                                         ALU.subtract, ALU.mult),
                           waits=[(s_a, mza), (s_pe, m_pv)], inc=s_v,
                           tag=f"p4.zn{qt}")
                    m_zev = p.mark(s_v)
                    p.emit("vector", _fn("tensor_tensor", x2b[:], psO[:],
                                         xt4[:], ALU.add),
                           waits=[(s_w, mx4)], inc=s_v, tag=f"p4.x2{qt}")
                    m_x2 = p.mark(s_v)
                    for i in range(4):
                        p.emit("vector", _fn("bn_stats", vec(V_BNST + 6 * i, 6),
                                             x2b[:, 512 * i:512 * (i + 1)]),
                               tag=f"p4.b2{qt}.{i}")
                    gap()
                    p.emit("vector", _fn("bn_aggr", vec(V_STAT, 2),
                                         vec(V_BNST, 24)),
                           inc=s_v, tag=f"p4.a2{qt}")
                    m2s = p.mark(s_v)
                    p.emit("scalar", _fn("activation", vec(V_LNT),
                                         vec(V_STAT + 1), AF.Ln,
                                         bias=epsb[:, 0:1], scale=1.0),
                           waits=[(s_v, m2s)], tag=f"p4.l2{qt}")
                    p.emit("scalar", _fn("activation", vec(V_RSTD), vec(V_LNT),
                                         AF.Exp, bias=0.0, scale=-0.5),
                           tag=f"p4.r2{qt}")
                    p.emit("scalar", _fn("activation", vec(V_SD2 + qt),
                                         vec(V_LNT), AF.Exp, bias=0.0,
                                         scale=0.5),
                           tag=f"p4.s2{qt}")
                    p.emit("scalar", _fn("copy", vec(V_M2 + qt), vec(V_STAT)),
                           inc=s_a, tag=f"p4.m2{qt}")
                    m_m2cp = p.mark(s_a)
                    p.emit("vector", _fn("tensor_scalar", xn2tm[:], x2b[:],
                                         vec(V_STAT), vec(V_RSTD),
                                         ALU.subtract, ALU.mult),
                           waits=[(s_a, m_m2cp), (s_pe, m_xnt)], inc=s_v,
                           tag=f"p4.xn{qt}")
                    mxn = p.mark(s_v)
                    pending_x2 = (qt, mxn)
                flush_x2()

        if phases >= 5:
            preload_marks = {}
            f2marks = {}
            for ps_i in range(4):
                sl = [4 * ps_i + j for j in range(4)]
                if ps_i == 0:
                    p.phase()
                fm = f2marks.get(ps_i, {})
                with ExitStack() as ph:
                    h32 = [ph.enter_context(
                        nc.sbuf_tensor(f"h32{b_}_{ps_i}", [128, 2048], F32))
                        for b_ in range(2)]
                    sg = ph.enter_context(
                        nc.sbuf_tensor(f"sg_{ps_i}", [128, 2048], F32))
                    hsc4 = ph.enter_context(
                        nc.sbuf_tensor(f"hsc4_{ps_i}", [128, 4096], BF16))
                    aabn = ph.enter_context(
                        nc.sbuf_tensor(f"aabn_{ps_i}", [128, 1024], F32))
                    aab = aabn[:, 0:512]
                    nb = [aabn[:, 512 + i * 128:512 + (i + 1) * 128]
                          for i in range(4)]

                    def hscv(ic_, pt_):
                        o = (ic_ % 2) * 2048 + pt_ * 512
                        return hsc4[:, o:o + 512]

                    def half(s):
                        if s < 4:
                            return rkv[:, s * 4096:(s + 1) * 4096]
                        return wexs[:]

                    def wtile(L, f):
                        h = half((2 * L + (1 if f >= 8 else 0)) % 5)
                        return h[:, (f % 8) * 512:((f % 8) + 1) * 512]

                    def dma_half(L, k, waits):
                        ic, j = divmod(L, 2)
                        dst = half((2 * L + k) % 5)[:].rearrange(
                            "q (f i) -> q f i", i=512)
                        srcw = wg_in if j == 0 else wu_in
                        s_ = srcw[k * 1024:(k + 1) * 1024,
                                  ic * 512:(ic + 1) * 512]
                        s_ = s_.rearrange("(f q) i -> q f i", q=128)
                        p.emit("gpsimd", _fn("dma_start", dst, s_),
                               waits=waits, inc=s_w, inc_amt=16,
                               tag=f"f1.w{ps_i}.{L}.{k}")
                        return p.mark(s_w)

                    mld = {}
                    if ps_i in preload_marks:
                        mld[0], mld[1] = preload_marks[ps_i]
                        dma_half(2, 0, [])
                    else:
                        dma_half(0, 0, [])
                        mld[0] = dma_half(0, 1, [])
                        dma_half(1, 0, [])
                        mld[1] = dma_half(1, 1, [])
                        dma_half(2, 0, [])
                    m_silu = 0
                    m_hf = 0
                    m_hf_prev = 0
                    m_sp_par = [0, 0]
                    m_dj = 0
                    m_hst = 0
                    m_hsev = 0
                    m_hsev2 = [0, 0]
                    pend_hst = []
                    pend_sp = None

                    def emit_hst(ic, all_=False):
                        nonlocal m_hst, m_hsev
                        todo = [e for e in pend_hst
                                if all_ or e[0] <= ic - 2]
                        pend_hst[:] = [e for e in pend_hst
                                       if not (all_ or e[0] <= ic - 2)]
                        for (ric, pt_, msp) in todo:
                            par = pt_ % 2
                            bk = psO_bf0[:, par * 1024:(par + 1) * 1024]
                            for k in range(4):
                                p.emit("tensor",
                                       _fn("transpose",
                                           bk[:, k * 128:(k + 1) * 128],
                                           hscv(ric, pt_)[:, k * 128:
                                                          (k + 1) * 128],
                                           ident[:]),
                                       waits=([(s_v, msp),
                                               (s_v, m_hsev2[par])]
                                              if k == 0 else []),
                                       inc=(s_pe if k == 3 else None),
                                       tag=f"f1.hT{ps_i}.{ric}.{pt_}.{k}")
                            m_hst = p.mark(s_pe)
                            rw_r = rw[:].rearrange("q (it t) -> q it t", t=512)
                            dstv = rw_r[:, ric * 4:(ric + 1) * 4,
                                        pt_ * 128:(pt_ + 1) * 128]
                            srcv = bk[:, 0:512].rearrange(
                                "q (k t) -> q k t", t=128)
                            p.emit("vector", _fn("tensor_copy", dstv, srcv),
                                   waits=[(s_pe, m_hst)], inc=s_v,
                                   tag=f"f1.he{ps_i}.{ric}.{pt_}")
                            m_hsev2[par] = p.mark(s_v)
                            m_hsev = m_hsev2[par]

                    def emit_sparsify():
                        nonlocal pend_sp, m_dj, m_sp_par
                        if pend_sp is None:
                            return
                        ric, mhf = pend_sp
                        pend_sp = None

                        def jv(t_, j):
                            v3 = t_.rearrange("q (g j) -> q g j", j=4)
                            return v3[:, :, j]

                        for pt_ in range(4):
                            hv = h32[ric % 2][:, pt_ * 512:(pt_ + 1) * 512]
                            p.emit("scalar", _fn("activation", aab, hv, AF.Abs),
                                   waits=[(s_v, mhf), (s_v, m_dj)],
                                   inc=s_a, tag=f"f1.ab{ps_i}.{ric}.{pt_}")
                            m_ab = p.mark(s_a)
                            gap()
                            p.emit("vector", _fn("tensor_tensor", nb[0][:],
                                                 jv(aab, 0), jv(aab, 1),
                                                 ALU.min),
                                   waits=[(s_a, m_ab)],
                                   tag=f"f1.n1{ps_i}.{ric}.{pt_}")
                            p.emit("vector", _fn("tensor_tensor", nb[1][:],
                                                 jv(aab, 0), jv(aab, 1),
                                                 ALU.max),
                                   tag=f"f1.n2{ps_i}.{ric}.{pt_}")
                            p.emit("vector", _fn("tensor_tensor", nb[2][:],
                                                 jv(aab, 2), jv(aab, 3),
                                                 ALU.min),
                                   tag=f"f1.n3{ps_i}.{ric}.{pt_}")
                            p.emit("vector", _fn("tensor_tensor", nb[3][:],
                                                 jv(aab, 2), jv(aab, 3),
                                                 ALU.max),
                                   tag=f"f1.n4{ps_i}.{ric}.{pt_}")
                            p.emit("vector", _fn("tensor_tensor", nb[0][:],
                                                 nb[0][:], nb[2][:], ALU.max),
                                   tag=f"f1.n5{ps_i}.{ric}.{pt_}")
                            p.emit("vector", _fn("tensor_tensor", nb[1][:],
                                                 nb[1][:], nb[3][:], ALU.min),
                                   tag=f"f1.n6{ps_i}.{ric}.{pt_}")
                            gap()
                            p.emit("vector", _fn("tensor_tensor", nb[0][:],
                                                 nb[0][:], nb[1][:], ALU.min),
                                   tag=f"f1.n7{ps_i}.{ric}.{pt_}")
                            gap()
                            for j in range(4):
                                p.emit("vector",
                                       _fn("tensor_tensor", jv(aab, j),
                                           jv(aab, j), nb[0][:], ALU.subtract),
                                       tag=f"f1.d{ps_i}.{ric}.{pt_}.{j}")
                            for j in range(4):
                                p.emit("vector",
                                       _fn("scalar_tensor_tensor",
                                           jv(hscv(ric, pt_)[:], j),
                                           jv(aab, j), 0.0,
                                           jv(hv, j), ALU.is_gt, ALU.mult),
                                       waits=([(s_pe, m_hst)] if j == 0
                                              else []),
                                       inc=(s_v if j == 3 else None),
                                       tag=f"f1.m{ps_i}.{ric}.{pt_}.{j}")
                            m_sp = p.mark(s_v)
                            m_dj = m_sp
                            pend_hst.append((ric, pt_, m_sp))
                        m_sp_par[ric % 2] = m_dj

                    for ic in range(16):
                        Lg, Lu = 2 * ic, 2 * ic + 1
                        for pl in range(4):
                            for f in range(NFT):
                                w0 = ([(s_w, mld.get(Lg, 0)), (s_a, m_silu)]
                                      if (pl == 0 and f == 0) else [])
                                if pl == 0 and f == 0 and ic == 0 and fm:
                                    w0 = w0 + [(s_v, fm["psfree"])]
                                p.emit("tensor",
                                       _fn("matmul",
                                           psE[:, pl * 512:(pl + 1) * 512],
                                           slot(sl[pl])[:, f * 128:(f + 1) * 128],
                                           wtile(Lg, f),
                                           start=(f == 0), stop=(f == NFT - 1)),
                                       waits=w0,
                                       inc=(s_pe if (pl == 3 and f == NFT - 1)
                                            else None),
                                       tag=f"f1.g{ps_i}.{ic}.{pl}.{f}")
                        m_gd = p.mark(s_pe)
                        if Lg + 2 < 32:
                            dma_half(Lg + 2, 1, [(s_pe, m_gd)])
                            mld[Lg + 2] = p.mark(s_w)
                        if Lg + 3 < 32:
                            dma_half(Lg + 3, 0, [])
                        emit_hst(ic)
                        for pl in range(4):
                            for f in range(NFT):
                                p.emit("tensor",
                                       _fn("matmul",
                                           psO[:, pl * 512:(pl + 1) * 512],
                                           slot(sl[pl])[:, f * 128:(f + 1) * 128],
                                           wtile(Lu, f),
                                           start=(f == 0), stop=(f == NFT - 1)),
                                       waits=(([(s_w, mld.get(Lu, 0)),
                                                (s_v, m_hsev), (s_v, m_hf)] +
                                               ([(s_v, fm["psfree"])]
                                                if (ic == 0 and fm) else []))
                                              if (pl == 0 and f == 0) else []),
                                       inc=(s_pe if (pl == 3 and f == NFT - 1)
                                            else None),
                                       tag=f"f1.u{ps_i}.{ic}.{pl}.{f}")
                        m_umm = p.mark(s_pe)
                        if Lu + 2 < 32:
                            dma_half(Lu + 2, 1, [(s_pe, m_umm)])
                            mld[Lu + 2] = p.mark(s_w)
                        if Lu + 3 < 32:
                            dma_half(Lu + 3, 0, [])
                        p.emit("scalar", _fn("activation", sg[:],
                                             psE[:, 0:2048], AF.Silu),
                               waits=[(s_pe, m_gd), (s_v, m_hf)] +
                                     ([(s_pe, fm["mmlast"]),
                                       (s_out, fm["stores"]),
                                       (s_v, fm["psfree"])]
                                      if (ic == 0 and fm) else []),
                               inc=s_a, tag=f"f1.si{ps_i}.{ic}")
                        m_silu = p.mark(s_a)
                        hd_ = h32[ic % 2][:].rearrange("q (pt e) -> q pt e",
                                                       e=512)
                        sg3 = sg[:].rearrange("q (pt e) -> q pt e", e=512)
                        pu3 = psO[:].rearrange("q (pt e) -> q pt e", e=512)
                        p.emit("vector", _fn("tensor_tensor", hd_, sg3, pu3,
                                             ALU.mult),
                               waits=[(s_a, m_silu), (s_pe, m_umm),
                                      (s_v, m_sp_par[ic % 2])] +
                                     ([(s_out, fm["stores"])]
                                      if (ic == 0 and fm) else []),
                               inc=s_v, tag=f"f1.h{ps_i}.{ic}")
                        m_hf_prev = m_hf
                        m_hf = p.mark(s_v)
                        emit_sparsify()
                        pend_sp = (ic, m_hf)
                    emit_sparsify()
                    emit_hst(99, all_=True)
                    if ps_i + 1 < 4:
                        m_lastu = p.mark(s_pe)
                        sl2 = [4 * (ps_i + 1) + j for j in range(4)]

                        def dma_half2(L, k, waits):
                            ic_, j_ = divmod(L, 2)
                            dst = half((2 * L + k) % 5)[:].rearrange(
                                "q (f i) -> q f i", i=512)
                            srcw = wg_in if j_ == 0 else wu_in
                            s_ = srcw[k * 1024:(k + 1) * 1024,
                                      ic_ * 512:(ic_ + 1) * 512]
                            s_ = s_.rearrange("(f q) i -> q f i", q=128)
                            p.emit("gpsimd", _fn("dma_start", dst, s_),
                                   waits=waits, inc=s_w, inc_amt=16,
                                   tag=f"f1.pw{ps_i + 1}.{L}.{k}")
                            return p.mark(s_w)

                        dma_half2(0, 0, [(s_pe, m_lastu)])
                        mA = dma_half2(0, 1, [])
                        dma_half2(1, 0, [])
                        mB = dma_half2(1, 1, [])
                        preload_marks[ps_i + 1] = (mA, mB)

                p.phase()
                with ExitStack() as ph:
                    fbf = ph.enter_context(
                        nc.sbuf_tensor(f"fbf_{ps_i}", [128, 4 * C], BF16))
                    wd0 = ph.enter_context(
                        nc.sbuf_tensor(f"wd0_{ps_i}", [128, 4096], BF16))
                    wd1 = ph.enter_context(
                        nc.sbuf_tensor(f"wd1_{ps_i}", [128, 4096], BF16))
                    fo1 = ph.enter_context(
                        nc.sbuf_tensor(f"fo1_{ps_i}", [128, 512], F32))
                    fo3 = ph.enter_context(
                        nc.sbuf_tensor(f"fo3_{ps_i}", [128, 512], F32))
                    wdsl = [wd0, wd1, wexs]

                    def hsf_t(it, pl):
                        return rw[:, it * 512 + pl * 128: it * 512 + (pl + 1) * 128]

                    def dma_wd(os_, ic4, waits):
                        a = (os_ * 8 + ic4) % 3
                        dst = wdsl[a][:].rearrange("q (it o) -> q it o", o=512)
                        src = wd_in[ic4 * 1024:(ic4 + 1) * 1024,
                                    os_ * 512:(os_ + 1) * 512]
                        src = src.rearrange("(it q) o -> q it o", q=128)
                        p.emit("gpsimd", _fn("dma_start", dst, src),
                               waits=waits, inc=s_w, inc_amt=16,
                               tag=f"f2.wd{ps_i}.{os_}.{ic4}")
                        return p.mark(s_w)

                    m_fev2 = [0, 0]
                    m_wmm = {}
                    mw2carry = {}
                    for os_ in range(4):
                        mw2 = {}
                        if os_ == 0:
                            mw2[0] = dma_wd(os_, 0, [])
                            mw2[1] = dma_wd(os_, 1, [])
                            mw2[2] = dma_wd(os_, 2, [])
                        else:
                            for k in list(mw2carry):
                                mw2[k - 8] = mw2carry[k]
                            mw2carry.clear()
                        acc = psE if os_ % 2 == 0 else psO
                        for ic4 in range(8):
                            a = (os_ * 8 + ic4) % 3
                            for pl in range(4):
                                for it in range(8):
                                    itg = ic4 * 8 + it
                                    p.emit("tensor",
                                           _fn("matmul",
                                               acc[:, pl * 512:(pl + 1) * 512],
                                               hsf_t(itg, pl),
                                               wdsl[a][:, it * 512:(it + 1) * 512],
                                               start=(ic4 == 0 and it == 0),
                                               stop=(ic4 == 7 and it == 7)),
                                           waits=([(s_w, mw2[ic4]),
                                                   (s_v, m_fev2[os_ % 2])]
                                                  if (pl == 0 and it == 0)
                                                  else []),
                                           inc=(s_pe if (pl == 3 and it == 7)
                                                else None),
                                           tag=f"f2.m{ps_i}.{os_}.{ic4}.{pl}")
                            m_wmm[(os_, ic4)] = p.mark(s_pe)
                            nxt = ic4 + 3
                            if os_ * 8 + nxt < 32:
                                m_ = dma_wd(os_ + nxt // 8, nxt % 8,
                                            [(s_pe, m_wmm[(os_, ic4)])])
                                if nxt < 8:
                                    mw2[nxt] = m_
                                else:
                                    mw2carry[nxt] = m_
                        mlast = m_wmm[(os_, 7)]
                        for pl in range(4):
                            p.emit("vector",
                                   _fn("bn_stats",
                                       vec(V_BNF + pl * 24 + os_ * 6, 6),
                                       acc[:, pl * 512:(pl + 1) * 512]),
                                   waits=[(s_pe, mlast)] if pl == 0 else [],
                                   tag=f"f2.bn{ps_i}.{os_}.{pl}")
                            p.emit("vector",
                                   _fn("tensor_copy",
                                       fbf[:, pl * C + os_ * 512:
                                           pl * C + (os_ + 1) * 512],
                                       acc[:, pl * 512:(pl + 1) * 512]),
                                   inc=(s_v if pl == 3 else None),
                                   tag=f"f2.fe{ps_i}.{os_}.{pl}")
                        m_fev2[os_ % 2] = p.mark(s_v)

                    if debug and ps_i == 3:
                        p.emit("sync", _fn("dma_start", dbg_fbf, fbf[:]),
                               waits=[(s_v, m_fev2[0]), (s_v, m_fev2[1])],
                               inc=s_out, inc_amt=16, tag="dbg.fbf")
                    m_store = p.mark(s_out)
                    for pl in range(4):
                        gap()
                        p.emit("vector", _fn("bn_aggr", vec(V_M3 + pl * 2, 2),
                                             vec(V_BNF + pl * 24, 24)),
                               waits=([(s_v, m_fev2[0]), (s_v, m_fev2[1])]
                                      if pl == 0 else []),
                               inc=s_v, tag=f"f2.ag{ps_i}.{pl}")
                    mag = p.mark(s_v)
                    for pl in range(4):
                        p.emit("scalar", _fn("activation", vec(V_LNT),
                                             vec(V_M3 + pl * 2 + 1), AF.Ln,
                                             bias=epsb[:, 0:1], scale=1.0),
                               waits=[(s_v, mag)] if pl == 0 else [],
                               tag=f"f2.l{ps_i}.{pl}")
                        p.emit("scalar", _fn("activation", vec(V_RSTD3 + pl),
                                             vec(V_LNT), AF.Exp, bias=0.0,
                                             scale=-0.5),
                               inc=s_a, tag=f"f2.r{ps_i}.{pl}")
                    mra = p.mark(s_a)
                    for pl in range(4):
                        p.emit("vector", _fn("tensor_tensor",
                                             vec(V_CMX + 4 + pl),
                                             vec(V_M3 + pl * 2),
                                             vec(V_RSTD3 + pl), ALU.mult),
                               waits=[(s_a, mra)] if pl == 0 else [],
                               tag=f"f2.c1{ps_i}.{pl}")
                    gap()
                    for pl in range(4):
                        p.emit("vector", _fn("tensor_tensor",
                                             vec(V_CMX + 4 + pl),
                                             vec(V_M2 + sl[pl]),
                                             vec(V_CMX + 4 + pl), ALU.subtract),
                               inc=(s_v if pl == 3 else None),
                               tag=f"f2.c2{ps_i}.{pl}")
                    mc2 = p.mark(s_v)
                    m_ps_free = [0, 0]
                    for pl in range(4):
                        par = pl % 2
                        bk = (psE if par == 0 else psO)[:, 0:1024].bitcast(BF16)
                        for f in range(NFT):
                            p.emit("tensor",
                                   _fn("transpose",
                                       bk[:, f * 128:(f + 1) * 128],
                                       slot(sl[pl])[:, f * 128:(f + 1) * 128],
                                       ident[:]),
                                   waits=([(s_v, m_ps_free[par]),
                                           (s_v, mc2)] if f == 0 else []),
                                   inc=(s_pe if f == NFT - 1 else None),
                                   tag=f"f2.xT{ps_i}.{pl}.{f}")
                        m_xnt5 = p.mark(s_pe)
                        for os_ in range(4):
                            cs = slice(os_ * 512, (os_ + 1) * 512)
                            p.emit("vector",
                                   _fn("tensor_scalar", fo1[:], bk[:, cs],
                                       vec(V_SD2 + sl[pl]),
                                       vec(V_CMX + 4 + pl),
                                       ALU.mult, ALU.add),
                                   waits=([(s_pe, m_xnt5), (s_out, m_store)]
                                          if os_ == 0 else
                                          [(s_out, m_store)]),
                                   tag=f"f2.o1{ps_i}.{pl}.{os_}")
                            p.emit("vector",
                                   _fn("scalar_tensor_tensor", fo3[:],
                                       fbf[:, pl * C + os_ * 512:
                                           pl * C + (os_ + 1) * 512],
                                       vec(V_RSTD3 + pl), fo1[:],
                                       ALU.mult, ALU.add),
                                   inc=s_v, tag=f"f2.o3{ps_i}.{pl}.{os_}")
                            mo = p.mark(s_v)
                            r0 = sl[pl] * 128
                            p.emit("sync", _fn("dma_start",
                                               out[r0:r0 + 128, cs], fo3[:]),
                                   waits=[(s_v, mo)], inc=s_out, inc_amt=16,
                                   tag=f"f2.st{ps_i}.{pl}.{os_}")
                            m_store = p.mark(s_out)
                        m_ps_free[par] = p.mark(s_v)
                    f2marks[ps_i + 1] = {"mmlast": m_wmm[(3, 7)],
                                         "psfree": p.mark(s_v),
                                         "stores": p.mark(s_out)}

            p.phase()
            p.waitop("sync", [(s_out, p.mark(s_out))], tag="final")

        if debug:
            p.phase()
            nst = 0
            p.emit("sync", _fn("dma_start", dbg_rq, rq[:]), inc=s_out,
                   inc_amt=16, tag="dbg.rq")
            p.emit("sync", _fn("dma_start", dbg_rkv, rkv[:]), inc=s_out,
                   inc_amt=16, tag="dbg.rkv")
            p.emit("sync", _fn("dma_start", dbg_rw, rw[:]), inc=s_out,
                   inc_amt=16, tag="dbg.rw")
            p.emit("sync", _fn("dma_start", dbg_vecs, vecs[:]), inc=s_out,
                   inc_amt=16, tag="dbg.vecs")
            p.waitop("sync", [(s_out, p.mark(s_out))], tag="dbg.final")

        p.simulate()

        with nc.Block() as block:
            @block.sync
            def _(e):
                for f in p.ops["sync"]:
                    f(e)

            @block.gpsimd
            def _(e):
                for f in p.ops["gpsimd"]:
                    f(e)

            @block.tensor
            def _(e):
                for f in p.ops["tensor"]:
                    f(e)

            @block.vector
            def _(e):
                for f in p.ops["vector"]:
                    f(e)

            @block.scalar
            def _(e):
                for f in p.ops["scalar"]:
                    f(e)

    return nc


def make_consts():
    ident = np.eye(128, dtype=np.float32).astype(ml_dtypes.bfloat16)
    mask = np.where(np.arange(128)[None, :] <= np.arange(128)[:, None],
                    np.float32(0.0), np.float32(NEG)).astype(ml_dtypes.bfloat16)
    inv = (10000.0 ** (-(np.arange(0, HD, 2, dtype=np.float64) / HD)))
    th = np.arange(T, dtype=np.float64)[None, :] * inv[:, None]
    cos = np.cos(th).astype(np.float32)
    sin = np.sin(th).astype(np.float32)
    cosf = np.concatenate([cos, cos], axis=0).astype(ml_dtypes.bfloat16)
    sinf = np.concatenate([sin, sin], axis=0).astype(ml_dtypes.bfloat16)
    cosf2 = (np.concatenate([cos, cos], axis=0) / math.sqrt(HD)).astype(
        ml_dtypes.bfloat16)
    sinf2 = (np.concatenate([sin, sin], axis=0) / math.sqrt(HD)).astype(
        ml_dtypes.bfloat16)
    rot = np.zeros((128, 128), np.float32)
    for d in range(64):
        rot[d + 64, d] = -1.0
        rot[d, d + 64] = 1.0
    rot = rot.astype(ml_dtypes.bfloat16)
    onesbv = np.ones((128, 8), dtype=ml_dtypes.bfloat16)
    onesfv = np.ones((128, 128), dtype=np.float32)
    return ident, mask, cosf, sinf, rot, cosf2, sinf2, onesbv, onesfv


def kernel(L0_attn_x, L0_attn_Wq, L0_attn_Wk, L0_attn_Wv, L0_attn_Wproj,
           L0_ffn_w_gate, L0_ffn_w_up, L0_ffn_w_down,
           _phases=5, _debug=False):
    x = np.ascontiguousarray(np.asarray(L0_attn_x, dtype=np.float32))
    wq = np.ascontiguousarray(np.asarray(L0_attn_Wq, dtype=np.float32))
    wk = np.ascontiguousarray(np.asarray(L0_attn_Wk, dtype=np.float32))
    wv = np.ascontiguousarray(np.asarray(L0_attn_Wv, dtype=np.float32))
    wp = np.ascontiguousarray(np.asarray(L0_attn_Wproj, dtype=np.float32))
    wg = np.ascontiguousarray(np.asarray(L0_ffn_w_gate, dtype=np.float32))
    wu = np.ascontiguousarray(np.asarray(L0_ffn_w_up, dtype=np.float32))
    wd = np.ascontiguousarray(np.asarray(L0_ffn_w_down, dtype=np.float32))
    ident, mask, cosf, sinf, rot, cosf2, sinf2, onesbv, onesfv = make_consts()
    epsv = np.full((128, 8), EPS, dtype=np.float32)

    nc = build_nc(phases=_phases, debug=_debug)
    in_maps = []
    for c in range(8):
        in_maps.append({
            "x": np.ascontiguousarray(x[c]),
            "wq": wq, "wk": wk, "wv": wv, "wp": wp,
            "wg": wg, "wu": wu, "wd": wd,
            "identw": ident, "maskw": mask, "cosw": cosf, "sinw": sinf,
            "epsw": epsv, "rotw": rot, "cosw2": cosf2, "sinw2": sinf2,
            "onesbw": onesbv, "onesfw": onesfv,
        })
    res = run_bass_kernel_spmd(nc, in_maps, core_ids=list(range(8)))
    outs = np.stack([np.asarray(r["out"], dtype=np.float32)
                     for r in res.results], axis=0)
    if _debug:
        dbgs = [{k: np.asarray(v) for k, v in r.items() if k.startswith("dbg")}
                for r in res.results]
        return outs, dbgs
    return outs


# version 17
# speedup vs baseline: 1.0769x; 1.0069x over previous
import math
from collections import defaultdict
from contextlib import ExitStack

import numpy as np
import ml_dtypes

import concourse.bass as bass
import concourse.mybir as mybir
from concourse.bass_utils import run_bass_kernel_spmd

F32 = mybir.dt.float32
BF16 = mybir.dt.bfloat16
AF = mybir.ActivationFunctionType
ALU = mybir.AluOpType
AX = mybir.AxisListType

T = 2048
C = 2048
NH = 16
NKV = 4
HD = 128
II = 8192
NPT = T // 128
NFT = C // 128
EPS = 1e-5
SCALE = 1.0 / math.sqrt(HD)
NEG = -1.0e30

ENGS = ("sync", "gpsimd", "tensor", "vector", "scalar")


class Prog:

    def __init__(self, sems):
        self.ops = {e: [] for e in ENGS}
        self.cnt = defaultdict(int)
        self.meta = []
        self.sems = list(sems)
        self.bar = {}
        self.bar_done = set(ENGS)
        self.sem_by_id = {id(s): s for s in sems}

    def mark(self, sem):
        return self.cnt[id(sem)]

    def phase(self):
        self.bar = {id(s): self.cnt[id(s)] for s in self.sems}
        self.bar_done = set()

    def emit(self, eng, fn, waits=(), inc=None, inc_amt=1, tag=""):
        waits = [(s, v) for (s, v) in waits if v > 0]
        if eng not in self.bar_done:
            self.bar_done.add(eng)
            waits = [(self.sem_by_id[i], v) for i, v in self.bar.items()
                     if v > 0] + waits
        if inc is not None:
            self.cnt[id(inc)] += inc_amt
        self.meta.append((eng, [(id(s), v) for s, v in waits],
                          [(id(inc), inc_amt)] if inc is not None else [], tag))

        def run(e, fn=fn, waits=waits, inc=inc, amt=inc_amt):
            for s, v in waits:
                e.wait_ge(s, v)
            ins = fn(e)
            if inc is not None:
                ins.then_inc(inc, amt)
        self.ops[eng].append(run)

    def waitop(self, eng, waits, tag=""):
        self.emit(eng, lambda e: None, waits=waits, tag=tag)

    def simulate(self):
        queues = {e: [] for e in ENGS}
        for (eng, waits, incs, tag) in self.meta:
            queues[eng].append((waits, incs, tag))
        heads = {e: 0 for e in ENGS}
        cnt = defaultdict(int)
        progress = True
        while progress:
            progress = False
            for e in ENGS:
                while heads[e] < len(queues[e]):
                    waits, incs, tag = queues[e][heads[e]]
                    if all(cnt[s] >= v for s, v in waits):
                        for s, a in incs:
                            cnt[s] += a
                        heads[e] += 1
                        progress = True
                    else:
                        break
        if any(heads[e] < len(queues[e]) for e in ENGS):
            lines = []
            for e in ENGS:
                if heads[e] < len(queues[e]):
                    waits, incs, tag = queues[e][heads[e]]
                    missing = [(str(s), v, cnt[s]) for s, v in waits
                               if cnt[s] < v]
                    lines.append(f"{e}[{heads[e]}/{len(queues[e])}] tag={tag} "
                                 f"missing={missing}")
            raise RuntimeError("build-time DEADLOCK:\n" + "\n".join(lines))


def _fn(method, *args, **kwargs):
    return lambda e: getattr(e, method)(*args, **kwargs)


def build_nc(phases=5, debug=False):
    nc = bass.Bass("TRN2", target_bir_lowering=False)

    x_in = nc.dram_tensor("x", [T, C], F32, kind="ExternalInput").ap()
    wq_in = nc.dram_tensor("wq", [C, C], F32, kind="ExternalInput").ap()
    wk_in = nc.dram_tensor("wk", [C, NKV * HD], F32, kind="ExternalInput").ap()
    wv_in = nc.dram_tensor("wv", [C, NKV * HD], F32, kind="ExternalInput").ap()
    wp_in = nc.dram_tensor("wp", [C, C], F32, kind="ExternalInput").ap()
    wg_in = nc.dram_tensor("wg", [C, II], F32, kind="ExternalInput").ap()
    wu_in = nc.dram_tensor("wu", [C, II], F32, kind="ExternalInput").ap()
    wd_in = nc.dram_tensor("wd", [II, C], F32, kind="ExternalInput").ap()
    identw = nc.dram_tensor("identw", [128, 128], BF16, kind="ExternalInput").ap()
    maskw = nc.dram_tensor("maskw", [128, 128], BF16, kind="ExternalInput").ap()
    rotw = nc.dram_tensor("rotw", [128, 128], BF16, kind="ExternalInput").ap()
    onesbw = nc.dram_tensor("onesbw", [128, 8], BF16, kind="ExternalInput").ap()
    onesfw = nc.dram_tensor("onesfw", [128, 128], F32, kind="ExternalInput").ap()
    cosw = nc.dram_tensor("cosw", [128, T], BF16, kind="ExternalInput").ap()
    cosw2 = nc.dram_tensor("cosw2", [128, T], BF16, kind="ExternalInput").ap()
    sinw2 = nc.dram_tensor("sinw2", [128, T], BF16, kind="ExternalInput").ap()
    epsw = nc.dram_tensor("epsw", [128, 8], F32, kind="ExternalInput").ap()
    sinw = nc.dram_tensor("sinw", [128, T], BF16, kind="ExternalInput").ap()

    out = nc.dram_tensor("out", [T, C], F32, kind="ExternalOutput").ap()
    dbg = {}
    if debug:
        dbg_rq = nc.dram_tensor("dbg_rq", [128, NPT * C], BF16,
                                kind="ExternalOutput").ap()
        dbg_rkv = nc.dram_tensor("dbg_rkv", [128, 16384], BF16,
                                 kind="ExternalOutput").ap()
        dbg_vecs = nc.dram_tensor("dbg_vecs", [128, 256], F32,
                                  kind="ExternalOutput").ap()
        dbg_rw = nc.dram_tensor("dbg_rw", [128, 16 * C], BF16,
                                kind="ExternalOutput").ap()
        dbg_fbf = nc.dram_tensor("dbg_fbf", [128, 4 * C], BF16,
                                 kind="ExternalOutput").ap()

    with ExitStack() as top:
        E_ = top.enter_context
        rq = E_(nc.sbuf_tensor("rq", [128, NPT * C], BF16))
        rkv = E_(nc.sbuf_tensor("rkv", [128, 16384], BF16))
        rw = E_(nc.sbuf_tensor("rw", [128, 16 * C], BF16))
        vecs = E_(nc.sbuf_tensor("vecs", [128, 256], F32))
        ident = E_(nc.sbuf_tensor("ident", [128, 128], BF16))
        dmask = E_(nc.sbuf_tensor("dmask", [128, 128], BF16))
        rmat = E_(nc.sbuf_tensor("rmat", [128, 128], BF16))
        onesb = E_(nc.sbuf_tensor("onesb", [128, 8], BF16))
        onesf = E_(nc.sbuf_tensor("onesf", [128, 128], F32))
        wexs = E_(nc.sbuf_tensor("wexs", [128, 4096], BF16))
        epsb = E_(nc.sbuf_tensor("epsb", [128, 8], F32))
        psE = E_(nc.psum_tensor("psE", [128, 2048], F32))
        psO = E_(nc.psum_tensor("psO", [128, 2048], F32))

        s_x = E_(nc.semaphore("s_x"))
        s_w = E_(nc.semaphore("s_w"))
        s_pe = E_(nc.semaphore("s_pe"))
        s_v = E_(nc.semaphore("s_v"))
        s_a = E_(nc.semaphore("s_a"))
        s_out = E_(nc.semaphore("s_out"))
        p = Prog([s_x, s_w, s_pe, s_v, s_a, s_out])

        V_BNST, V_STAT = 0, 24
        V_NEGM, V_BIAS, V_SSUM, V_RSUM = 26, 27, 28, 29
        V_LNT, V_RSTD = 30, 32
        V_EPS = 34
        V_CMX = 36
        V_B16 = 80
        V_KN = 204
        V_QN = 212
        V_NTMP = 250
        V_M2, V_SD2 = 48, 64
        V_BNF = 96
        V_M3 = 192
        V_RSTD3 = 200

        def vec(col, n=1):
            return vecs[:, col:col + n]

        def gap(eng="vector"):
            p.emit(eng, _fn("tensor_copy", vecs[:, 44:48], epsb[:, 0:4]),
                   tag="gap")

        def slot(s):
            return rq[:, s * C:(s + 1) * C]

        def K_kvh(k):
            return rkv[:, k * T:(k + 1) * T]

        def V_pt(pt):
            return rkv[:, 8192 + pt * 512: 8192 + (pt + 1) * 512]

        psE_bf = psE[:, 0:1024].bitcast(BF16)
        psO_bf = psO[:, 1024:2048].bitcast(BF16)
        psO_bf0 = psO[:, 0:1024].bitcast(BF16)

        p.emit("sync", _fn("dma_start", ident[:], identw), inc=s_x, inc_amt=16,
               tag="c.id")
        p.emit("sync", _fn("dma_start", dmask[:], maskw), inc=s_x, inc_amt=16,
               tag="c.mask")
        p.emit("sync", _fn("dma_start", rmat[:], rotw), inc=s_x, inc_amt=16,
               tag="c.rot")
        p.emit("sync", _fn("dma_start", onesb[:], onesbw), inc=s_x,
               inc_amt=16, tag="c.ob")
        p.emit("sync", _fn("dma_start", onesf[:], onesfw), inc=s_x,
               inc_amt=16, tag="c.of")
        p.emit("sync", _fn("dma_start", epsb[:], epsw), inc=s_x,
               inc_amt=16, tag="c.eps")

        p.phase()
        with ExitStack() as ph:
            xt = [ph.enter_context(nc.sbuf_tensor(f"p1x{i}", [128, C], F32))
                  for i in range(2)]
            consumed = [0, 0]
            for pt in range(NPT):
                b = pt % 2
                p.emit("sync", _fn("dma_start", xt[b][:],
                                   x_in[pt * 128:(pt + 1) * 128, :]),
                       waits=[(s_v, consumed[b])], inc=s_x, inc_amt=16,
                       tag=f"p1.ld{pt}")
                mx_ld = p.mark(s_x)
                for i in range(4):
                    p.emit("vector", _fn("bn_stats", vec(V_BNST + 6 * i, 6),
                                         xt[b][:, 512 * i:512 * (i + 1)]),
                           waits=[(s_x, mx_ld)] if i == 0 else [],
                           tag=f"p1.bn{pt}.{i}")
                gap()
                p.emit("vector", _fn("bn_aggr", vec(V_STAT, 2), vec(V_BNST, 24)),
                       inc=s_v, tag=f"p1.ag{pt}")
                mv = p.mark(s_v)
                p.emit("scalar", _fn("activation", vec(V_LNT), vec(V_STAT + 1),
                                     AF.Ln, bias=epsb[:, 0:1], scale=1.0),
                       waits=[(s_v, mv)], tag=f"p1.ln{pt}")
                p.emit("scalar", _fn("activation", vec(V_RSTD), vec(V_LNT),
                                     AF.Exp, bias=0.0, scale=-0.5),
                       inc=s_a, tag=f"p1.ex{pt}")
                ma = p.mark(s_a)
                p.emit("vector", _fn("tensor_scalar", slot(pt)[:], xt[b][:],
                                     vec(V_STAT), vec(V_RSTD),
                                     ALU.subtract, ALU.mult),
                       waits=[(s_a, ma)], inc=s_v, tag=f"p1.nm{pt}")
                consumed[b] = p.mark(s_v)

        def qkv_pass(kind, ph):
            xnf4 = ph.enter_context(
                nc.sbuf_tensor(f"xnf4{kind}", [128, NFT * 512], BF16))
            cosb = ph.enter_context(nc.sbuf_tensor(f"cosb{kind}", [128, 512], BF16))
            sinb = ph.enter_context(nc.sbuf_tensor(f"sinb{kind}", [128, 512], BF16))
            rt1 = ph.enter_context(nc.sbuf_tensor(f"rt1{kind}", [128, 512], F32))
            rt2 = ph.enter_context(nc.sbuf_tensor(f"rt2{kind}", [128, 512], F32))
            kraw = ph.enter_context(nc.sbuf_tensor(f"kraw{kind}", [128, 512], BF16))
            sqb = ph.enter_context(nc.sbuf_tensor(f"sqb{kind}", [128, 512], BF16))

            if kind == "kv":
                for f in range(NFT):
                    p.emit("gpsimd", _fn("dma_start",
                                         rw[:, f * 512:(f + 1) * 512],
                                         wk_in[f * 128:(f + 1) * 128, :]),
                           inc=s_w, inc_amt=16, tag=f"p2.wk{f}")
                for f in range(NFT):
                    p.emit("gpsimd", _fn("dma_start",
                                         rw[:, 8192 + f * 512:8192 + (f + 1) * 512],
                                         wv_in[f * 128:(f + 1) * 128, :]),
                           inc=s_w, inc_amt=16, tag=f"p2.wv{f}")
            else:
                for f in range(NFT):
                    p.emit("gpsimd", _fn("dma_start",
                                         rw[:, f * C:(f + 1) * C],
                                         wq_in[f * 128:(f + 1) * 128, :]),
                           inc=s_w, inc_amt=16, tag=f"p3.wq{f}")
            mw = p.mark(s_w)

            m_xnf_ev = 0
            m_rope = 0
            m_cons_E = 0
            m_mm_done = 0
            m_csfree = 0
            for g in range(4):
                csrc = cosw if kind == "kv" else cosw2
                ssrc = sinw if kind == "kv" else sinw2
                p.emit("sync", _fn("dma_start", cosb[:],
                                   csrc[:, g * 512:(g + 1) * 512]),
                       waits=[(s_v, m_csfree)], inc=s_x, inc_amt=16,
                       tag=f"{kind}.cos{g}")
                p.emit("sync", _fn("dma_start", sinb[:],
                                   ssrc[:, g * 512:(g + 1) * 512]),
                       inc=s_x, inc_amt=16, tag=f"{kind}.sin{g}")
                m_cs = p.mark(s_x)

                ev = []
                prev_ev = m_xnf_ev
                for pl in range(4):
                    pt = g * 4 + pl
                    for f in range(NFT):
                        p.emit("tensor",
                               _fn("transpose",
                                   psE_bf[:, f * 128:(f + 1) * 128],
                                   slot(pt)[:, f * 128:(f + 1) * 128],
                                   ident[:]),
                               waits=([(s_v, prev_ev), (s_v, m_cons_E)]
                                      if f == 0 else []),
                               inc=(s_pe if f == NFT - 1 else None),
                               tag=f"{kind}.T{pt}.{f}")
                    mtp = p.mark(s_pe)
                    dst = xnf4[:].rearrange("q (f t) -> q f t", t=512)
                    dst = dst[:, :, pl * 128:(pl + 1) * 128]
                    src = psE_bf[:].rearrange("q (f t) -> q f t", t=128)
                    p.emit("vector", _fn("tensor_copy", dst, src),
                           waits=[(s_pe, mtp), (s_pe, m_mm_done)], inc=s_v,
                           tag=f"{kind}.ev{pt}")
                    ev.append(p.mark(s_v))
                    prev_ev = ev[-1]
                m_xnf_ev = ev[-1]

                def xnf_t(f, width=512, off=0):
                    return xnf4[:, f * 512 + off: f * 512 + off + width]

                if kind == "kv":
                    def wk_t(f, kvh):
                        return rw[:, f * 512 + kvh * 128: f * 512 + (kvh + 1) * 128]

                    def wv_t(f):
                        return rw[:, 8192 + f * 512: 8192 + (f + 1) * 512]

                    for kvh in range(NKV):
                        for f in range(NFT):
                            p.emit("tensor",
                                   _fn("matmul",
                                       psO[:, kvh * 512:(kvh + 1) * 512],
                                       wk_t(f, kvh), xnf_t(f),
                                       start=(f == 0), stop=(f == NFT - 1)),
                                   waits=([(s_v, m_xnf_ev), (s_w, mw),
                                           (s_v, m_rope)]
                                          if (kvh == 0 and f == 0) else []),
                                   inc=(s_pe if f == NFT - 1 else None),
                                   tag=f"p2.K{g}.{kvh}.{f}")
                    mk = p.mark(s_pe)
                    for pl in range(4):
                        for f in range(NFT):
                            p.emit("tensor",
                                   _fn("matmul",
                                       psE[:, pl * 512:(pl + 1) * 512],
                                       xnf_t(f, 128, pl * 128), wv_t(f),
                                       start=(f == 0), stop=(f == NFT - 1)),
                                   waits=([(s_v, m_cons_E)]
                                          if (pl == 0 and f == 0) else []),
                                   inc=(s_pe if f == NFT - 1 else None),
                                   tag=f"p2.V{g}.{pl}.{f}")
                    m_mm_done = p.mark(s_pe)
                    mvv = m_mm_done

                    for kvh in range(NKV):
                        ps = psO[:, kvh * 512:(kvh + 1) * 512]
                        kd = K_kvh(kvh)[:, g * 512:(g + 1) * 512]
                        p.emit("vector", _fn("tensor_copy", kraw[:], ps),
                               waits=[(s_pe, mk), (s_x, m_cs),
                                      (s_pe, p.mark(s_pe))],
                               inc=s_v, tag=f"p2.kr{g}.{kvh}")
                        mkraw = p.mark(s_v)
                        p.emit("tensor", _fn("matmul", ps, rmat[:], kraw[:],
                                             start=True, stop=True),
                               waits=[(s_v, mkraw)], inc=s_pe,
                               tag=f"p2.rot{g}.{kvh}")
                        mrot = p.mark(s_pe)
                        p.emit("vector", _fn("tensor_tensor", rt1[:], kraw[:],
                                             cosb[:], ALU.mult),
                               tag=f"p2.r{g}.{kvh}.a")
                        p.emit("vector", _fn("tensor_tensor", rt2[:], ps,
                                             sinb[:], ALU.mult),
                               waits=[(s_pe, mrot)],
                               tag=f"p2.r{g}.{kvh}.b")
                        p.emit("vector", _fn("tensor_tensor", kd, rt1[:],
                                             rt2[:], ALU.add),
                               inc=s_v, tag=f"p2.r{g}.{kvh}.c")
                        p.emit("vector", _fn("tensor_tensor", sqb[:], kraw[:],
                                             kraw[:], ALU.mult),
                               inc=s_v, tag=f"p2.sq{g}.{kvh}")
                        msq = p.mark(s_v)
                        p.emit("tensor", _fn("matmul", ps[0:1, :],
                                             onesb[:, 0:1], sqb[:],
                                             start=True, stop=True),
                               waits=[(s_v, msq)], inc=s_pe,
                               tag=f"p2.nm{g}.{kvh}")
                        mnm = p.mark(s_pe)
                        p.emit("vector", _fn("reduce_max",
                                             vecs[0:1, V_NTMP:V_NTMP + 1],
                                             ps[0:1, :], AX.X),
                               waits=[(s_pe, mnm)], tag=f"p2.nr{g}.{kvh}")
                        gap()
                        ca = V_KN + kvh * 2 + (g % 2)
                        cb = V_KN + kvh * 2 + ((g + 1) % 2)
                        if g == 0:
                            p.emit("vector", _fn("tensor_copy",
                                                 vecs[0:1, cb:cb + 1],
                                                 vecs[0:1, V_NTMP:V_NTMP + 1]),
                                   inc=s_v, tag=f"p2.nc{g}.{kvh}")
                        else:
                            p.emit("vector", _fn("tensor_tensor",
                                                 vecs[0:1, cb:cb + 1],
                                                 vecs[0:1, ca:ca + 1],
                                                 vecs[0:1, V_NTMP:V_NTMP + 1],
                                                 ALU.max),
                                   inc=s_v, tag=f"p2.nc{g}.{kvh}")
                    m_rope = p.mark(s_v)
                    m_csfree = m_rope
                    for pl in range(4):
                        p.emit("vector", _fn("tensor_copy", V_pt(g * 4 + pl)[:],
                                             psE[:, pl * 512:(pl + 1) * 512]),
                               waits=[(s_pe, mvv)] if pl == 0 else [],
                               inc=(s_v if pl == 3 else None),
                               tag=f"p2.vev{g}.{pl}")
                    m_cons_E = p.mark(s_v)
                else:
                    def wq_t(f, h):
                        return rw[:, f * C + h * 128: f * C + (h + 1) * 128]

                    rq_r = rq[:].rearrange("q (s e) -> q s e", e=C)
                    m_ropeq = [0, 0, 0, 0]
                    for h in range(NH):
                        qtr = h % 4
                        ps = psO[:, qtr * 512:(qtr + 1) * 512]
                        for f in range(NFT):
                            p.emit("tensor",
                                   _fn("matmul", ps, wq_t(f, h), xnf_t(f),
                                       start=(f == 0), stop=(f == NFT - 1)),
                                   waits=([(s_v, m_xnf_ev), (s_w, mw)]
                                          if (h == 0 and f == 0) else []) +
                                         ([(s_v, m_ropeq[qtr])] if f == 0 else []),
                                   inc=(s_pe if f == NFT - 1 else None),
                                   tag=f"p3.Q{g}.{h}.{f}")
                        mq = p.mark(s_pe)
                        if h == NH - 1:
                            m_mm_done = mq
                        dst = rq_r[:, 4 * g:4 * g + 4, h * 128:(h + 1) * 128]
                        p.emit("vector", _fn("tensor_copy", kraw[:], ps),
                               waits=[(s_pe, mq)] + ([(s_x, m_cs)]
                                                     if h == 0 else []),
                               inc=s_v, tag=f"p3.kr{g}.{h}")
                        mkraw = p.mark(s_v)
                        p.emit("tensor", _fn("matmul", ps, rmat[:], kraw[:],
                                             start=True, stop=True),
                               waits=[(s_v, mkraw)], inc=s_pe,
                               tag=f"p3.rot{g}.{h}")
                        mrot = p.mark(s_pe)
                        p.emit("vector", _fn("tensor_tensor", rt1[:], kraw[:],
                                             cosb[:], ALU.mult),
                               tag=f"p3.r{g}.{h}.a")
                        p.emit("vector", _fn("tensor_tensor", rt2[:], ps,
                                             sinb[:], ALU.mult),
                               waits=[(s_pe, mrot)],
                               tag=f"p3.r{g}.{h}.b")
                        p.emit("vector",
                               _fn("tensor_tensor", dst,
                                   rt1[:].rearrange("q (s e) -> q s e", e=128),
                                   rt2[:].rearrange("q (s e) -> q s e", e=128),
                                   ALU.add),
                               inc=s_v, tag=f"p3.r{g}.{h}.c")
                        p.emit("vector", _fn("tensor_tensor", sqb[:], kraw[:],
                                             kraw[:], ALU.mult),
                               inc=s_v, tag=f"p3.sq{g}.{h}")
                        msq = p.mark(s_v)
                        p.emit("tensor", _fn("matmul", ps[0:1, :],
                                             onesb[:, 0:1], sqb[:],
                                             start=True, stop=True),
                               waits=[(s_v, msq)], inc=s_pe,
                               tag=f"p3.nm{g}.{h}")
                        mnm = p.mark(s_pe)
                        p.emit("vector", _fn("reduce_max",
                                             vecs[0:1, V_NTMP:V_NTMP + 1],
                                             ps[0:1, :], AX.X),
                               waits=[(s_pe, mnm)], tag=f"p3.nr{g}.{h}")
                        gap()
                        ca = V_QN + h * 2 + (g % 2)
                        cb = V_QN + h * 2 + ((g + 1) % 2)
                        if g == 0:
                            p.emit("vector", _fn("tensor_copy",
                                                 vecs[0:1, cb:cb + 1],
                                                 vecs[0:1, V_NTMP:V_NTMP + 1]),
                                   inc=s_v, tag=f"p3.nc{g}.{h}")
                        else:
                            p.emit("vector", _fn("tensor_tensor",
                                                 vecs[0:1, cb:cb + 1],
                                                 vecs[0:1, ca:ca + 1],
                                                 vecs[0:1, V_NTMP:V_NTMP + 1],
                                                 ALU.max),
                                   inc=s_v, tag=f"p3.nc{g}.{h}")
                        m_ropeq[qtr] = p.mark(s_v)
                    m_rope = p.mark(s_v)
                    m_csfree = m_rope

        if phases >= 2:
            p.phase()
            with ExitStack() as ph:
                qkv_pass("kv", ph)
        if phases >= 3:
            p.phase()
            with ExitStack() as ph:
                qkv_pass("q", ph)

        if phases >= 4:
            p.phase()
            with ExitStack() as ph:
                xt4 = ph.enter_context(nc.sbuf_tensor("p4x", [128, C], BF16))
                x2b = ph.enter_context(nc.sbuf_tensor("p4x2b", [128, C], F32))
                pbf2 = ph.enter_context(
                    nc.sbuf_tensor("p4pb", [128, 2 * T], BF16))
                ptsb = ph.enter_context(
                    nc.sbuf_tensor("p4pt", [128, 2 * T], BF16))
                ysb = ph.enter_context(nc.sbuf_tensor("p4y", [128, C], BF16))
                xn2tm = ph.enter_context(nc.sbuf_tensor("p4x2", [128, C], BF16))

                for f in range(NFT):
                    p.emit("gpsimd", _fn("dma_start", rw[:, f * C:(f + 1) * C],
                                         wp_in[f * 128:(f + 1) * 128, :]),
                           inc=s_w, inc_amt=16, tag=f"p4.wp{f}")
                mwp = p.mark(s_w)

                def wp_t(f, oc):
                    return rw[:, f * C + oc * 512: f * C + (oc + 1) * 512]

                for h in range(NH):
                    p.emit("vector", _fn("tensor_tensor",
                                         vecs[0:1, V_CMX:V_CMX + 1],
                                         vecs[0:1, V_QN + h * 2:
                                              V_QN + h * 2 + 1],
                                         vecs[0:1, V_KN + (h // 4) * 2:
                                              V_KN + (h // 4) * 2 + 1],
                                         ALU.mult),
                           inc=s_v, tag=f"p4.bb{h}")
                    mbb = p.mark(s_v)
                    p.emit("scalar", _fn("activation",
                                         vecs[0:1, V_CMX + 1:V_CMX + 2],
                                         vecs[0:1, V_CMX:V_CMX + 1], AF.Ln,
                                         bias=0.0, scale=SCALE * SCALE),
                           waits=[(s_v, mbb)], tag=f"p4.bl{h}")
                    p.emit("scalar", _fn("activation",
                                         vecs[0:1, V_NTMP:V_NTMP + 1],
                                         vecs[0:1, V_CMX + 1:V_CMX + 2],
                                         AF.Exp, bias=0.0, scale=0.5),
                           inc=s_a, tag=f"p4.be{h}")
                    mbe = p.mark(s_a)
                    p.emit("tensor", _fn("matmul", psE[:, 0:1],
                                         onesf[0:1, 0:128],
                                         vecs[0:1, V_NTMP:V_NTMP + 1],
                                         start=True, stop=True),
                           waits=[(s_a, mbe), (s_v, p.mark(s_v))],
                           inc=s_pe, tag=f"p4.bm{h}")
                    mbm = p.mark(s_pe)
                    p.emit("vector", _fn("tensor_scalar_mul",
                                         vec(V_B16 + h), psE[:, 0:1], -1.0),
                           waits=[(s_pe, mbm)], inc=s_v, tag=f"p4.bn{h}")
                m_bias16 = p.mark(s_v)
                p.waitop("tensor", [(s_v, m_bias16)], tag="p4.biasbar")

                m_exp = 0
                m_norm = 0
                m_tp = 0
                m_tp2 = [0, 0]
                m_ptev = 0
                m_pv = 0
                m_yev = [0, 0]
                m_proj = 0
                m_zev = 0
                m_x2 = 0
                m_xnt = 0
                m_m2cp = 0
                m_slot_ev = 0

                def head_pe(qt, h):
                    nk = qt + 1
                    L = nk * 128
                    nch = (L + 511) // 512
                    kvh = h // 4
                    first = [(s_a, m_exp), (s_v, m_zev)]
                    marks = []
                    for c_ in range(nch):
                        wdt = min(512, L - c_ * 512)
                        last = (c_ == nch - 1)
                        p.emit("tensor",
                               _fn("matmul", psE[:, c_ * 512:c_ * 512 + wdt],
                                   slot(qt)[:, h * 128:(h + 1) * 128],
                                   K_kvh(kvh)[:, c_ * 512:c_ * 512 + wdt],
                                   start=True, stop=not last),
                               waits=first if c_ == 0 else [],
                               tag=f"p4.S{qt}.{h}.{c_}")
                        first = []
                    p.emit("tensor",
                           _fn("matmul", psE[:, qt * 128:L], ident[:], dmask[:],
                               start=False, stop=True),
                           inc=s_pe, tag=f"p4.M{qt}.{h}")
                    marks.append(p.mark(s_pe))
                    return marks

                def head_va(qt, h, marks):
                    nonlocal m_exp, m_norm
                    L = (qt + 1) * 128
                    pb = pbf2[:, (h % 2) * T:(h % 2) * T + L]
                    p.emit("scalar", _fn("activation", pb, psE[:, 0:L],
                                         AF.Exp, bias=vec(V_B16 + h), scale=1.0,
                                         accum_out=vec(V_SSUM)),
                           waits=[(s_pe, marks[-1]), (s_pe, m_tp2[h % 2])],
                           inc=s_a, tag=f"p4.ex{qt}.{h}")
                    m_exp = p.mark(s_a)
                    p.emit("vector", _fn("reciprocal", vec(V_RSUM), vec(V_SSUM)),
                           waits=[(s_a, m_exp)], tag=f"p4.rs{qt}.{h}")
                    gap()
                    p.emit("vector", _fn("tensor_scalar_mul", pb, pb,
                                         vec(V_RSUM)),
                           inc=s_v, tag=f"p4.nm{qt}.{h}")
                    m_norm = p.mark(s_v)

                def head_tail_pe(qt, h):
                    nonlocal m_tp, m_pv, m_ptev, m_yev
                    nk = qt + 1
                    kvh = h // 4
                    pr = h // 2
                    hh = h % 2
                    pt_r = ptsb[:].rearrange("q (kt e) -> q kt e", e=256)
                    pbh = pbf2[:, (h % 2) * T:(h % 2) * T + T]
                    for kt in range(nk):
                        p.emit("tensor",
                               _fn("transpose",
                                   psO_bf[:, kt * 128:(kt + 1) * 128],
                                   pbh[:, kt * 128:(kt + 1) * 128], ident[:]),
                               waits=([(s_v, m_norm), (s_v, m_ptev),
                                       (s_v, m_slot_ev), (s_v, m_x2)]
                                      if kt == 0 else []),
                               inc=(s_pe if kt == nk - 1 else None),
                               tag=f"p4.T{qt}.{h}.{kt}")
                    m_tp = p.mark(s_pe)
                    m_tp2[h % 2] = m_tp
                    dst = pt_r[:, 0:nk, hh * 128:(hh + 1) * 128]
                    src_ = psO_bf[:, 0:nk * 128].rearrange(
                        "q (kt e) -> q kt e", e=128)
                    p.emit("vector", _fn("tensor_copy", dst, src_),
                           waits=[(s_pe, m_tp), (s_pe, m_pv)], inc=s_v,
                           tag=f"p4.PT{qt}.{h}")
                    m_ptev = p.mark(s_v)
                    if hh == 1:
                        ycol = (pr % 2) * 512
                        for kt in range(nk):
                            p.emit("tensor",
                                   _fn("matmul", psO[:, ycol:ycol + 256],
                                       V_pt(kt)[:, kvh * 128:(kvh + 1) * 128],
                                       ptsb[:, kt * 256:(kt + 1) * 256],
                                       start=(kt == 0), stop=(kt == nk - 1)),
                                   waits=([(s_v, m_ptev),
                                           (s_v, m_yev[pr % 2]),
                                           (s_v, m_x2)]
                                          if kt == 0 else []),
                                   inc=(s_pe if kt == nk - 1 else None),
                                   tag=f"p4.PV{qt}.{pr}.{kt}")
                        m_pv = p.mark(s_pe)
                        p.emit("vector",
                               _fn("tensor_copy",
                                   ysb[:, pr * 256:(pr + 1) * 256],
                                   psO[:, ycol:ycol + 256]),
                               waits=[(s_pe, m_pv), (s_pe, m_proj)],
                               inc=s_v, tag=f"p4.ye{qt}.{pr}")
                        m_yev[pr % 2] = p.mark(s_v)

                pending_x2 = None

                def flush_x2():
                    nonlocal m_xnt, m_slot_ev, pending_x2
                    if pending_x2 is None:
                        return
                    fqt, fmxn = pending_x2
                    pending_x2 = None
                    for f in range(NFT):
                        p.emit("tensor",
                               _fn("transpose",
                                   psO_bf[:, f * 128:(f + 1) * 128],
                                   xn2tm[:, f * 128:(f + 1) * 128], ident[:]),
                               waits=([(s_v, fmxn), (s_v, m_ptev)]
                                      if f == 0 else []),
                               inc=(s_pe if f == NFT - 1 else None),
                               tag=f"p4.xT{fqt}.{f}")
                    m_xnt = p.mark(s_pe)
                    p.emit("vector", _fn("tensor_copy", slot(fqt)[:],
                                         psO_bf[:]),
                           waits=[(s_pe, m_xnt)], inc=s_v, tag=f"p4.xe{fqt}")
                    m_slot_ev = p.mark(s_v)

                for qt in range(NPT):
                    p.emit("gpsimd", _fn("dma_start", xt4[:],
                                         x_in[qt * 128:(qt + 1) * 128, :]),
                           waits=[(s_v, m_x2)], inc=s_w, inc_amt=16,
                           tag=f"p4.ldx{qt}")
                    mx4 = p.mark(s_w)
                    for h in range(NH):
                        marks = head_pe(qt, h)
                        if h == 2:
                            flush_x2()
                        if h > 0:
                            head_tail_pe(qt, h - 1)
                        head_va(qt, h, marks)
                    head_tail_pe(qt, NH - 1)
                    for f in range(NFT):
                        for oc in range(4):
                            p.emit("tensor",
                                   _fn("matmul", psE[:, oc * 512:(oc + 1) * 512],
                                       ysb[:, f * 128:(f + 1) * 128],
                                       wp_t(f, oc),
                                       start=(f == 0), stop=(f == NFT - 1)),
                                   waits=([(s_v, m_yev[0]), (s_v, m_yev[1]),
                                           (s_a, m_exp), (s_w, mwp)]
                                          if (oc == 0 and f == 0) else []),
                                   inc=(s_pe if (oc == 3 and f == NFT - 1)
                                        else None),
                                   tag=f"p4.pj{qt}.{oc}.{f}")
                    m_proj = p.mark(s_pe)
                    for i in range(4):
                        p.emit("vector", _fn("bn_stats", vec(V_BNST + 6 * i, 6),
                                             psE[:, 512 * i:512 * (i + 1)]),
                               waits=([(s_a, m_m2cp), (s_pe, m_proj)]
                                      if i == 0 else []),
                               tag=f"p4.zb{qt}.{i}")
                    gap()
                    p.emit("vector", _fn("bn_aggr", vec(V_STAT, 2),
                                         vec(V_BNST, 24)),
                           inc=s_v, tag=f"p4.za{qt}")
                    mzs = p.mark(s_v)
                    p.emit("scalar", _fn("activation", vec(V_LNT),
                                         vec(V_STAT + 1), AF.Ln,
                                         bias=epsb[:, 0:1], scale=1.0),
                           waits=[(s_v, mzs)], tag=f"p4.zl{qt}")
                    p.emit("scalar", _fn("activation", vec(V_RSTD), vec(V_LNT),
                                         AF.Exp, bias=0.0, scale=-0.5),
                           inc=s_a, tag=f"p4.zr{qt}")
                    mza = p.mark(s_a)
                    p.emit("vector", _fn("tensor_scalar", psO[:], psE[:],
                                         vec(V_STAT), vec(V_RSTD),
                                         ALU.subtract, ALU.mult),
                           waits=[(s_a, mza), (s_pe, m_pv)], inc=s_v,
                           tag=f"p4.zn{qt}")
                    m_zev = p.mark(s_v)
                    p.emit("vector", _fn("tensor_tensor", x2b[:], psO[:],
                                         xt4[:], ALU.add),
                           waits=[(s_w, mx4)], inc=s_v, tag=f"p4.x2{qt}")
                    m_x2 = p.mark(s_v)
                    for i in range(4):
                        p.emit("vector", _fn("bn_stats", vec(V_BNST + 6 * i, 6),
                                             x2b[:, 512 * i:512 * (i + 1)]),
                               tag=f"p4.b2{qt}.{i}")
                    gap()
                    p.emit("vector", _fn("bn_aggr", vec(V_STAT, 2),
                                         vec(V_BNST, 24)),
                           inc=s_v, tag=f"p4.a2{qt}")
                    m2s = p.mark(s_v)
                    p.emit("scalar", _fn("activation", vec(V_LNT),
                                         vec(V_STAT + 1), AF.Ln,
                                         bias=epsb[:, 0:1], scale=1.0),
                           waits=[(s_v, m2s)], tag=f"p4.l2{qt}")
                    p.emit("scalar", _fn("activation", vec(V_RSTD), vec(V_LNT),
                                         AF.Exp, bias=0.0, scale=-0.5),
                           tag=f"p4.r2{qt}")
                    p.emit("scalar", _fn("activation", vec(V_SD2 + qt),
                                         vec(V_LNT), AF.Exp, bias=0.0,
                                         scale=0.5),
                           tag=f"p4.s2{qt}")
                    p.emit("scalar", _fn("copy", vec(V_M2 + qt), vec(V_STAT)),
                           inc=s_a, tag=f"p4.m2{qt}")
                    m_m2cp = p.mark(s_a)
                    p.emit("vector", _fn("tensor_scalar", xn2tm[:], x2b[:],
                                         vec(V_STAT), vec(V_RSTD),
                                         ALU.subtract, ALU.mult),
                           waits=[(s_a, m_m2cp), (s_pe, m_xnt)], inc=s_v,
                           tag=f"p4.xn{qt}")
                    mxn = p.mark(s_v)
                    pending_x2 = (qt, mxn)
                flush_x2()

        if phases >= 5:
            preload_marks = {}
            f2marks = {}
            for ps_i in range(4):
                sl = [4 * ps_i + j for j in range(4)]
                if ps_i == 0:
                    p.phase()
                fm = f2marks.get(ps_i, {})
                with ExitStack() as ph:
                    h32 = [ph.enter_context(
                        nc.sbuf_tensor(f"h32{b_}_{ps_i}", [128, 2048], F32))
                        for b_ in range(2)]
                    sg = ph.enter_context(
                        nc.sbuf_tensor(f"sg_{ps_i}", [128, 2048], F32))
                    hsc4 = ph.enter_context(
                        nc.sbuf_tensor(f"hsc4_{ps_i}", [128, 4096], BF16))
                    aabn = ph.enter_context(
                        nc.sbuf_tensor(f"aabn_{ps_i}", [128, 1024], F32))
                    aab = aabn[:, 0:512]
                    nb = [aabn[:, 512 + i * 128:512 + (i + 1) * 128]
                          for i in range(4)]

                    def hscv(ic_, pt_):
                        o = (ic_ % 2) * 2048 + pt_ * 512
                        return hsc4[:, o:o + 512]

                    def half(s):
                        if s < 4:
                            return rkv[:, s * 4096:(s + 1) * 4096]
                        return wexs[:]

                    def wtile(L, f):
                        h = half((2 * L + (1 if f >= 8 else 0)) % 5)
                        return h[:, (f % 8) * 512:((f % 8) + 1) * 512]

                    def dma_half(L, k, waits):
                        ic, j = divmod(L, 2)
                        dst = half((2 * L + k) % 5)[:].rearrange(
                            "q (f i) -> q f i", i=512)
                        srcw = wg_in if j == 0 else wu_in
                        s_ = srcw[k * 1024:(k + 1) * 1024,
                                  ic * 512:(ic + 1) * 512]
                        s_ = s_.rearrange("(f q) i -> q f i", q=128)
                        p.emit("gpsimd", _fn("dma_start", dst, s_),
                               waits=waits, inc=s_w, inc_amt=16,
                               tag=f"f1.w{ps_i}.{L}.{k}")
                        return p.mark(s_w)

                    mld = {}
                    if ps_i in preload_marks:
                        mld[0], mld[1] = preload_marks[ps_i]
                        dma_half(2, 0, [])
                    else:
                        dma_half(0, 0, [])
                        mld[0] = dma_half(0, 1, [])
                        dma_half(1, 0, [])
                        mld[1] = dma_half(1, 1, [])
                        dma_half(2, 0, [])
                    m_silu = 0
                    m_hf = 0
                    m_hf_prev = 0
                    m_sp_par = [0, 0]
                    m_dj = 0
                    m_hst = 0
                    m_hsev = 0
                    m_hsev2 = [0, 0]
                    pend_hst = []
                    pend_sp = None

                    def emit_hst(ic, all_=False):
                        nonlocal m_hst, m_hsev
                        todo = [e for e in pend_hst
                                if all_ or e[0] <= ic - 2]
                        pend_hst[:] = [e for e in pend_hst
                                       if not (all_ or e[0] <= ic - 2)]
                        for (ric, pt_, msp) in todo:
                            par = pt_ % 2
                            bk = psO_bf0[:, par * 1024:(par + 1) * 1024]
                            for k in range(4):
                                p.emit("tensor",
                                       _fn("transpose",
                                           bk[:, k * 128:(k + 1) * 128],
                                           hscv(ric, pt_)[:, k * 128:
                                                          (k + 1) * 128],
                                           ident[:]),
                                       waits=([(s_v, msp),
                                               (s_v, m_hsev2[par])]
                                              if k == 0 else []),
                                       inc=(s_pe if k == 3 else None),
                                       tag=f"f1.hT{ps_i}.{ric}.{pt_}.{k}")
                            m_hst = p.mark(s_pe)
                            rw_r = rw[:].rearrange("q (it t) -> q it t", t=512)
                            dstv = rw_r[:, ric * 4:(ric + 1) * 4,
                                        pt_ * 128:(pt_ + 1) * 128]
                            srcv = bk[:, 0:512].rearrange(
                                "q (k t) -> q k t", t=128)
                            p.emit("vector", _fn("tensor_copy", dstv, srcv),
                                   waits=[(s_pe, m_hst)], inc=s_v,
                                   tag=f"f1.he{ps_i}.{ric}.{pt_}")
                            m_hsev2[par] = p.mark(s_v)
                            m_hsev = m_hsev2[par]

                    def emit_sparsify():
                        nonlocal pend_sp, m_dj, m_sp_par
                        if pend_sp is None:
                            return
                        ric, mhf = pend_sp
                        pend_sp = None

                        def jv(t_, j):
                            v3 = t_.rearrange("q (g j) -> q g j", j=4)
                            return v3[:, :, j]

                        for pt_ in range(4):
                            hv = h32[ric % 2][:, pt_ * 512:(pt_ + 1) * 512]
                            p.emit("scalar", _fn("activation", aab, hv, AF.Abs),
                                   waits=[(s_v, mhf), (s_v, m_dj)],
                                   inc=s_a, tag=f"f1.ab{ps_i}.{ric}.{pt_}")
                            m_ab = p.mark(s_a)
                            gap()
                            p.emit("vector", _fn("tensor_tensor", nb[0][:],
                                                 jv(aab, 0), jv(aab, 1),
                                                 ALU.min),
                                   waits=[(s_a, m_ab)],
                                   tag=f"f1.n1{ps_i}.{ric}.{pt_}")
                            p.emit("vector", _fn("tensor_tensor", nb[1][:],
                                                 jv(aab, 0), jv(aab, 1),
                                                 ALU.max),
                                   tag=f"f1.n2{ps_i}.{ric}.{pt_}")
                            p.emit("vector", _fn("tensor_tensor", nb[2][:],
                                                 jv(aab, 2), jv(aab, 3),
                                                 ALU.min),
                                   tag=f"f1.n3{ps_i}.{ric}.{pt_}")
                            p.emit("vector", _fn("tensor_tensor", nb[3][:],
                                                 jv(aab, 2), jv(aab, 3),
                                                 ALU.max),
                                   tag=f"f1.n4{ps_i}.{ric}.{pt_}")
                            p.emit("vector", _fn("tensor_tensor", nb[0][:],
                                                 nb[0][:], nb[2][:], ALU.max),
                                   tag=f"f1.n5{ps_i}.{ric}.{pt_}")
                            p.emit("vector", _fn("tensor_tensor", nb[1][:],
                                                 nb[1][:], nb[3][:], ALU.min),
                                   tag=f"f1.n6{ps_i}.{ric}.{pt_}")
                            gap()
                            p.emit("vector", _fn("tensor_tensor", nb[0][:],
                                                 nb[0][:], nb[1][:], ALU.min),
                                   tag=f"f1.n7{ps_i}.{ric}.{pt_}")
                            gap()
                            for j in range(4):
                                p.emit("vector",
                                       _fn("tensor_tensor", jv(aab, j),
                                           jv(aab, j), nb[0][:], ALU.subtract),
                                       tag=f"f1.d{ps_i}.{ric}.{pt_}.{j}")
                            for j in range(4):
                                p.emit("vector",
                                       _fn("scalar_tensor_tensor",
                                           jv(hscv(ric, pt_)[:], j),
                                           jv(aab, j), 0.0,
                                           jv(hv, j), ALU.is_gt, ALU.mult),
                                       waits=([(s_pe, m_hst)] if j == 0
                                              else []),
                                       inc=(s_v if j == 3 else None),
                                       tag=f"f1.m{ps_i}.{ric}.{pt_}.{j}")
                            m_sp = p.mark(s_v)
                            m_dj = m_sp
                            pend_hst.append((ric, pt_, m_sp))
                        m_sp_par[ric % 2] = m_dj

                    for ic in range(16):
                        Lg, Lu = 2 * ic, 2 * ic + 1
                        for pl in range(4):
                            for f in range(NFT):
                                w0 = ([(s_w, mld.get(Lg, 0)), (s_a, m_silu)]
                                      if (pl == 0 and f == 0) else [])
                                if pl == 0 and f == 0 and ic == 0 and fm:
                                    w0 = w0 + [(s_v, fm["psfree"])]
                                p.emit("tensor",
                                       _fn("matmul",
                                           psE[:, pl * 512:(pl + 1) * 512],
                                           slot(sl[pl])[:, f * 128:(f + 1) * 128],
                                           wtile(Lg, f),
                                           start=(f == 0), stop=(f == NFT - 1)),
                                       waits=w0,
                                       inc=(s_pe if (pl == 3 and f == NFT - 1)
                                            else None),
                                       tag=f"f1.g{ps_i}.{ic}.{pl}.{f}")
                        m_gd = p.mark(s_pe)
                        if Lg + 2 < 32:
                            dma_half(Lg + 2, 1, [(s_pe, m_gd)])
                            mld[Lg + 2] = p.mark(s_w)
                        if Lg + 3 < 32:
                            dma_half(Lg + 3, 0, [])
                        emit_hst(ic)
                        for pl in range(4):
                            for f in range(NFT):
                                p.emit("tensor",
                                       _fn("matmul",
                                           psO[:, pl * 512:(pl + 1) * 512],
                                           slot(sl[pl])[:, f * 128:(f + 1) * 128],
                                           wtile(Lu, f),
                                           start=(f == 0), stop=(f == NFT - 1)),
                                       waits=(([(s_w, mld.get(Lu, 0)),
                                                (s_v, m_hsev), (s_v, m_hf)] +
                                               ([(s_v, fm["psfree"])]
                                                if (ic == 0 and fm) else []))
                                              if (pl == 0 and f == 0) else []),
                                       inc=(s_pe if (pl == 3 and f == NFT - 1)
                                            else None),
                                       tag=f"f1.u{ps_i}.{ic}.{pl}.{f}")
                        m_umm = p.mark(s_pe)
                        if Lu + 2 < 32:
                            dma_half(Lu + 2, 1, [(s_pe, m_umm)])
                            mld[Lu + 2] = p.mark(s_w)
                        if Lu + 3 < 32:
                            dma_half(Lu + 3, 0, [])
                        p.emit("scalar", _fn("activation", sg[:],
                                             psE[:, 0:2048], AF.Silu),
                               waits=[(s_pe, m_gd), (s_v, m_hf)] +
                                     ([(s_pe, fm["mmlast"]),
                                       (s_out, fm["stores"]),
                                       (s_v, fm["psfree"])]
                                      if (ic == 0 and fm) else []),
                               inc=s_a, tag=f"f1.si{ps_i}.{ic}")
                        m_silu = p.mark(s_a)
                        hd_ = h32[ic % 2][:].rearrange("q (pt e) -> q pt e",
                                                       e=512)
                        sg3 = sg[:].rearrange("q (pt e) -> q pt e", e=512)
                        pu3 = psO[:].rearrange("q (pt e) -> q pt e", e=512)
                        p.emit("vector", _fn("tensor_tensor", hd_, sg3, pu3,
                                             ALU.mult),
                               waits=[(s_a, m_silu), (s_pe, m_umm),
                                      (s_v, m_sp_par[ic % 2])] +
                                     ([(s_out, fm["stores"])]
                                      if (ic == 0 and fm) else []),
                               inc=s_v, tag=f"f1.h{ps_i}.{ic}")
                        m_hf_prev = m_hf
                        m_hf = p.mark(s_v)
                        emit_sparsify()
                        pend_sp = (ic, m_hf)
                    emit_sparsify()
                    emit_hst(99, all_=True)
                    if ps_i + 1 < 4:
                        m_lastu = p.mark(s_pe)
                        sl2 = [4 * (ps_i + 1) + j for j in range(4)]

                        def dma_half2(L, k, waits):
                            ic_, j_ = divmod(L, 2)
                            dst = half((2 * L + k) % 5)[:].rearrange(
                                "q (f i) -> q f i", i=512)
                            srcw = wg_in if j_ == 0 else wu_in
                            s_ = srcw[k * 1024:(k + 1) * 1024,
                                      ic_ * 512:(ic_ + 1) * 512]
                            s_ = s_.rearrange("(f q) i -> q f i", q=128)
                            p.emit("gpsimd", _fn("dma_start", dst, s_),
                                   waits=waits, inc=s_w, inc_amt=16,
                                   tag=f"f1.pw{ps_i + 1}.{L}.{k}")
                            return p.mark(s_w)

                        dma_half2(0, 0, [(s_pe, m_lastu)])
                        mA = dma_half2(0, 1, [])
                        dma_half2(1, 0, [])
                        mB = dma_half2(1, 1, [])
                        preload_marks[ps_i + 1] = (mA, mB)

                p.phase()
                with ExitStack() as ph:
                    fbf = ph.enter_context(
                        nc.sbuf_tensor(f"fbf_{ps_i}", [128, 4 * C], BF16))
                    wd0 = ph.enter_context(
                        nc.sbuf_tensor(f"wd0_{ps_i}", [128, 4096], BF16))
                    wd1 = ph.enter_context(
                        nc.sbuf_tensor(f"wd1_{ps_i}", [128, 4096], BF16))
                    fo1 = ph.enter_context(
                        nc.sbuf_tensor(f"fo1_{ps_i}", [128, 512], F32))
                    fo3 = ph.enter_context(
                        nc.sbuf_tensor(f"fo3_{ps_i}", [128, 512], F32))
                    wdsl = [wd0, wd1, wexs]

                    def hsf_t(it, pl):
                        return rw[:, it * 512 + pl * 128: it * 512 + (pl + 1) * 128]

                    def dma_wd(os_, ic4, waits):
                        a = (os_ * 8 + ic4) % 3
                        dst = wdsl[a][:].rearrange("q (it o) -> q it o", o=512)
                        src = wd_in[ic4 * 1024:(ic4 + 1) * 1024,
                                    os_ * 512:(os_ + 1) * 512]
                        src = src.rearrange("(it q) o -> q it o", q=128)
                        p.emit("gpsimd", _fn("dma_start", dst, src),
                               waits=waits, inc=s_w, inc_amt=16,
                               tag=f"f2.wd{ps_i}.{os_}.{ic4}")
                        return p.mark(s_w)

                    m_fev2 = [0, 0]
                    m_wmm = {}
                    mw2carry = {}
                    for os_ in range(4):
                        mw2 = {}
                        if os_ == 0:
                            mw2[0] = dma_wd(os_, 0, [])
                            mw2[1] = dma_wd(os_, 1, [])
                            mw2[2] = dma_wd(os_, 2, [])
                        else:
                            for k in list(mw2carry):
                                mw2[k - 8] = mw2carry[k]
                            mw2carry.clear()
                        acc = psE if os_ % 2 == 0 else psO
                        for ic4 in range(8):
                            a = (os_ * 8 + ic4) % 3
                            for pl in range(4):
                                for it in range(8):
                                    itg = ic4 * 8 + it
                                    p.emit("tensor",
                                           _fn("matmul",
                                               acc[:, pl * 512:(pl + 1) * 512],
                                               hsf_t(itg, pl),
                                               wdsl[a][:, it * 512:(it + 1) * 512],
                                               start=(ic4 == 0 and it == 0),
                                               stop=(ic4 == 7 and it == 7)),
                                           waits=([(s_w, mw2[ic4]),
                                                   (s_v, m_fev2[os_ % 2])]
                                                  if (pl == 0 and it == 0)
                                                  else []),
                                           inc=(s_pe if (pl == 3 and it == 7)
                                                else None),
                                           tag=f"f2.m{ps_i}.{os_}.{ic4}.{pl}")
                            m_wmm[(os_, ic4)] = p.mark(s_pe)
                            nxt = ic4 + 3
                            if os_ * 8 + nxt < 32:
                                m_ = dma_wd(os_ + nxt // 8, nxt % 8,
                                            [(s_pe, m_wmm[(os_, ic4)])])
                                if nxt < 8:
                                    mw2[nxt] = m_
                                else:
                                    mw2carry[nxt] = m_
                        mlast = m_wmm[(os_, 7)]
                        for pl in range(4):
                            p.emit("vector",
                                   _fn("bn_stats",
                                       vec(V_BNF + pl * 24 + os_ * 6, 6),
                                       acc[:, pl * 512:(pl + 1) * 512]),
                                   waits=[(s_pe, mlast)] if pl == 0 else [],
                                   tag=f"f2.bn{ps_i}.{os_}.{pl}")
                            p.emit("vector",
                                   _fn("tensor_copy",
                                       fbf[:, pl * C + os_ * 512:
                                           pl * C + (os_ + 1) * 512],
                                       acc[:, pl * 512:(pl + 1) * 512]),
                                   inc=(s_v if pl == 3 else None),
                                   tag=f"f2.fe{ps_i}.{os_}.{pl}")
                        m_fev2[os_ % 2] = p.mark(s_v)

                    if debug and ps_i == 3:
                        p.emit("sync", _fn("dma_start", dbg_fbf, fbf[:]),
                               waits=[(s_v, m_fev2[0]), (s_v, m_fev2[1])],
                               inc=s_out, inc_amt=16, tag="dbg.fbf")
                    m_store = p.mark(s_out)
                    for pl in range(4):
                        gap()
                        p.emit("vector", _fn("bn_aggr", vec(V_M3 + pl * 2, 2),
                                             vec(V_BNF + pl * 24, 24)),
                               waits=([(s_v, m_fev2[0]), (s_v, m_fev2[1])]
                                      if pl == 0 else []),
                               inc=s_v, tag=f"f2.ag{ps_i}.{pl}")
                    mag = p.mark(s_v)
                    for pl in range(4):
                        p.emit("scalar", _fn("activation", vec(V_LNT),
                                             vec(V_M3 + pl * 2 + 1), AF.Ln,
                                             bias=epsb[:, 0:1], scale=1.0),
                               waits=[(s_v, mag)] if pl == 0 else [],
                               tag=f"f2.l{ps_i}.{pl}")
                        p.emit("scalar", _fn("activation", vec(V_RSTD3 + pl),
                                             vec(V_LNT), AF.Exp, bias=0.0,
                                             scale=-0.5),
                               inc=s_a, tag=f"f2.r{ps_i}.{pl}")
                    mra = p.mark(s_a)
                    for pl in range(4):
                        p.emit("vector", _fn("tensor_tensor",
                                             vec(V_CMX + 4 + pl),
                                             vec(V_M3 + pl * 2),
                                             vec(V_RSTD3 + pl), ALU.mult),
                               waits=[(s_a, mra)] if pl == 0 else [],
                               tag=f"f2.c1{ps_i}.{pl}")
                    gap()
                    for pl in range(4):
                        p.emit("vector", _fn("tensor_tensor",
                                             vec(V_CMX + 4 + pl),
                                             vec(V_M2 + sl[pl]),
                                             vec(V_CMX + 4 + pl), ALU.subtract),
                               inc=(s_v if pl == 3 else None),
                               tag=f"f2.c2{ps_i}.{pl}")
                    mc2 = p.mark(s_v)
                    m_ps_free = [0, 0]
                    for pl in range(4):
                        par = pl % 2
                        bk = (psE if par == 0 else psO)[:, 0:1024].bitcast(BF16)
                        for f in range(NFT):
                            p.emit("tensor",
                                   _fn("transpose",
                                       bk[:, f * 128:(f + 1) * 128],
                                       slot(sl[pl])[:, f * 128:(f + 1) * 128],
                                       ident[:]),
                                   waits=([(s_v, m_ps_free[par]),
                                           (s_v, mc2)] if f == 0 else []),
                                   inc=(s_pe if f == NFT - 1 else None),
                                   tag=f"f2.xT{ps_i}.{pl}.{f}")
                        m_xnt5 = p.mark(s_pe)
                        for os_ in range(4):
                            cs = slice(os_ * 512, (os_ + 1) * 512)
                            p.emit("vector",
                                   _fn("tensor_scalar", fo1[:], bk[:, cs],
                                       vec(V_SD2 + sl[pl]),
                                       vec(V_CMX + 4 + pl),
                                       ALU.mult, ALU.add),
                                   waits=([(s_pe, m_xnt5), (s_out, m_store)]
                                          if os_ == 0 else
                                          [(s_out, m_store)]),
                                   tag=f"f2.o1{ps_i}.{pl}.{os_}")
                            p.emit("vector",
                                   _fn("scalar_tensor_tensor", fo3[:],
                                       fbf[:, pl * C + os_ * 512:
                                           pl * C + (os_ + 1) * 512],
                                       vec(V_RSTD3 + pl), fo1[:],
                                       ALU.mult, ALU.add),
                                   inc=s_v, tag=f"f2.o3{ps_i}.{pl}.{os_}")
                            mo = p.mark(s_v)
                            r0 = sl[pl] * 128
                            p.emit("sync", _fn("dma_start",
                                               out[r0:r0 + 128, cs], fo3[:]),
                                   waits=[(s_v, mo)], inc=s_out, inc_amt=16,
                                   tag=f"f2.st{ps_i}.{pl}.{os_}")
                            m_store = p.mark(s_out)
                        m_ps_free[par] = p.mark(s_v)
                    f2marks[ps_i + 1] = {"mmlast": m_wmm[(3, 7)],
                                         "psfree": p.mark(s_v),
                                         "stores": p.mark(s_out)}

            p.phase()
            p.waitop("sync", [(s_out, p.mark(s_out))], tag="final")

        if debug:
            p.phase()
            nst = 0
            p.emit("sync", _fn("dma_start", dbg_rq, rq[:]), inc=s_out,
                   inc_amt=16, tag="dbg.rq")
            p.emit("sync", _fn("dma_start", dbg_rkv, rkv[:]), inc=s_out,
                   inc_amt=16, tag="dbg.rkv")
            p.emit("sync", _fn("dma_start", dbg_rw, rw[:]), inc=s_out,
                   inc_amt=16, tag="dbg.rw")
            p.emit("sync", _fn("dma_start", dbg_vecs, vecs[:]), inc=s_out,
                   inc_amt=16, tag="dbg.vecs")
            p.waitop("sync", [(s_out, p.mark(s_out))], tag="dbg.final")

        p.simulate()

        with nc.Block() as block:
            @block.sync
            def _(e):
                for f in p.ops["sync"]:
                    f(e)

            @block.gpsimd
            def _(e):
                for f in p.ops["gpsimd"]:
                    f(e)

            @block.tensor
            def _(e):
                for f in p.ops["tensor"]:
                    f(e)

            @block.vector
            def _(e):
                for f in p.ops["vector"]:
                    f(e)

            @block.scalar
            def _(e):
                for f in p.ops["scalar"]:
                    f(e)

    return nc


def make_consts():
    ident = np.eye(128, dtype=np.float32).astype(ml_dtypes.bfloat16)
    mask = np.where(np.arange(128)[None, :] <= np.arange(128)[:, None],
                    np.float32(0.0), np.float32(NEG)).astype(ml_dtypes.bfloat16)
    inv = (10000.0 ** (-(np.arange(0, HD, 2, dtype=np.float64) / HD)))
    th = np.arange(T, dtype=np.float64)[None, :] * inv[:, None]
    cos = np.cos(th).astype(np.float32)
    sin = np.sin(th).astype(np.float32)
    cosf = np.concatenate([cos, cos], axis=0).astype(ml_dtypes.bfloat16)
    sinf = np.concatenate([sin, sin], axis=0).astype(ml_dtypes.bfloat16)
    cosf2 = (np.concatenate([cos, cos], axis=0) / math.sqrt(HD)).astype(
        ml_dtypes.bfloat16)
    sinf2 = (np.concatenate([sin, sin], axis=0) / math.sqrt(HD)).astype(
        ml_dtypes.bfloat16)
    rot = np.zeros((128, 128), np.float32)
    for d in range(64):
        rot[d + 64, d] = -1.0
        rot[d, d + 64] = 1.0
    rot = rot.astype(ml_dtypes.bfloat16)
    onesbv = np.ones((128, 8), dtype=ml_dtypes.bfloat16)
    onesfv = np.ones((128, 128), dtype=np.float32)
    return ident, mask, cosf, sinf, rot, cosf2, sinf2, onesbv, onesfv


def kernel(L0_attn_x, L0_attn_Wq, L0_attn_Wk, L0_attn_Wv, L0_attn_Wproj,
           L0_ffn_w_gate, L0_ffn_w_up, L0_ffn_w_down,
           _phases=5, _debug=False):
    x = np.ascontiguousarray(np.asarray(L0_attn_x, dtype=np.float32))
    wq = np.ascontiguousarray(np.asarray(L0_attn_Wq, dtype=np.float32))
    wk = np.ascontiguousarray(np.asarray(L0_attn_Wk, dtype=np.float32))
    wv = np.ascontiguousarray(np.asarray(L0_attn_Wv, dtype=np.float32))
    wp = np.ascontiguousarray(np.asarray(L0_attn_Wproj, dtype=np.float32))
    wg = np.ascontiguousarray(np.asarray(L0_ffn_w_gate, dtype=np.float32))
    wu = np.ascontiguousarray(np.asarray(L0_ffn_w_up, dtype=np.float32))
    wd = np.ascontiguousarray(np.asarray(L0_ffn_w_down, dtype=np.float32))
    ident, mask, cosf, sinf, rot, cosf2, sinf2, onesbv, onesfv = make_consts()
    epsv = np.full((128, 8), EPS, dtype=np.float32)

    nc = build_nc(phases=_phases, debug=_debug)
    in_maps = []
    for c in range(8):
        in_maps.append({
            "x": np.ascontiguousarray(x[c]),
            "wq": wq, "wk": wk, "wv": wv, "wp": wp,
            "wg": wg, "wu": wu, "wd": wd,
            "identw": ident, "maskw": mask, "cosw": cosf, "sinw": sinf,
            "epsw": epsv, "rotw": rot, "cosw2": cosf2, "sinw2": sinf2,
            "onesbw": onesbv, "onesfw": onesfv,
        })
    res = run_bass_kernel_spmd(nc, in_maps, core_ids=list(range(8)))
    outs = np.stack([np.asarray(r["out"], dtype=np.float32)
                     for r in res.results], axis=0)
    if _debug:
        dbgs = [{k: np.asarray(v) for k, v in r.items() if k.startswith("dbg")}
                for r in res.results]
        return outs, dbgs
    return outs
